# version 08
# baseline (speedup 1.0000x reference)
import contextlib
import numpy as np
import ml_dtypes
import concourse.bass as bass
import concourse.mybir as mybir
from concourse.bass_utils import run_bass_kernel_spmd
from concourse.aot_env import allow_ici_all_to_all

F32 = mybir.dt.float32
BF16 = mybir.dt.bfloat16
AF = mybir.ActivationFunctionType
ALU = mybir.AluOpType
AX = mybir.AxisListType.X
BF = ml_dtypes.bfloat16

ENGINES = ("sync", "gpsimd", "tensor", "vector", "scalar")


class Buf:
    __slots__ = ("ap", "ptok", "ctoks", "name", "serial")

    def __init__(self, ap, name="", serial=False):
        self.ap = ap
        self.name = name
        self.serial = serial
        self.ptok = None
        self.ctoks = {}


class P:

    def __init__(self, nc, stack):
        self.nc = nc
        self.stack = stack
        self.streams = {n: [] for n in ENGINES}
        self.sems = {}
        self.lastw = {}

    def sem(self, name):
        if name not in self.sems:
            self.sems[name] = [self.stack.enter_context(self.nc.semaphore(name)), 0]
        return self.sems[name]

    def raw(self, eng, fn):
        self.streams[eng].append(fn)

    def wait(self, eng, semname, val):
        if val is None or val <= 0:
            return
        if self.lastw.get((eng, semname), 0) >= val:
            return
        self.lastw[(eng, semname)] = val
        sem = self.sem(semname)[0]
        self.raw(eng, lambda e, sem=sem, val=val: e.wait_ge(sem, val))

    def _pre(self, eng, reads, writes):
        for b in reads:
            if b.ptok:
                self.wait(eng, *b.ptok)
            if b.serial:
                for sn, v in b.ctoks.items():
                    self.wait(eng, sn, v)
        for b in writes:
            for sn, v in b.ctoks.items():
                self.wait(eng, sn, v)
            if b.ptok:
                self.wait(eng, *b.ptok)

    def _post(self, tok, reads, writes):
        for b in reads:
            b.ctoks[tok[0]] = max(b.ctoks.get(tok[0], 0), tok[1])
        for b in writes:
            b.ptok = tok
            b.ctoks = {}

    def op(self, eng, fn, reads=(), writes=(), dma=False):
        self._pre(eng, reads, writes)
        if dma:
            semname = dma if isinstance(dma, str) else "q_" + eng
            c = self.sem(semname)
            c[1] += 16
            tok = (semname, c[1])
            self.raw(eng, lambda e, fn=fn, sem=c[0], v=c[1]: fn(e).then_inc(sem, 16))
        else:
            semname = "e_" + eng
            c = self.sem(semname)
            c[1] += 1
            tok = (semname, c[1])
            self.raw(eng, lambda e, fn=fn, sem=c[0]: fn(e).then_inc(sem, 1))
        self._post(tok, reads, writes)
        return tok

    def dma_serial(self, eng, out, in_, reads=(), writes=(), sem=None, **kw):
        semname = sem or ("q_" + eng)
        self.wait(eng, semname, self.sem(semname)[1])
        return self.dma(eng, out, in_, reads=reads, writes=writes, sem=semname, **kw)

    def dma(self, eng, out, in_, reads=(), writes=(), sem=None, **kw):
        return self.op(
            eng,
            lambda e, out=out, in_=in_, kw=kw: e.dma_start(out=out, in_=in_, **kw),
            reads=reads, writes=writes, dma=(sem or True),
        )

    def mmgroup(self, psum_buf, out_ap, pairs, reads=()):
        self._pre("tensor", reads, [psum_buf])
        n = len(pairs)
        c = self.sem("e_tensor")
        for i, (lhsT, rhs) in enumerate(pairs):
            if i == n - 1:
                c[1] += 1
                self.raw("tensor", lambda e, l=lhsT, r=rhs, o=out_ap, s=(i == 0), sem=c[0]:
                         e.matmul(o, l, r, start=s, stop=True).then_inc(sem, 1))
            else:
                self.raw("tensor", lambda e, l=lhsT, r=rhs, o=out_ap, s=(i == 0):
                         e.matmul(o, l, r, start=s, stop=False))
        tok = ("e_tensor", c[1])
        self._post(tok, reads, [psum_buf])
        return tok

    def barrier(self):
        snap = [(sn, c[1]) for sn, c in self.sems.items()]
        for eng in ENGINES:
            for sn, v in snap:
                if sn == "e_" + eng:
                    continue
                self.wait(eng, sn, v)

    def accum_step(self, psum_buf, out_ap, lhsT, rhs, start, stop, reads=()):
        if start:
            self._pre("tensor", reads, [psum_buf])
        else:
            self._pre("tensor", reads, [])
        if stop:
            c = self.sem("e_tensor")
            c[1] += 1
            self.raw("tensor", lambda e, l=lhsT, r=rhs, o=out_ap, s=start, sem=c[0]:
                     e.matmul(o, l, r, start=s, stop=True).then_inc(sem, 1))
            tok = ("e_tensor", c[1])
            for b in reads:
                b.ctoks[tok[0]] = max(b.ctoks.get(tok[0], 0), tok[1])
            psum_buf.ptok = tok
            psum_buf.ctoks = {}
            return tok
        else:
            self.raw("tensor", lambda e, l=lhsT, r=rhs, o=out_ap, s=start:
                     e.matmul(o, l, r, start=s, stop=False))
            tok = ("e_tensor", self.sem("e_tensor")[1])
            for b in reads:
                b.ctoks[tok[0]] = max(b.ctoks.get(tok[0], 0), tok[1] + 1)
            return None

    def emit(self, block):
        for n in ENGINES:
            fns = self.streams[n]
            if not fns:
                continue
            dec = getattr(block, n)

            def body(e, fns=fns):
                for f in fns:
                    f(e)
            dec(body)


_NAME_PREFIX = [""]


def _sb(nc, stack, name, shape, dt):
    name = _NAME_PREFIX[0] + name
    return Buf(stack.enter_context(nc.sbuf_tensor(name, list(shape), dt)), name)


def _bank(nc, stack, name, dt=F32, cols=None):
    name = _NAME_PREFIX[0] + name
    cols = cols or (512 if dt == F32 else 1024)
    return Buf(stack.enter_context(nc.psum_tensor(name, [128, cols], dt)), name, serial=True)


class Pool:

    def __init__(self, banks):
        self.banks = banks
        self.i = 0

    def take(self):
        b = self.banks[self.i % len(self.banks)]
        self.i += 1
        return b



def emit_stage_a(nc, p, stack, E, NCH=64, debug=False):
    H2 = 2
    D = 2048
    T = NCH * 128
    GSZ = NCH // 4

    xfm = E["xfm"]
    wq_s = E["wq_s"]
    wk_s = E["wk_s"]
    wv_s = E["wv_s"]
    ws_s = E["ws_s"]
    wg_s = E["wg_s"]
    gnorm = E["gnorm"]
    u32 = E["u32"]
    ubf = E["ubf"]
    idbf = E["idbf"]
    ones1 = E["ones1"]
    onesc = E["onesc"]

    o_scr = E["o_scr"]
    dbg = {}
    if debug:
        for nm, sh, dt in [("dQs", (64, 128), F32), ("dKf", (64, 128), F32), ("dKtm", (128, 128), F32),
                           ("dSc", (128, 128), F32), ("dE32", (128, 128), F32), ("dDd", (128, 128), F32),
                           ("dokk", (128, 64), F32), ("dpvU", (128, 64), F32), ("dpvD", (128, 64), F32),
                           ("dEt", (64, 128), F32), ("dpvDt", (64, 128), F32), ("dBTm", (128, 128), F32),
                           ("dATm", (128, 128), F32), ("doh", (128, 128), F32), ("dVtm", (128, 256), F32),
                           ("dGt", (128, 256), F32), ("drstd", (128, 1), F32)]:
            dbg[nm] = nc.dram_tensor(nm, list(sh), dt, kind="ExternalOutput").ap()

    W = {}
    W["q"] = _sb(nc, stack, "w_q", (128, 16 * 128), BF16)
    W["k"] = _sb(nc, stack, "w_k", (128, 16 * 128), BF16)
    W["s"] = _sb(nc, stack, "w_s", (128, 16 * 128), BF16)
    W["v"] = _sb(nc, stack, "w_v", (128, 16 * 256), BF16)
    W["g"] = _sb(nc, stack, "w_g", (128, 16 * 256), BF16)
    sU32 = _sb(nc, stack, "sU32", (128, 128), F32)
    sUbf = _sb(nc, stack, "sUbf", (128, 128), BF16)
    sIdb = _sb(nc, stack, "sIdb", (128, 128), BF16)
    sOne1 = _sb(nc, stack, "sOne1", (1, 128), F32)
    sOneC = _sb(nc, stack, "sOneC", (128, 1), F32)
    sGn = _sb(nc, stack, "sGn", (1, 128), F32)
    sGnB = _sb(nc, stack, "sGnB", (128, 128), F32)

    xb = [[_sb(nc, stack, f"xb{b}{i}", (128, 16 * 128), BF16) for i in range(2)] for b in range(2)]

    Qs = [[[_sb(nc, stack, f"Qs{s}{b}{k}", (64, 128), BF16) for k in range(H2)] for b in range(2)] for s in range(2)]
    Kf = [[[_sb(nc, stack, f"Kf{s}{b}{k}", (64, 128), BF16) for k in range(H2)] for b in range(2)] for s in range(2)]
    Ktm = [[_sb(nc, stack, f"Ktm{s}{b}", (128, 128), BF16) for b in range(2)] for s in range(2)]
    Sc = [[_sb(nc, stack, f"Sc{s}{b}", (128, 128), F32) for b in range(2)] for s in range(2)]
    Vtm = [[_sb(nc, stack, f"Vtm{s}{b}", (128, 256), BF16) for b in range(2)] for s in range(2)]
    sGt = [[_sb(nc, stack, f"sGt{s}{b}", (128, 256), F32) for b in range(2)] for s in range(2)]
    E32 = [_sb(nc, stack, f"E32{b}", (128, 128), F32) for b in range(2)]
    Ebf = [_sb(nc, stack, f"Ebf{b}", (128, 128), BF16) for b in range(2)]
    Dd = [_sb(nc, stack, f"Dd{b}", (128, 128), F32) for b in range(2)]
    ATm = [_sb(nc, stack, f"ATm{c}", (128, 128), BF16) for c in range(4)]
    okk = [_sb(nc, stack, f"okk{c}", (128, 64), F32) for c in range(4)]
    nmx = [_sb(nc, stack, f"nmx{c}", (128, 1), F32) for c in range(4)]
    pvU = [_sb(nc, stack, f"pvU{c}", (128, 64), F32) for c in range(4)]
    ssum = [_sb(nc, stack, f"ssum{c}", (128, 1), F32) for c in range(4)]
    rs_ = [_sb(nc, stack, f"rs_{c}", (128, 1), F32) for c in range(4)]
    pvD = [_sb(nc, stack, f"pvD{c}", (128, 64), BF16) for c in range(4)]
    Et = [_sb(nc, stack, f"Et{c}", (64, 128), BF16) for c in range(4)]
    pvDt = [_sb(nc, stack, f"pvDt{c}", (64, 128), BF16) for c in range(4)]
    BTm = [_sb(nc, stack, f"BTm{c}", (128, 128), BF16) for c in range(4)]
    ssq = [_sb(nc, stack, f"ssq{c}", (128, 1), F32) for c in range(4)]
    rstd = [_sb(nc, stack, f"rstd{c}", (128, 1), F32) for c in range(4)]
    rstd2 = [_sb(nc, stack, f"rstd2{c}", (128, 1), F32) for c in range(4)]
    gate2 = [_sb(nc, stack, f"gate2{c}", (128, 128), F32) for c in range(4)]
    o_h = [_sb(nc, stack, f"o_h{c}", (128, 128), BF16) for c in range(4)]
    scr128 = _sb(nc, stack, "scr128", (128, 128), F32)
    sEps = _sb(nc, stack, "sEps", (128, 1), F32)

    Wrow = [[_sb(nc, stack, f"Wr{b}{i}", (1, 128), F32) for i in range(2)] for b in range(2)]
    HkU = [[_sb(nc, stack, f"HkU{c}{i}", (64, 64), F32) for i in range(2)] for c in range(4)]
    HvU = [[_sb(nc, stack, f"HvU{c}{i}", (64, 128), F32) for i in range(2)] for c in range(4)]
    HkB = [_sb(nc, stack, f"HkB{c}", (64, 64), BF16) for c in range(4)]
    HvB = [_sb(nc, stack, f"HvB{c}", (64, 128), BF16) for c in range(4)]

    obuf = [_sb(nc, stack, f"obuf{i}", (128, 4 * GSZ * 128), BF16) for i in range(2)]

    prj = Pool([_bank(nc, stack, "bkP0"), _bank(nc, stack, "bkP1")])
    pool = Pool([_bank(nc, stack, "bkA"), _bank(nc, stack, "bkB")])
    bOv_ = _bank(nc, stack, "bkOv0")
    bOv = [bOv_, bOv_]
    bT = [_bank(nc, stack, "bkT0", BF16), _bank(nc, stack, "bkT1", BF16)]
    bM = _bank(nc, stack, "bkM")

    for buf, src in ((sU32, u32), (sUbf, ubf), (sIdb, idbf), (sOne1, ones1),
                     (sOneC, onesc), (sGn, gnorm)):
        p.dma_serial("sync", buf.ap[:], src, writes=[buf], sem="q_cst")
    for name, src, wdt in (("q", wq_s, 128), ("k", wk_s, 128), ("s", ws_s, 128),
                           ("v", wv_s, 256), ("g", wg_s, 256)):
        p.dma_serial("gpsimd", W[name].ap[:].rearrange("p (c o) -> p c o", o=wdt),
              src.rearrange("(c p) o -> p c o", p=128),
              writes=[W[name]], sem="q_wld")
    p.op("tensor", lambda e: e.matmul(bM.ap[:, 0:128], sOne1.ap[:], sGn.ap[:], start=True, stop=True),
         reads=[sOne1, sGn], writes=[bM])
    p.op("vector", lambda e: e.tensor_copy(sGnB.ap[:], bM.ap[:, 0:128]), reads=[bM], writes=[sGnB])
    p.op("vector", lambda e: e.memset(sEps.ap[:], 1e-5), writes=[sEps])
    for b in range(2):
        for i in range(2):
            p.op("vector", lambda e, t=Wrow[b][i]: e.memset(t.ap[:], 0.0), writes=[Wrow[b][i]])
    for c in range(4):
        for i in range(2):
            p.op("vector", lambda e, t=HkU[c][i]: e.memset(t.ap[:], 0.0), writes=[HkU[c][i]])
            p.op("vector", lambda e, t=HvU[c][i]: e.memset(t.ap[:], 0.0), writes=[HvU[c][i]])
        p.op("vector", lambda e, t=HkB[c]: e.memset(t.ap[:], 0.0), writes=[HkB[c]])
        p.op("vector", lambda e, t=HvB[c]: e.memset(t.ap[:], 0.0), writes=[HvB[c]])

    n_odma = [0]

    def proj_units(j):
        units = []
        sl = j % 2
        Qs_, Kf_, Ktm_, Sc_, Vtm_, sGt_ = Qs[sl], Kf[sl], Ktm[sl], Sc[sl], Vtm[sl], sGt[sl]
        for b in range(2):
            x = xb[b][j % 2]
            kc = [x.ap[:, i * 128:(i + 1) * 128] for i in range(16)]

            def u_load(b=b, x=x):
                p.dma("gpsimd", x.ap[:].rearrange("p (c t) -> p c t", t=128),
                      xfm[b, :, j * 128:(j + 1) * 128].rearrange("(c p) t -> p c t", p=128),
                      writes=[x], sem=f"q_x{b}{j % 2}")

            def u_q(b=b, x=x, kc=kc, Qs_=Qs_):
                bk = prj.take()
                p.mmgroup(bk, bk.ap[:, 0:128],
                          [(W["q"].ap[:, i * 128:(i + 1) * 128], kc[i]) for i in range(16)],
                          reads=[W["q"], x])
                for k in range(H2):
                    p.op("scalar", lambda e, k=k, bk=bk: e.activation(
                        Qs_[b][k].ap[:], bk.ap[k * 64:(k + 1) * 64, 0:128], AF.Copy, scale=0.125),
                        reads=[bk], writes=[Qs_[b][k]])

            def u_k(b=b, x=x, kc=kc, Kf_=Kf_):
                bk = prj.take()
                p.mmgroup(bk, bk.ap[:, 0:128],
                          [(W["k"].ap[:, i * 128:(i + 1) * 128], kc[i]) for i in range(16)],
                          reads=[W["k"], x])
                for k in range(H2):
                    p.op("scalar", lambda e, k=k, bk=bk: e.copy(
                        Kf_[b][k].ap[:], bk.ap[k * 64:(k + 1) * 64, 0:128]),
                        reads=[bk], writes=[Kf_[b][k]])

            def u_ktm(b=b, x=x, kc=kc, Ktm_=Ktm_):
                bk = prj.take()
                p.mmgroup(bk, bk.ap[:, 0:128],
                          [(kc[i], W["k"].ap[:, i * 128:(i + 1) * 128]) for i in range(16)],
                          reads=[W["k"], x])
                p.op("scalar", lambda e, b=b, bk=bk: e.copy(Ktm_[b].ap[:], bk.ap[:, 0:128]),
                     reads=[bk], writes=[Ktm_[b]])

            def u_s(b=b, x=x, kc=kc, Sc_=Sc_):
                bk = prj.take()
                p.mmgroup(bk, bk.ap[:, 0:128],
                          [(kc[i], W["s"].ap[:, i * 128:(i + 1) * 128]) for i in range(16)],
                          reads=[W["s"], x])
                p.op("vector", lambda e, b=b, bk=bk: e.tensor_scalar(
                    Sc_[b].ap[:], bk.ap[:, 0:128], 32.0, -32.0, ALU.min, ALU.max),
                    reads=[bk], writes=[Sc_[b]])

            def u_v(b=b, x=x, kc=kc, Vtm_=Vtm_):
                bk = prj.take()
                p.mmgroup(bk, bk.ap[:, 0:256],
                          [(kc[i], W["v"].ap[:, i * 256:(i + 1) * 256]) for i in range(16)],
                          reads=[W["v"], x])
                p.op("scalar", lambda e, b=b, bk=bk: e.copy(Vtm_[b].ap[:], bk.ap[:, 0:256]),
                     reads=[bk], writes=[Vtm_[b]])

            def u_g(b=b, x=x, kc=kc, sGt_=sGt_):
                bk = prj.take()
                p.mmgroup(bk, bk.ap[:, 0:256],
                          [(kc[i], W["g"].ap[:, i * 256:(i + 1) * 256]) for i in range(16)],
                          reads=[W["g"], x])
                p.op("scalar", lambda e, b=b, bk=bk: e.activation(sGt_[b].ap[:], bk.ap[:, 0:256], AF.Silu),
                     reads=[bk], writes=[sGt_[b]])
            units += [u_load, u_q, u_k, u_ktm, u_s, u_v, u_g]
        return units


    def cproc(j, fill=()):
        fit = iter(fill)

        def F():
            u = next(fit, None)
            if u is not None:
                u()
        sl = j % 2
        g = j // GSZ
        jj = j % GSZ
        ob = obuf[g % 2]
        Qs_, Kf_, Ktm_, Sc_, Vtm_, sGt_ = Qs[sl], Kf[sl], Ktm[sl], Sc[sl], Vtm[sl], sGt[sl]
        for b in range(2):
            p.op("scalar", lambda e, b=b: e.activation(E32[b].ap[:], Sc_[b].ap[:], AF.Exp),
                 reads=[Sc_[b]], writes=[E32[b]])
            p.op("vector", lambda e, b=b: e.tensor_copy(Ebf[b].ap[:], E32[b].ap[:]),
                 reads=[E32[b]], writes=[Ebf[b]])
            wr_old = Wrow[b][j % 2]
            wr_new = Wrow[b][(j + 1) % 2]
            bk = pool.take()
            p.mmgroup(bk, bk.ap[:, 0:128], [(sU32.ap[:], E32[b].ap[:]), (sOne1.ap[:], wr_old.ap[:])],
                      reads=[sU32, E32[b], sOne1, wr_old])
            F()
            p.op("vector", lambda e, b=b, bk=bk: e.reciprocal(Dd[b].ap[:], bk.ap[:, 0:128]),
                 reads=[bk], writes=[Dd[b]])
            p.op("tensor", lambda e, b=b: e.matmul(bM.ap[0:1, 0:128], sOneC.ap[:], E32[b].ap[:],
                                                   start=True, stop=True),
                 reads=[sOneC, E32[b]], writes=[bM])
            p.op("vector", lambda e, wo=wr_old, wn=wr_new: e.tensor_tensor(
                wn.ap[:], wo.ap[:], bM.ap[0:1, 0:128], ALU.add),
                reads=[wr_old, bM], writes=[wr_new])

            def tap(nm, buf, ap=None):
                if debug and j == 0 and b == 0:
                    tmp = dbg[nm]
                    p.dma_serial("gpsimd", tmp, ap if ap is not None else buf.ap[:], reads=[buf], sem="q_dbg")
            if debug and j == 0 and b == 0:
                tap("dE32", E32[0]); tap("dDd", Dd[0]); tap("dSc", Sc_[0])
                tap("dKtm", Ktm_[0]); tap("dVtm", Vtm_[0]); tap("dGt", sGt_[0])
                tap("dQs", Qs_[0][0]); tap("dKf", Kf_[0][0])
            for k in range(H2):
                c = b * 2 + k
                ms = slice(k * 64, (k + 1) * 64)
                vs = slice(k * 128, (k + 1) * 128)
                tAT, tok_, tnm, tpv, tss, trs, tpd = ATm[c], okk[c], nmx[c], pvU[c], ssum[c], rs_[c], pvD[c]
                tEt, tpt, tBT = Et[c], pvDt[c], BTm[c]
                tb = bT[c % 2]
                bk = pool.take()
                p.op("tensor", lambda e, b=b, k=k, bk=bk: e.matmul(
                    bk.ap[:, 0:128], Kf_[b][k].ap[:], Qs_[b][k].ap[:], start=True, stop=True),
                    reads=[Kf_[b][k], Qs_[b][k]], writes=[bk])
                p.op("vector", lambda e, bk=bk, tAT=tAT: e.tensor_tensor(
                    tAT.ap[:], bk.ap[:, 0:128], sUbf.ap[:], ALU.mult),
                    reads=[bk, sUbf], writes=[tAT])
                if k == 0:
                    tap("dATm", tAT)
                bk = pool.take()
                p.mmgroup(bk, bk.ap[:, 0:64], [
                    (tAT.ap[:], Ebf[b].ap[:, ms]),
                    (Qs_[b][k].ap[:], HkB[c].ap[:]),
                ], reads=[tAT, Ebf[b], Qs_[b][k], HkB[c]])
                F()
                p.op("vector", lambda e, b=b, ms=ms, bk=bk, tok_=tok_: e.tensor_tensor(
                    tok_.ap[:], bk.ap[:, 0:64], Dd[b].ap[:, ms], ALU.mult),
                    reads=[bk, Dd[b]], writes=[tok_])
                p.op("vector", lambda e, tok_=tok_, tnm=tnm: e.reduce_max(
                    tnm.ap[:], tok_.ap[:], axis=AX, negate=True),
                    reads=[tok_], writes=[tnm])
                p.op("scalar", lambda e, tok_=tok_, tnm=tnm, tpv=tpv, tss=tss: e.activation(
                    tpv.ap[:], tok_.ap[:], AF.Exp, bias=tnm.ap[:], accum_out=tss.ap[:]),
                    reads=[tok_, tnm], writes=[tpv, tss])
                p.op("vector", lambda e, tss=tss, trs=trs: e.reciprocal(trs.ap[:], tss.ap[:]),
                     reads=[tss], writes=[trs])
                p.op("vector", lambda e, b=b, ms=ms, tpv=tpv, trs=trs, tpd=tpd: e.scalar_tensor_tensor(
                    tpd.ap[:], tpv.ap[:], trs.ap[:], Dd[b].ap[:, ms], ALU.mult, ALU.mult),
                    reads=[tpv, trs, Dd[b]], writes=[tpd])
                if k == 0:
                    tap("dokk", tok_); tap("dpvU", tpv); tap("dpvD", tpd)
                p.op("tensor", lambda e, b=b, ms=ms, tb=tb: e.transpose(
                    tb.ap[0:64, 0:128], Ebf[b].ap[:, ms], sIdb.ap[:]),
                    reads=[Ebf[b], sIdb], writes=[tb])
                p.op("vector", lambda e, tb=tb, tEt=tEt: e.tensor_copy(tEt.ap[:], tb.ap[0:64, 0:128]),
                     reads=[tb], writes=[tEt])
                p.op("tensor", lambda e, tb=tb, tpd=tpd: e.transpose(
                    tb.ap[0:64, 128:256], tpd.ap[:], sIdb.ap[:]),
                    reads=[tpd, sIdb], writes=[tb])
                p.op("vector", lambda e, tb=tb, tpt=tpt: e.tensor_copy(tpt.ap[:], tb.ap[0:64, 128:256]),
                     reads=[tb], writes=[tpt])
                if k == 0:
                    tap("dEt", tEt); tap("dpvDt", tpt)
                bk = pool.take()
                p.op("tensor", lambda e, bk=bk, tEt=tEt, tpt=tpt: e.matmul(
                    bk.ap[:, 0:128], tEt.ap[:], tpt.ap[:], start=True, stop=True),
                    reads=[tEt, tpt], writes=[bk])
                p.op("vector", lambda e, bk=bk, tBT=tBT: e.tensor_tensor(
                    tBT.ap[:], bk.ap[:, 0:128], sUbf.ap[:], ALU.mult),
                    reads=[bk, sUbf], writes=[tBT])
                if k == 0:
                    tap("dBTm", tBT)
                p.mmgroup(bOv[b], bOv[b].ap[:, vs], [
                    (tBT.ap[:], Vtm_[b].ap[:, vs]),
                    (tpt.ap[:], HvB[c].ap[:]),
                ], reads=[tBT, Vtm_[b], tpt, HvB[c]])
                F()
                p.op("tensor", lambda e, b=b, ms=ms: e.matmul(
                    bM.ap[0:64, 128:192], Ktm_[b].ap[:, ms], Ebf[b].ap[:, ms], start=True, stop=True),
                    reads=[Ktm_[b], Ebf[b]], writes=[bM])
                hk_old, hk_new = HkU[c][j % 2], HkU[c][(j + 1) % 2]
                p.op("vector", lambda e, ho=hk_old, hn=hk_new: e.tensor_tensor(
                    hn.ap[:], ho.ap[:], bM.ap[0:64, 128:192], ALU.add),
                    reads=[hk_old, bM], writes=[hk_new])
                p.op("vector", lambda e, c=c, hn=hk_new: e.tensor_copy(HkB[c].ap[:], hn.ap[:]),
                     reads=[hk_new], writes=[HkB[c]])
                p.op("tensor", lambda e, b=b, ms=ms, vs=vs: e.matmul(
                    bM.ap[0:64, 256:384], Ebf[b].ap[:, ms], Vtm_[b].ap[:, vs], start=True, stop=True),
                    reads=[Ebf[b], Vtm_[b]], writes=[bM])
                hv_old, hv_new = HvU[c][j % 2], HvU[c][(j + 1) % 2]
                p.op("vector", lambda e, ho=hv_old, hn=hv_new: e.tensor_tensor(
                    hn.ap[:], ho.ap[:], bM.ap[0:64, 256:384], ALU.add),
                    reads=[hv_old, bM], writes=[hv_new])
                p.op("vector", lambda e, c=c, hn=hv_new: e.tensor_copy(HvB[c].ap[:], hn.ap[:]),
                     reads=[hv_new], writes=[HvB[c]])
                F()

            for k in range(H2):
                c = b * 2 + k
                vs = slice(k * 128, (k + 1) * 128)
                tsq, tr1, tr2, tg2, toh = ssq[c], rstd[c], rstd2[c], gate2[c], o_h[c]
                tb = bT[c % 2]
                p.op("scalar", lambda e, b=b, vs=vs, tsq=tsq: e.activation(
                    scr128.ap[:], bOv[b].ap[:, vs], AF.Square, accum_out=tsq.ap[:]),
                    reads=[bOv[b]], writes=[scr128, tsq])
                p.op("scalar", lambda e, tsq=tsq, tr1=tr1: e.activation(
                    tr1.ap[:], tsq.ap[:], AF.Sqrt, scale=1.0 / 128.0, bias=sEps.ap[:]),
                    reads=[tsq, sEps], writes=[tr1])
                p.op("vector", lambda e, tr1=tr1, tr2=tr2: e.reciprocal(tr2.ap[:], tr1.ap[:]),
                     reads=[tr1], writes=[tr2])
                p.op("vector", lambda e, b=b, vs=vs, tg2=tg2: e.tensor_tensor(
                    tg2.ap[:], sGt_[b].ap[:, vs], sGnB.ap[:], ALU.mult),
                    reads=[sGt_[b], sGnB], writes=[tg2])
                p.op("vector", lambda e, b=b, vs=vs, tr2=tr2, tg2=tg2, toh=toh: e.scalar_tensor_tensor(
                    toh.ap[:], bOv[b].ap[:, vs], tr2.ap[:], tg2.ap[:], ALU.mult, ALU.mult),
                    reads=[bOv[b], tr2, tg2], writes=[toh])
                if k == 0:
                    tap("doh", toh); tap("drstd", tr2)
                p.op("tensor", lambda e, tb=tb, toh=toh: e.transpose(
                    tb.ap[:, 256:384], toh.ap[:], sIdb.ap[:]),
                    reads=[toh, sIdb], writes=[tb])
                ocol = (c * GSZ + jj) * 128
                p.op("vector", lambda e, ocol=ocol, ob=ob, tb=tb: e.tensor_copy(
                    ob.ap[:, ocol:ocol + 128], tb.ap[:, 256:384]),
                    reads=[tb], writes=[ob])
                F()

        while True:
            u = next(fit, None)
            if u is None:
                break
            u()
        if jj == GSZ - 1:
            for c in range(4):
                b = c // 2
                k = c % 2
                dst = 4 * b + g
                p.dma("sync", o_scr[dst * 256 + k * 128:dst * 256 + (k + 1) * 128, :],
                      ob.ap[:, c * GSZ * 128:(c + 1) * GSZ * 128],
                      reads=[ob], sem=f"q_out{g % 2}")
                n_odma[0] += 1


    for u in proj_units(0):
        u()
    for j in range(NCH):
        cproc(j, proj_units(j + 1) if j + 1 < NCH else ())

    p.wait("gpsimd", "q_out0", p.sem("q_out0")[1])
    p.wait("gpsimd", "q_out1", p.sem("q_out1")[1])
    p.raw("gpsimd", lambda e: e.engine_nop())




def _bcols(v, dtype=np.float32):
    return np.ascontiguousarray(v.reshape(-1, 128).T).astype(dtype)


def _consts_a():
    u = np.triu(np.ones((128, 128), np.float32))
    return dict(
        u32=u,
        ubf=u.astype(BF),
        idbf=np.eye(128, dtype=BF),
        ones1=np.ones((1, 128), np.float32),
        onesc=np.ones((128, 1), np.float32),
    )


def stage_a_inmaps(inputs):
    x = inputs["L0_attn_hidden_states"]
    xT = np.ascontiguousarray(x.transpose(0, 2, 1))
    consts = _consts_a()
    ims = []
    for c in range(8):
        h0 = 2 * c
        ims.append(dict(
            xfm=xT,
            wq_s=np.ascontiguousarray(inputs["L0_attn_Wq"][:, h0 * 64:(h0 + 2) * 64]),
            wk_s=np.ascontiguousarray(inputs["L0_attn_Wk"][:, h0 * 64:(h0 + 2) * 64]),
            wv_s=np.ascontiguousarray(inputs["L0_attn_Wv"][:, h0 * 128:(h0 + 2) * 128]),
            ws_s=np.ascontiguousarray(inputs["L0_attn_Ws"][:, h0 * 64:(h0 + 2) * 64]),
            wg_s=np.ascontiguousarray(inputs["L0_attn_Wg"][:, h0 * 128:(h0 + 2) * 128]),
            gnorm=np.ascontiguousarray(inputs["L0_attn_g_norm_weight"].reshape(1, 128)),
            **consts,
        ))
    return ims


def assemble_o(results):
    o = np.zeros((2, 2048, 8192), dtype=BF)
    for c in range(8):
        sc = np.asarray(results[c]["o_scr"]).reshape(8, 256, -1)
        for dst in range(8):
            b, gg = dst // 4, dst % 4
            o[b, 2 * c * 128:(2 * c + 2) * 128, gg * 2048:(gg + 1) * 2048] = sc[dst]
    return o


def kernel_unfused(**inputs):
    cores = list(range(8))
    nc_a = build_stage_a()
    res_a = run_bass_kernel_spmd(nc_a, stage_a_inmaps(inputs), core_ids=cores)
    o_fm = assemble_o(res_a.results)

    nc_b = build_stage_b()
    res_b = run_bass_kernel_spmd(nc_b, stage_b_inmaps(inputs, o_fm), core_ids=cores)
    k_fm = np.zeros((2, 2048, 8192), dtype=BF)
    v_tm = np.zeros((2, 8192, 2048), dtype=BF)
    q_fm = []
    x2_fm = []
    for c in cores:
        b, g = c // 4, c % 4
        r = res_b.results[c]
        k_fm[b][:, g * 2048:(g + 1) * 2048] = np.asarray(r["k_out"])
        v_tm[b][g * 2048:(g + 1) * 2048, :] = np.asarray(r["v_out"])
        q_fm.append(np.asarray(r["q_out"]))
        x2_fm.append(np.asarray(r["x2_out"]))

    nc_c = build_stage_c()
    res_c = run_bass_kernel_spmd(nc_c, stage_c_inmaps(inputs, q_fm, k_fm, v_tm, x2_fm), core_ids=cores)
    x3_fm = [np.asarray(res_c.results[c]["x3_out"]) for c in cores]

    nc_d = build_stage_d()
    res_d = run_bass_kernel_spmd(nc_d, stage_d_inmaps(inputs, x3_fm), core_ids=cores)
    out = np.zeros((2, 8192, 2048), np.float32)
    for c in cores:
        b, g = c // 4, c % 4
        out[b, g * 2048:(g + 1) * 2048, :] = np.asarray(res_d.results[c]["out_sl"]).T
    return out



def emit_stage_b(nc, p, stack, E, mid_cb=None):
    D = 2048
    SC = 2048.0 ** -0.5

    o_sl = E["o_sl"]
    x_sl = E["x_sl"]
    wo0 = E["wo0"]
    basewT = E["basewT"]
    basebc = E["basebc"]
    rwT = E["rwT"]
    rbrow = E["rbrow"]
    lA = E["lA"]
    lB = E["lB"]
    wq1 = E["wq1"]
    wk1 = E["wk1"]
    wv1 = E["wv1"]
    bqc = E["bqc"]
    bkc = E["bkc"]
    bvr = E["bvr"]
    inds = E["inds"]
    ones1 = E["ones1"]
    onesc = E["onesc"]
    idf = E["idf"]

    x2_out = E["x2_out"]
    q_out = E["q_out"]
    k_out = E["k_out"]
    v_out = E["v_out"]
    x1_scr = E["x1_scr"]

    sOne1 = _sb(nc, stack, "sOne1", (1, 128), F32)
    sOneC = _sb(nc, stack, "sOneC", (128, 1), F32)
    sInds = _sb(nc, stack, "sInds", (8, 128), F32)
    sIdf = _sb(nc, stack, "sIdf", (128, 128), F32)
    sEps1 = _sb(nc, stack, "sEps1", (1, 1), F32)
    sRb = _sb(nc, stack, "sRb", (1, 8), F32)
    sBvr = _sb(nc, stack, "sBvr", (1, D), F32)
    sRcB = _sb(nc, stack, "sRcB", (128, 8), F32)
    sBvB = _sb(nc, stack, "sBvB", (128, 512), F32)
    sBqs = _sb(nc, stack, "sBqs", (128, 16), F32)
    sBkc = _sb(nc, stack, "sBkc", (128, 16), F32)
    sBbc = _sb(nc, stack, "sBbc", (128, 16), F32)
    sRwT = _sb(nc, stack, "sRwT", (128, 16 * 8), BF16)
    sLA = _sb(nc, stack, "sLA", (128, 16 * 128), BF16)
    sLB = _sb(nc, stack, "sLB", (128, 16 * 128), BF16)
    wsl = [_sb(nc, stack, f"wsl{i}", (128, 16 * 128), BF16) for i in range(2)]
    bsl = [_sb(nc, stack, f"bsl{i}", (128, 16 * 512), BF16) for i in range(2)]
    vsl = bsl
    osl = bsl
    x1sl = bsl
    xsl = [_sb(nc, stack, f"xsl{i}", (128, 512), F32) for i in range(4)]
    xrb = _sb(nc, stack, "xrb", (128, 16 * 512), F32)
    sqs = [_sb(nc, stack, f"sqs{i}", (128, 512), F32) for i in range(2)]
    scA = [_sb(nc, stack, f"scA{i}", (128, 512), F32) for i in range(2)]
    scB = [_sb(nc, stack, f"scB{i}", (128, 512), F32) for i in range(2)]
    ebf = [_sb(nc, stack, f"ebf{i}", (128, 512), BF16) for i in range(2)]
    x2b = _sb(nc, stack, "x2b", (128, 16 * 2048), BF16)
    rowA = _sb(nc, stack, "rowA", (1, 512), F32)
    rowB = _sb(nc, stack, "rowB", (1, 512), F32)
    rowC = _sb(nc, stack, "rowC", (1, 512), F32)
    lg8 = _sb(nc, stack, "lg8", (128, 8), F32)
    pr8 = _sb(nc, stack, "pr8", (128, 8), F32)
    prb = _sb(nc, stack, "prb", (128, 8), F32)
    ind8 = _sb(nc, stack, "ind8", (128, 8), F32)
    wg8 = _sb(nc, stack, "wg8", (128, 8), F32)
    colA = _sb(nc, stack, "colA", (128, 1), F32)
    colB = _sb(nc, stack, "colB", (128, 1), F32)
    wT8 = _sb(nc, stack, "wT8", (8, 512), F32)
    gB = _sb(nc, stack, "gB", (128, 128), F32)
    midg = _sb(nc, stack, "midg", (128, 512), BF16)

    pool = Pool([_bank(nc, stack, "bkA"), _bank(nc, stack, "bkB")])
    bS1 = _bank(nc, stack, "bS1")
    bS2 = _bank(nc, stack, "bS2")
    bB1 = _bank(nc, stack, "bB1")
    bB2 = _bank(nc, stack, "bB2")
    bSm = _bank(nc, stack, "bSm")
    bSm2 = _bank(nc, stack, "bSm2")

    for buf, src_ in ((sOne1, ones1), (sOneC, onesc), (sInds, inds), (sIdf, idf),
                      (sBqs, bqc), (sBkc, bkc), (sBbc, basebc), (sRb, rbrow), (sBvr, bvr)):
        p.dma_serial("sync", buf.ap[:], src_, writes=[buf], sem="q_cst")
    p.dma_serial("gpsimd", sRwT.ap[:].rearrange("p (c o) -> p c o", o=8),
                 rwT.rearrange("(c p) o -> p c o", p=128), writes=[sRwT], sem="q_wld")
    p.dma_serial("gpsimd", sLA.ap[:].rearrange("p (c o) -> p c o", o=128),
                 lA.rearrange("(c p) o -> p c o", p=128), writes=[sLA], sem="q_wld")
    p.dma_serial("gpsimd", sLB.ap[:].rearrange("p (c o) -> p c o", o=128),
                 lB.rearrange("(c p) o -> p c o", p=128), writes=[sLB], sem="q_wld")
    p.op("vector", lambda e: e.memset(sEps1.ap[:], 1e-5), writes=[sEps1])
    p.op("vector", lambda e: e.tensor_scalar(sBqs.ap[:], sBqs.ap[:], SC, None, ALU.mult, ALU.bypass),
         reads=[sBqs], writes=[sBqs])
    p.op("tensor", lambda e: e.matmul(bSm.ap[:, 0:8], sOne1.ap[:], sRb.ap[:], start=True, stop=True),
         reads=[sOne1, sRb], writes=[bSm])
    p.op("vector", lambda e: e.tensor_copy(sRcB.ap[:], bSm.ap[:, 0:8]), reads=[bSm], writes=[sRcB])

    def wstream(idx, wdram, oc, semn="q_w"):
        sl = wsl[idx % 2]
        p.dma("gpsimd", sl.ap[:].rearrange("p (c o) -> p c o", o=128),
              wdram[:, oc * 128:(oc + 1) * 128].rearrange("(c p) o -> p c o", p=128),
              writes=[sl], sem=f"{semn}{idx % 2}")
        return sl

    def ln_stats():
        p.op("vector", lambda e: e.tensor_scalar(rowA.ap[:], bS1.ap[0:1, 0:512], 1.0 / D, None, ALU.mult, ALU.bypass),
             reads=[bS1], writes=[rowA])
        p.op("vector", lambda e: e.tensor_scalar(rowB.ap[:], bS2.ap[0:1, 0:512], 1.0 / D, None, ALU.mult, ALU.bypass),
             reads=[bS2], writes=[rowB])
        p.op("vector", lambda e: e.tensor_tensor(rowC.ap[:], rowA.ap[:], rowA.ap[:], ALU.mult),
             reads=[rowA], writes=[rowC])
        p.op("vector", lambda e: e.tensor_tensor(rowB.ap[:], rowB.ap[:], rowC.ap[:], ALU.subtract),
             reads=[rowB, rowC], writes=[rowB])
        p.op("scalar", lambda e: e.activation(rowB.ap[:], rowB.ap[:], AF.Sqrt, bias=sEps1.ap[:]),
             reads=[rowB, sEps1], writes=[rowB])
        p.op("vector", lambda e: e.reciprocal(rowB.ap[:], rowB.ap[:]), reads=[rowB], writes=[rowB])
        p.op("vector", lambda e: e.tensor_tensor(rowC.ap[:], rowA.ap[:], rowB.ap[:], ALU.mult),
             reads=[rowA, rowB], writes=[rowC])
        p.op("tensor", lambda e: e.matmul(bB1.ap[:, 0:512], sOne1.ap[:], rowB.ap[:], start=True, stop=True),
             reads=[sOne1, rowB], writes=[bB1])
        p.op("tensor", lambda e: e.matmul(bB2.ap[:, 0:512], sOne1.ap[:], rowC.ap[:], start=True, stop=True),
             reads=[sOne1, rowC], writes=[bB2])

    def resid_stats(bk, oc, xt, bias_col):
        xr_ap = xrb.ap[:, oc * 512:(oc + 1) * 512]
        if bias_col is None:
            p.op("vector", lambda e, bk=bk, xt=xt, xr_ap=xr_ap: e.tensor_tensor(
                xr_ap, bk.ap[:, 0:512], xt.ap[:], ALU.add), reads=[bk, xt], writes=[xrb])
        else:
            p.op("vector", lambda e, bk=bk, xt=xt, xr_ap=xr_ap, bc=bias_col: e.scalar_tensor_tensor(
                xr_ap, bk.ap[:, 0:512], bc, xt.ap[:], ALU.add, ALU.add),
                reads=[bk, xt, sBbc], writes=[xrb])
        sq = sqs[oc % 2]
        p.op("scalar", lambda e, xr_ap=xr_ap, sq=sq: e.activation(sq.ap[:], xr_ap, AF.Square),
             reads=[xrb], writes=[sq])
        p.accum_step(bS1, bS1.ap[0:1, 0:512], sOneC.ap[:], xr_ap, oc == 0, oc == 15, reads=[sOneC, xrb])
        p.accum_step(bS2, bS2.ap[0:1, 0:512], sOneC.ap[:], sq.ap[:], oc == 0, oc == 15, reads=[sOneC, sq])

    def normalize_out(tt, write_cb):
        for oc in range(16):
            xr_ap = xrb.ap[:, oc * 512:(oc + 1) * 512]
            sa = scA[oc % 2]
            p.op("vector", lambda e, xr_ap=xr_ap, sa=sa: e.tensor_tensor(
                sa.ap[:], xr_ap, bB1.ap[:, 0:512], ALU.mult), reads=[xrb, bB1], writes=[sa])
            sb_ = scB[oc % 2]
            p.op("vector", lambda e, sa=sa, sb_=sb_: e.tensor_tensor(
                sb_.ap[:], sa.ap[:], bB2.ap[:, 0:512], ALU.subtract), reads=[sa, bB2], writes=[sb_])
            write_cb(oc, sb_)

    for tt in range(4):
        ot = osl[tt % 2]
        p.dma("gpsimd", ot.ap[:].rearrange("p (c t) -> p c t", t=512),
              o_sl[:, tt * 512:(tt + 1) * 512].rearrange("(c p) t -> p c t", p=128),
              writes=[ot], sem=f"q_o{tt % 2}")
        for oc in range(16):
            wt = wstream(tt * 16 + oc, wo0, oc)
            bk = pool.take()
            p.mmgroup(bk, bk.ap[:, 0:512],
                      [(wt.ap[:, kc * 128:(kc + 1) * 128], ot.ap[:, kc * 512:(kc + 1) * 512])
                       for kc in range(16)], reads=[wt, ot])
            xt = xsl[oc % 4]
            p.dma("sync", xt.ap[:], x_sl[oc * 128:(oc + 1) * 128, tt * 512:(tt + 1) * 512],
                  writes=[xt], sem=f"q_xf{oc % 4}")
            resid_stats(bk, oc, xt, None)
        ln_stats()

        def wr1(oc, sb_, tt=tt):
            p.dma("sync", x1_scr[oc * 128:(oc + 1) * 128, tt * 512:(tt + 1) * 512], sb_.ap[:],
                  reads=[sb_], sem=f"q_x1w{oc % 2}")
        normalize_out(tt, wr1)

    for eng in ("gpsimd", "sync"):
        p.wait(eng, "q_x1w0", p.sem("q_x1w0")[1])
        p.wait(eng, "q_x1w1", p.sem("q_x1w1")[1])
    for tt in range(4):
        xt1 = x1sl[tt % 2]
        p.dma("gpsimd", xt1.ap[:].rearrange("p (c t) -> p c t", t=512),
              x1_scr[:, tt * 512:(tt + 1) * 512].rearrange("(c p) t -> p c t", p=128),
              writes=[xt1], sem=f"q_x1r{tt % 2}")
        for qb in range(4):
            qsl = slice(qb * 128, (qb + 1) * 128)
            bk = pool.take()
            p.mmgroup(bk, bk.ap[:, 0:8],
                      [(xt1.ap[:, kc * 512 + qb * 128:kc * 512 + (qb + 1) * 128],
                        sRwT.ap[:, kc * 8:(kc + 1) * 8]) for kc in range(16)], reads=[xt1, sRwT])
            p.op("vector", lambda e, bk=bk: e.tensor_tensor(lg8.ap[:], bk.ap[:, 0:8], sRcB.ap[:], ALU.add),
                 reads=[bk, sRcB], writes=[lg8])
            p.op("vector", lambda e: e.reduce_max(colA.ap[:], lg8.ap[:], axis=AX, negate=True),
                 reads=[lg8], writes=[colA])
            p.op("scalar", lambda e: e.activation(pr8.ap[:], lg8.ap[:], AF.Exp,
                                                  bias=colA.ap[:], accum_out=colB.ap[:]),
                 reads=[lg8, colA], writes=[pr8, colB])
            p.op("vector", lambda e: e.reciprocal(colB.ap[:], colB.ap[:]), reads=[colB], writes=[colB])
            p.op("vector", lambda e: e.tensor_scalar(prb.ap[:], pr8.ap[:], colB.ap[:], None, ALU.mult, ALU.bypass),
                 reads=[pr8, colB], writes=[prb])
            p.op("vector", lambda e: e.tensor_scalar(ind8.ap[:], prb.ap[:], colB.ap[:], -1e30, ALU.is_ge, ALU.mult),
                 reads=[prb, colB], writes=[ind8])
            p.op("vector", lambda e: e.tensor_tensor(ind8.ap[:], ind8.ap[:], prb.ap[:], ALU.add),
                 reads=[ind8, prb], writes=[ind8])
            p.op("vector", lambda e: e.reduce_max(colA.ap[:], ind8.ap[:], axis=AX),
                 reads=[ind8], writes=[colA])
            p.op("vector", lambda e: e.tensor_scalar(ind8.ap[:], prb.ap[:], colA.ap[:], None, ALU.is_ge, ALU.bypass),
                 reads=[prb, colA], writes=[ind8])
            p.op("vector", lambda e: e.tensor_tensor(colA.ap[:], colA.ap[:], colB.ap[:], ALU.add),
                 reads=[colA, colB], writes=[colA])
            p.op("vector", lambda e: e.reciprocal(colA.ap[:], colA.ap[:]), reads=[colA], writes=[colA])
            p.op("vector", lambda e: e.scalar_tensor_tensor(
                wg8.ap[:], prb.ap[:], colA.ap[:], ind8.ap[:], ALU.mult, ALU.mult),
                reads=[prb, colA, ind8], writes=[wg8])
            p.op("tensor", lambda e: e.transpose(bSm.ap[0:8, 0:128], wg8.ap[:], sIdf.ap[:]),
                 reads=[wg8, sIdf], writes=[bSm])
            p.op("vector", lambda e, qsl=qsl: e.tensor_copy(wT8.ap[:, qsl], bSm.ap[0:8, 0:128]),
                 reads=[bSm], writes=[wT8])
            p.op("tensor", lambda e, qsl=qsl: e.matmul(bSm2.ap[:, 0:128], sInds.ap[:], wT8.ap[:, qsl],
                                                       start=True, stop=True),
                 reads=[sInds, wT8], writes=[bSm2])
            p.op("vector", lambda e: e.tensor_copy(gB.ap[:], bSm2.ap[:, 0:128]), reads=[bSm2], writes=[gB])
            bk2 = pool.take()
            p.mmgroup(bk2, bk2.ap[:, 0:128],
                      [(sLA.ap[:, kc * 128:(kc + 1) * 128],
                        xt1.ap[:, kc * 512 + qb * 128:kc * 512 + (qb + 1) * 128]) for kc in range(16)],
                      reads=[sLA, xt1])
            p.op("vector", lambda e, bk2=bk2, qsl=qsl: e.tensor_tensor(
                midg.ap[:, qsl], bk2.ap[:, 0:128], gB.ap[:], ALU.mult), reads=[bk2, gB], writes=[midg])
        for oc in range(16):
            wt = wstream(tt * 16 + oc, basewT, oc)
            bk = pool.take()
            pairs = [(wt.ap[:, kc * 128:(kc + 1) * 128], xt1.ap[:, kc * 512:(kc + 1) * 512])
                     for kc in range(16)]
            pairs += [(sLB.ap[:, oc * 128:(oc + 1) * 128], midg.ap[:, qb * 128:(qb + 1) * 128])
                      for qb in range(4)]
            p._pre("tensor", [wt, xt1, sLB, midg], [bk])
            cse = p.sem("e_tensor")
            for kc in range(16):
                p.raw("tensor", lambda e, w=wt.ap[:, kc * 128:(kc + 1) * 128],
                      r=xt1.ap[:, kc * 512:(kc + 1) * 512], o=bk.ap[:, 0:512], s=(kc == 0):
                      e.matmul(o, w, r, start=s, stop=False))
            for qb in range(4):
                last = qb == 3
                if last:
                    cse[1] += 1
                    p.raw("tensor", lambda e, w=sLB.ap[:, oc * 128:(oc + 1) * 128],
                          r=midg.ap[:, qb * 128:(qb + 1) * 128],
                          o=bk.ap[:, qb * 128:(qb + 1) * 128], sem=cse[0]:
                          e.matmul(o, w, r, start=False, stop=True).then_inc(sem, 1))
                else:
                    p.raw("tensor", lambda e, w=sLB.ap[:, oc * 128:(oc + 1) * 128],
                          r=midg.ap[:, qb * 128:(qb + 1) * 128],
                          o=bk.ap[:, qb * 128:(qb + 1) * 128]:
                          e.matmul(o, w, r, start=False, stop=False))
            tok = ("e_tensor", cse[1])
            for b_ in (wt, xt1, sLB, midg):
                b_.ctoks[tok[0]] = max(b_.ctoks.get(tok[0], 0), tok[1])
            bk.ptok = tok
            bk.ctoks = {}
            xt = xsl[oc % 4]
            p.dma("sync", xt.ap[:], x1_scr[oc * 128:(oc + 1) * 128, tt * 512:(tt + 1) * 512],
                  writes=[xt], sem=f"q_xf{oc % 4}")
            resid_stats(bk, oc, xt, sBbc.ap[:, oc:oc + 1])
        ln_stats()

        def wr2(oc, sb_, tt=tt):
            p.dma("sync", x2_out[oc * 128:(oc + 1) * 128, tt * 512:(tt + 1) * 512], sb_.ap[:],
                  reads=[sb_], sem=f"q_x2w{oc % 2}")
            p.op("vector", lambda e, oc=oc, tt=tt, sb_=sb_: e.tensor_copy(
                x2b.ap[:, oc * 2048 + tt * 512:oc * 2048 + (tt + 1) * 512], sb_.ap[:]),
                reads=[sb_], writes=[x2b])
        normalize_out(tt, wr2)

    _B3V_MARK = None
    for wdram, bias_sb, out_dram, scale in ((wq1, sBqs, q_out, SC), (wk1, sBkc, k_out, 1.0)):
        for oc in range(16):
            wt = wstream(oc, wdram, oc, semn="q_w")
            for tt in range(4):
                bk = pool.take()
                p.mmgroup(bk, bk.ap[:, 0:512],
                          [(wt.ap[:, kc * 128:(kc + 1) * 128],
                            x2b.ap[:, kc * 2048 + tt * 512:kc * 2048 + (tt + 1) * 512])
                           for kc in range(16)], reads=[wt, x2b])
                eb = ebf[tt % 2]
                p.op("scalar", lambda e, bk=bk, eb=eb, bias_sb=bias_sb, oc=oc, sc=scale: e.activation(
                    eb.ap[:], bk.ap[:, 0:512], AF.Identity, bias=bias_sb.ap[:, oc:oc + 1], scale=sc),
                    reads=[bk, bias_sb], writes=[eb])
                p.dma("scalar", out_dram[oc * 128:(oc + 1) * 128, tt * 512:(tt + 1) * 512],
                      eb.ap[:], reads=[eb], sem=f"q_qk{tt % 2}")
    if mid_cb is not None:
        mid_cb()
    for vg in range(4):
        vt = vsl[vg % 2]
        p.dma("gpsimd", vt.ap[:].rearrange("p (c o) -> p c o", o=512),
              wv1[:, vg * 512:(vg + 1) * 512].rearrange("(c p) o -> p c o", p=128),
              writes=[vt], sem=f"q_wv{vg % 2}")
        p.op("tensor", lambda e, vg=vg: e.matmul(bSm.ap[:, 0:512], sOne1.ap[:],
                                                 sBvr.ap[:, vg * 512:(vg + 1) * 512], start=True, stop=True),
             reads=[sOne1, sBvr], writes=[bSm])
        p.op("vector", lambda e: e.tensor_copy(sBvB.ap[:], bSm.ap[:, 0:512]), reads=[bSm], writes=[sBvB])
        for tb in range(16):
            bk = pool.take()
            p.mmgroup(bk, bk.ap[:, 0:512],
                      [(x2b.ap[:, kc * 2048 + tb * 128:kc * 2048 + (tb + 1) * 128],
                        vt.ap[:, kc * 512:(kc + 1) * 512]) for kc in range(16)], reads=[x2b, vt])
            eb = ebf[tb % 2]
            p.op("vector", lambda e, bk=bk, eb=eb: e.tensor_tensor(
                eb.ap[:], bk.ap[:, 0:512], sBvB.ap[:], ALU.add), reads=[bk, sBvB], writes=[eb])
            p.dma("scalar", v_out[tb * 128:(tb + 1) * 128, vg * 512:(vg + 1) * 512],
                  eb.ap[:], reads=[eb], sem=f"q_v{tb % 2}")

    for sn in ("q_x2w0", "q_x2w1", "q_qk0", "q_qk1", "q_v0", "q_v1"):
        p.wait("gpsimd", sn, p.sem(sn)[1])
    p.raw("gpsimd", lambda e: e.engine_nop())



def stage_b_inmaps(inputs, o_fm):
    x = inputs["L0_attn_hidden_states"]
    xT = np.ascontiguousarray(x.transpose(0, 2, 1))
    ind = np.zeros((8, 128), np.float32)
    for e in range(8):
        ind[e, e * 16:(e + 1) * 16] = 2.0
    consts = dict(
        inds=ind, ones1=np.ones((1, 128), np.float32), onesc=np.ones((128, 1), np.float32),
        idf=np.eye(128, dtype=np.float32),
    )
    shared = dict(
        wo0=inputs["L0_attn_Wo"],
        basewT=np.ascontiguousarray(inputs["L0_ffn_base_w"].T),
        basebc=_bcols(inputs["L0_ffn_base_b"]),
        rwT=np.ascontiguousarray(inputs["L0_ffn_router_w"].T),
        rbrow=inputs["L0_ffn_router_b"].reshape(1, 8),
        lA=np.ascontiguousarray(inputs["L0_ffn_lora_A"].transpose(1, 0, 2).reshape(2048, 128)),
        lB=np.ascontiguousarray(inputs["L0_ffn_lora_B"].reshape(128, 2048)),
        wq1=inputs["L1_attn_wq"], wk1=inputs["L1_attn_wk"], wv1=inputs["L1_attn_wv"],
        bqc=_bcols(inputs["L1_attn_bq"]), bkc=_bcols(inputs["L1_attn_bk"]),
        bvr=inputs["L1_attn_bv"].reshape(1, 2048),
        **consts,
    )
    ims = []
    for c in range(8):
        b, g = c // 4, c % 4
        ims.append(dict(
            o_sl=np.ascontiguousarray(o_fm[b][:, g * 2048:(g + 1) * 2048]),
            x_sl=np.ascontiguousarray(xT[b][:, g * 2048:(g + 1) * 2048]),
            **shared,
        ))
    return ims



def emit_stage_c(nc, p, stack, E):
    D = 2048

    q_sl = E["q_sl"]
    kf_parts = E["kf_parts"]
    vt_parts = E["vt_parts"]
    x2_sl = E["x2_sl"]
    wo1 = E["wo1"]
    boc = E["boc"]
    ones1 = E["ones1"]
    onesc = E["onesc"]
    onescb = E["onescb"]
    idbf = E["idbf"]

    x3_out = E["x3_out"]
    o1_scr = E["o1_scr"]

    sOne1 = _sb(nc, stack, "sOne1", (1, 128), F32)
    sOneC = _sb(nc, stack, "sOneC", (128, 1), F32)
    sOneCb = _sb(nc, stack, "sOneCb", (128, 1), BF16)
    sIdb = _sb(nc, stack, "sIdb", (128, 128), BF16)
    sEps1 = _sb(nc, stack, "sEps1", (1, 1), F32)
    sBoc = _sb(nc, stack, "sBoc", (128, 16), F32)
    khs = [_sb(nc, stack, f"khs{i}", (128, 8192), BF16) for i in range(2)]
    vhs = [_sb(nc, stack, f"vhs{i}", (128, 64 * 129), BF16) for i in range(2)]
    qhs = [_sb(nc, stack, f"qhs{i}", (128, 2048), BF16) for i in range(2)]
    sqq = _sb(nc, stack, "sqq", (128, 512), BF16)
    qn = _sb(nc, stack, "qn", (1, 2048), F32)
    kmx = _sb(nc, stack, "kmx", (1, 1), F32)
    kmt = _sb(nc, stack, "kmt", (1, 1), F32)
    brow = _sb(nc, stack, "brow", (1, 512), F32)
    nbg = _sb(nc, stack, "nbg", (1, 1), F32)
    nbc = _sb(nc, stack, "nbc", (128, 1), F32)
    pts = [_sb(nc, stack, f"pts{i}", (128, 1024), BF16) for i in range(2)]
    rn = _sb(nc, stack, "rn", (128, 1), F32)
    obf = _sb(nc, stack, "obf", (128, 128), BF16)
    otr = [_sb(nc, stack, f"otr{i}", (128, 128), BF16) for i in range(2)]
    wsl = [_sb(nc, stack, f"wsl{i}", (128, 16 * 128), BF16) for i in range(2)]
    osl = [_sb(nc, stack, f"osl{i}", (128, 16 * 512), BF16) for i in range(2)]
    xsl = [_sb(nc, stack, f"xsl{i}", (128, 512), F32) for i in range(4)]
    xrb = _sb(nc, stack, "xrb", (128, 16 * 512), F32)
    sqs = [_sb(nc, stack, f"sqs{i}", (128, 512), F32) for i in range(2)]
    scA = [_sb(nc, stack, f"scA{i}", (128, 512), F32) for i in range(2)]
    scB = [_sb(nc, stack, f"scB{i}", (128, 512), F32) for i in range(2)]
    rowA = _sb(nc, stack, "rowA", (1, 512), F32)
    rowB = _sb(nc, stack, "rowB", (1, 512), F32)
    rowC = _sb(nc, stack, "rowC", (1, 512), F32)

    bSp = [_bank(nc, stack, f"bkS{i}", F32, cols=1024) for i in range(2)]
    oB = [_bank(nc, stack, f"bkO{i}") for i in range(4)]
    bM = bSp[0]
    bM2 = bSp[1]

    for buf, src_ in ((sOne1, ones1), (sOneC, onesc), (sOneCb, onescb), (sIdb, idbf), (sBoc, boc)):
        p.dma_serial("sync", buf.ap[:], src_, writes=[buf], sem="q_cst")
    p.op("vector", lambda e: e.memset(sEps1.ap[:], 1e-5), writes=[sEps1])

    for h in range(16):
        kh = khs[h % 2]
        vh = vhs[h % 2]
        qh = qhs[h % 2]
        for r in range(4):
            p.dma("gpsimd", kh.ap[:, r * 2048:(r + 1) * 2048],
                  kf_parts[r][h * 128:(h + 1) * 128, :], writes=[kh], sem=f"q_kh{h % 2}")
        p.op("vector", lambda e, vh=vh: e.memset(vh.ap[:], 1.0), writes=[vh])
        for r in range(4):
            p.dma("gpsimd",
                  vh.ap[:, r * 16 * 129:(r + 1) * 16 * 129].rearrange("p (k w) -> p k w", w=129)[:, :, 0:128],
                  vt_parts[r][:, h * 128:(h + 1) * 128].rearrange("(k p) v -> p k v", p=128),
                  writes=[vh], sem=f"q_vh{h % 2}")
        p.dma("sync", qh.ap[:], q_sl[h * 128:(h + 1) * 128, :], writes=[qh], sem=f"q_qh{h % 2}")
        for i in range(4):
            p.op("scalar", lambda e, qh=qh, i=i: e.activation(
                sqq.ap[:], qh.ap[:, i * 512:(i + 1) * 512], AF.Square), reads=[qh], writes=[sqq])
            p.op("tensor", lambda e, i=i: e.matmul(bM.ap[0:1, 0:512], sOneCb.ap[:], sqq.ap[:],
                                                   start=True, stop=True),
                 reads=[sOneCb, sqq], writes=[bM])
            p.op("scalar", lambda e, i=i: e.activation(qn.ap[0:1, i * 512:(i + 1) * 512],
                                                       bM.ap[0:1, 0:512], AF.Sqrt),
                 reads=[bM], writes=[qn])
        for i in range(16):
            p.op("scalar", lambda e, kh=kh, i=i: e.activation(
                sqq.ap[:], kh.ap[:, i * 512:(i + 1) * 512], AF.Square), reads=[kh], writes=[sqq])
            p.op("tensor", lambda e: e.matmul(bM.ap[0:1, 0:512], sOneCb.ap[:], sqq.ap[:],
                                              start=True, stop=True),
                 reads=[sOneCb, sqq], writes=[bM])
            p.op("vector", lambda e: e.reduce_max(kmt.ap[:], bM.ap[0:1, 0:512], axis=AX),
                 reads=[bM], writes=[kmt])
            if i == 0:
                p.op("vector", lambda e: e.tensor_copy(kmx.ap[:], kmt.ap[:]), reads=[kmt], writes=[kmx])
            else:
                p.op("vector", lambda e: e.tensor_tensor(kmx.ap[:], kmx.ap[:], kmt.ap[:], ALU.max),
                     reads=[kmx, kmt], writes=[kmx])
        p.op("scalar", lambda e: e.activation(kmx.ap[:], kmx.ap[:], AF.Sqrt), reads=[kmx], writes=[kmx])

        if "dmp_qn" in E:
            p.dma("scalar", E["dmp_qn"][h:h + 1, :], qn.ap[:], reads=[qn], sem="q_dbg2")
            p.dma("scalar", E["dmp_km"][h:h + 1, :], kmx.ap[:], reads=[kmx], sem="q_dbg2")
        for qg in range(4):
            qsl = slice(qg * 512, (qg + 1) * 512)
            p.op("vector", lambda e, qsl=qsl: e.tensor_scalar(
                brow.ap[:], qn.ap[0:1, qsl], kmx.ap[:], None, ALU.mult, ALU.bypass),
                reads=[qn, kmx], writes=[brow])
            p.op("vector", lambda e: e.reduce_max(nbg.ap[:], brow.ap[:], axis=AX, negate=True),
                 reads=[brow], writes=[nbg])
            p.op("tensor", lambda e: e.matmul(bM.ap[:, 0:1], sOne1.ap[:], nbg.ap[:], start=True, stop=True),
                 reads=[sOne1, nbg], writes=[bM])
            p.op("vector", lambda e: e.tensor_copy(nbc.ap[:], bM.ap[:, 0:1]), reads=[bM], writes=[nbc])
            prev = None
            for ktp in range(32):
                sb_ = bSp[ktp % 2]
                p._pre("tensor", [kh, qh], [sb_])
                cse = p.sem("e_tensor")
                p.raw("tensor", lambda e, kh=kh, qh=qh, ktp=ktp, qsl=qsl, sb_=sb_: e.matmul(
                    sb_.ap[:, 0:512], kh.ap[:, (2 * ktp) * 128:(2 * ktp + 1) * 128], qh.ap[:, qsl],
                    start=True, stop=True))
                cse[1] += 1
                p.raw("tensor", lambda e, kh=kh, qh=qh, ktp=ktp, qsl=qsl, sb_=sb_, sem=cse[0]: e.matmul(
                    sb_.ap[:, 512:1024], kh.ap[:, (2 * ktp + 1) * 128:(2 * ktp + 2) * 128], qh.ap[:, qsl],
                    start=True, stop=True).then_inc(sem, 1))
                tok = ("e_tensor", cse[1])
                kh.ctoks[tok[0]] = max(kh.ctoks.get(tok[0], 0), tok[1])
                qh.ctoks[tok[0]] = max(qh.ctoks.get(tok[0], 0), tok[1])
                sb_.ptok = tok
                sb_.ctoks = {}
                pt = pts[ktp % 2]
                p.op("scalar", lambda e, sb_=sb_, pt=pt: e.activation(
                    pt.ap[:], sb_.ap[:, 0:1024], AF.Exp, bias=nbc.ap[:]),
                    reads=[sb_, nbc], writes=[pt])
                if prev is not None:
                    pk, ppt = prev
                    for i in range(4):
                        for j in range(2):
                            kt = 2 * pk + j
                            p.accum_step(oB[i], oB[i].ap[:, 0:129],
                                         ppt.ap[:, j * 512 + i * 128:j * 512 + (i + 1) * 128],
                                         vh.ap[:, kt * 129:(kt + 1) * 129],
                                         kt == 0, False, reads=[ppt, vh])
                prev = (ktp, pt)
            pk, ppt = prev
            for i in range(4):
                for j in range(2):
                    kt = 2 * pk + j
                    p.accum_step(oB[i], oB[i].ap[:, 0:129],
                                 ppt.ap[:, j * 512 + i * 128:j * 512 + (i + 1) * 128],
                                 vh.ap[:, kt * 129:(kt + 1) * 129],
                                 False, kt == 63, reads=[ppt, vh])
            for i in range(4):
                p.op("vector", lambda e, i=i: e.reciprocal(rn.ap[:], oB[i].ap[:, 128:129]),
                     reads=[oB[i]], writes=[rn])
                p.op("vector", lambda e, i=i: e.tensor_scalar(
                    obf.ap[:], oB[i].ap[:, 0:128], rn.ap[:], None, ALU.mult, ALU.bypass),
                    reads=[oB[i], rn], writes=[obf])
                p.op("tensor", lambda e: e.transpose(bM2.ap[:, 0:64].bitcast(BF16), obf.ap[:], sIdb.ap[:]),
                     reads=[obf, sIdb], writes=[bM2])
                ot = otr[i % 2]
                p.op("vector", lambda e, ot=ot: e.tensor_copy(ot.ap[:], bM2.ap[:, 0:64].bitcast(BF16)),
                     reads=[bM2], writes=[ot])
                p.dma("scalar", o1_scr[h * 128:(h + 1) * 128, qg * 512 + i * 128:qg * 512 + (i + 1) * 128],
                      ot.ap[:], reads=[ot], sem=f"q_o1{i % 2}")

    for eng in ("gpsimd", "sync"):
        p.wait(eng, "q_o10", p.sem("q_o10")[1])
        p.wait(eng, "q_o11", p.sem("q_o11")[1])
    bS1, bS2 = oB[0], oB[1]
    bB1, bB2 = oB[2], oB[3]
    pool = Pool([bSp[0], bSp[1]])

    def ln_stats():
        p.op("vector", lambda e: e.tensor_scalar(rowA.ap[:], bS1.ap[0:1, 0:512], 1.0 / D, None, ALU.mult, ALU.bypass),
             reads=[bS1], writes=[rowA])
        p.op("vector", lambda e: e.tensor_scalar(rowB.ap[:], bS2.ap[0:1, 0:512], 1.0 / D, None, ALU.mult, ALU.bypass),
             reads=[bS2], writes=[rowB])
        p.op("vector", lambda e: e.tensor_tensor(rowC.ap[:], rowA.ap[:], rowA.ap[:], ALU.mult),
             reads=[rowA], writes=[rowC])
        p.op("vector", lambda e: e.tensor_tensor(rowB.ap[:], rowB.ap[:], rowC.ap[:], ALU.subtract),
             reads=[rowB, rowC], writes=[rowB])
        p.op("scalar", lambda e: e.activation(rowB.ap[:], rowB.ap[:], AF.Sqrt, bias=sEps1.ap[:]),
             reads=[rowB, sEps1], writes=[rowB])
        p.op("vector", lambda e: e.reciprocal(rowB.ap[:], rowB.ap[:]), reads=[rowB], writes=[rowB])
        p.op("vector", lambda e: e.tensor_tensor(rowC.ap[:], rowA.ap[:], rowB.ap[:], ALU.mult),
             reads=[rowA, rowB], writes=[rowC])
        p.op("tensor", lambda e: e.matmul(bB1.ap[:, 0:512], sOne1.ap[:], rowB.ap[:], start=True, stop=True),
             reads=[sOne1, rowB], writes=[bB1])
        p.op("tensor", lambda e: e.matmul(bB2.ap[:, 0:512], sOne1.ap[:], rowC.ap[:], start=True, stop=True),
             reads=[sOne1, rowC], writes=[bB2])

    for tt in range(4):
        ot = osl[tt % 2]
        p.dma("gpsimd", ot.ap[:].rearrange("p (c t) -> p c t", t=512),
              o1_scr[:, tt * 512:(tt + 1) * 512].rearrange("(c p) t -> p c t", p=128),
              writes=[ot], sem=f"q_or{tt % 2}")
        for oc in range(16):
            sl = wsl[oc % 2]
            p.dma("gpsimd", sl.ap[:].rearrange("p (c o) -> p c o", o=128),
                  wo1[:, oc * 128:(oc + 1) * 128].rearrange("(c p) o -> p c o", p=128),
                  writes=[sl], sem=f"q_w{oc % 2}")
            bk = pool.take()
            p.mmgroup(bk, bk.ap[:, 0:512],
                      [(sl.ap[:, kc * 128:(kc + 1) * 128], ot.ap[:, kc * 512:(kc + 1) * 512])
                       for kc in range(16)], reads=[sl, ot])
            xt = xsl[oc % 4]
            p.dma("sync", xt.ap[:], x2_sl[oc * 128:(oc + 1) * 128, tt * 512:(tt + 1) * 512],
                  writes=[xt], sem=f"q_xf{oc % 4}")
            xr_ap = xrb.ap[:, oc * 512:(oc + 1) * 512]
            p.op("vector", lambda e, bk=bk, xt=xt, xr_ap=xr_ap, oc=oc: e.scalar_tensor_tensor(
                xr_ap, bk.ap[:, 0:512], sBoc.ap[:, oc:oc + 1], xt.ap[:], ALU.add, ALU.add),
                reads=[bk, xt, sBoc], writes=[xrb])
            sq = sqs[oc % 2]
            p.op("scalar", lambda e, xr_ap=xr_ap, sq=sq: e.activation(sq.ap[:], xr_ap, AF.Square),
                 reads=[xrb], writes=[sq])
            p.accum_step(bS1, bS1.ap[0:1, 0:512], sOneC.ap[:], xr_ap, oc == 0, oc == 15, reads=[sOneC, xrb])
            p.accum_step(bS2, bS2.ap[0:1, 0:512], sOneC.ap[:], sq.ap[:], oc == 0, oc == 15, reads=[sOneC, sq])
        ln_stats()
        for oc in range(16):
            xr_ap = xrb.ap[:, oc * 512:(oc + 1) * 512]
            sa = scA[oc % 2]
            p.op("vector", lambda e, xr_ap=xr_ap, sa=sa: e.tensor_tensor(
                sa.ap[:], xr_ap, bB1.ap[:, 0:512], ALU.mult), reads=[xrb, bB1], writes=[sa])
            sb_ = scB[oc % 2]
            p.op("vector", lambda e, sa=sa, sb_=sb_: e.tensor_tensor(
                sb_.ap[:], sa.ap[:], bB2.ap[:, 0:512], ALU.subtract), reads=[sa, bB2], writes=[sb_])
            p.dma("sync", x3_out[oc * 128:(oc + 1) * 128, tt * 512:(tt + 1) * 512], sb_.ap[:],
                  reads=[sb_], sem=f"q_x3{oc % 2}")

    p.wait("gpsimd", "q_x30", p.sem("q_x30")[1])
    p.wait("gpsimd", "q_x31", p.sem("q_x31")[1])
    p.raw("gpsimd", lambda e: e.engine_nop())



def stage_c_inmaps(inputs, q_fm, k_fm, v_tm, x2_fm):
    consts = dict(
        ones1=np.ones((1, 128), np.float32), onesc=np.ones((128, 1), np.float32),
        onescb=np.ones((128, 1), dtype=BF), idbf=np.eye(128, dtype=BF),
    )
    shared = dict(wo1=inputs["L1_attn_wo"], boc=_bcols(inputs["L1_attn_bo"]), **consts)
    ims = []
    for c in range(8):
        b = c // 4
        ims.append(dict(
            q_sl=q_fm[c], x2_sl=x2_fm[c],
            kf=np.ascontiguousarray(k_fm[b]), vt=np.ascontiguousarray(v_tm[b]),
            **shared,
        ))
    return ims



HID = [1024, 2048, 3072, 4096, 1024, 2048, 3072, 4096]


def emit_stage_d(nc, p, stack, E):
    D = 2048
    L = 1024
    ST = 512
    NST = 2048 // ST

    x3_sl = E["x3_sl"]
    downw = E["downw"]
    downbc = E["downbc"]
    upw = E["upw"]
    corew = E["corew"]
    bsumc = E["bsumc"]
    upbc = E["upbc"]
    corebc = E["corebc"]
    rw1 = E["rw1"]
    rb1 = E["rb1"]
    costr = E["costr"]
    sw1 = E["sw1"]
    sb1c = E["sb1c"]
    sw2 = E["sw2"]
    sb2c = E["sb2c"]
    ew1 = [nc.dram_tensor(f"e{e}w1", [L, HID[e]], F32, kind="ExternalInput").ap() for e in range(8)]
    eb1 = [nc.dram_tensor(f"e{e}b1", [128, HID[e] // 128], F32, kind="ExternalInput").ap() for e in range(8)]
    ew2 = [nc.dram_tensor(f"e{e}w2", [HID[e], L], F32, kind="ExternalInput").ap() for e in range(8)]
    eb2 = [nc.dram_tensor(f"e{e}b2", [128, 8], F32, kind="ExternalInput").ap() for e in range(8)]
    sels = E["sels"]
    ones1 = E["ones1"]
    onesc = E["onesc"]
    idf = E["idf"]

    out_sl = E["out_sl"]

    sOne1 = _sb(nc, stack, "sOne1", (1, 128), F32)
    sOneC = _sb(nc, stack, "sOneC", (128, 1), F32)
    sIdf = _sb(nc, stack, "sIdf", (128, 128), F32)
    sEps1 = _sb(nc, stack, "sEps1", (1, 1), F32)
    sSel = _sb(nc, stack, "sSel", (8, 8 * 128), F32)
    sRb = _sb(nc, stack, "sRb", (1, 8), F32)
    sCost = _sb(nc, stack, "sCost", (1, 8), F32)
    sRcB = _sb(nc, stack, "sRcB", (128, 8), F32)
    sRwT = _sb(nc, stack, "sRwT", (128, 16 * 8), BF16)
    sDbc = _sb(nc, stack, "sDbc", (128, 8), F32)
    sSb1 = _sb(nc, stack, "sSb1", (128, 8), F32)
    sSb2 = _sb(nc, stack, "sSb2", (128, 8), F32)
    sBsum = _sb(nc, stack, "sBsum", (128, 16), F32)
    sEb1 = [_sb(nc, stack, f"sEb1_{e}", (128, HID[e] // 128), F32) for e in range(8)]
    sG1b = [_sb(nc, stack, f"sG1b_{e}", (128, HID[e] // 128), F32) for e in range(8)]
    sG1bB = [_sb(nc, stack, f"sG1bB_{e}", (128, HID[e] // 128), BF16) for e in range(8)]
    sEb2 = [_sb(nc, stack, f"sEb2_{e}", (128, 8), F32) for e in range(8)]
    sCcol = _sb(nc, stack, "sCcol", (128, 8), F32)
    sCtmp = _sb(nc, stack, "sCtmp", (128, 8), F32)
    xt3 = _sb(nc, stack, "xt3", (128, 16 * ST), F32)
    xb = _sb(nc, stack, "xb", (128, 16 * ST), BF16)
    g0b = _sb(nc, stack, "g0b", (128, 16 * ST), BF16)
    g1b = _sb(nc, stack, "g1b", (128, 8 * ST), BF16)
    w1s = [_sb(nc, stack, f"w1s{i}", (128, 8 * 1024), BF16) for i in range(2)]
    w2s = [_sb(nc, stack, "w2s0", (128, 8 * 1024), BF16)]
    gss = _sb(nc, stack, "gss", (128, 8 * ST), BF16)
    scG = [_sb(nc, stack, f"scG{i}", (128, ST), F32) for i in range(2)]
    sWBe = _sb(nc, stack, "sWBe", (128, ST), F32)
    macc = [_sb(nc, stack, "macc0", (128, 8 * ST), F32)]
    moeb = _sb(nc, stack, "moeb", (128, 8 * ST), BF16)
    lg8 = _sb(nc, stack, "lg8", (128, 8), F32)
    pr8 = _sb(nc, stack, "pr8", (128, 8), F32)
    prb = _sb(nc, stack, "prb", (128, 8), F32)
    ind8 = _sb(nc, stack, "ind8", (128, 8), F32)
    em8 = _sb(nc, stack, "em8", (128, 8), F32)
    wg8 = _sb(nc, stack, "wg8", (128, 8), F32)
    colA = _sb(nc, stack, "colA", (128, 1), F32)
    colB = _sb(nc, stack, "colB", (128, 1), F32)
    colC = _sb(nc, stack, "colC", (128, 1), F32)
    wT8 = _sb(nc, stack, "wT8", (8, ST), F32)
    xrb = _sb(nc, stack, "xrb", (128, 16 * ST), F32)
    gssF = gss.ap[:].bitcast(F32)
    rowA = _sb(nc, stack, "rowA", (1, ST), F32)
    rowB = _sb(nc, stack, "rowB", (1, ST), F32)
    rowC = _sb(nc, stack, "rowC", (1, ST), F32)

    pool = Pool([_bank(nc, stack, "bkA"), _bank(nc, stack, "bkB")])
    epool = Pool([_bank(nc, stack, "bkE0"), _bank(nc, stack, "bkE1")])
    bS1 = _bank(nc, stack, "bS1")
    bS2 = _bank(nc, stack, "bS2")
    bB1 = _bank(nc, stack, "bB1")
    bB2 = _bank(nc, stack, "bB2")

    for buf, src_ in ((sOne1, ones1), (sOneC, onesc), (sIdf, idf), (sSel, sels),
                      (sRb, rb1), (sCost, costr), (sDbc, downbc), (sSb1, sb1c),
                      (sSb2, sb2c), (sBsum, upbc)):
        p.dma_serial("sync", buf.ap[:], src_, writes=[buf], sem="q_cst")
    sCoreb = _sb(nc, stack, "sCoreb", (128, 16), F32)
    p.dma_serial("sync", sCoreb.ap[:], corebc, writes=[sCoreb], sem="q_cst")
    for e in range(8):
        p.dma_serial("sync", sEb1[e].ap[:], eb1[e], writes=[sEb1[e]], sem="q_cst")
        p.dma_serial("sync", sEb2[e].ap[:], eb2[e], writes=[sEb2[e]], sem="q_cst")
    p.dma_serial("gpsimd", sRwT.ap[:].rearrange("p (c o) -> p c o", o=8),
                 rw1.rearrange("(c p) o -> p c o", p=128), writes=[sRwT], sem="q_wld")
    p.op("vector", lambda e: e.memset(sEps1.ap[:], 1e-5), writes=[sEps1])
    p.op("vector", lambda e: e.tensor_tensor(sBsum.ap[:], sBsum.ap[:], sCoreb.ap[:], ALU.add),
         reads=[sBsum, sCoreb], writes=[sBsum])
    p.op("vector", lambda e: e.tensor_tensor(sRb.ap[:], sRb.ap[:], sCost.ap[:], ALU.add),
         reads=[sRb, sCost], writes=[sRb])
    p.op("tensor", lambda e: e.matmul(bB1.ap[:, 0:8], sOne1.ap[:], sRb.ap[:], start=True, stop=True),
         reads=[sOne1, sRb], writes=[bB1])
    p.op("vector", lambda e: e.tensor_copy(sRcB.ap[:], bB1.ap[:, 0:8]), reads=[bB1], writes=[sRcB])
    for e in range(8):
        p.op("scalar", lambda e_, e=e: e_.activation(sG1b[e].ap[:], sEb1[e].ap[:], AF.Gelu),
             reads=[sEb1[e]], writes=[sG1b[e]])
        p.op("vector", lambda e_, e=e: e_.tensor_copy(sG1bB[e].ap[:], sG1b[e].ap[:]),
             reads=[sG1b[e]], writes=[sG1bB[e]])
    p.op("vector", lambda e: e.tensor_scalar(sCcol.ap[:], sSb2.ap[:], 0.1, None, ALU.mult, ALU.bypass),
         reads=[sSb2], writes=[sCcol])
    for e in range(8):
        p.op("vector", lambda e_, e=e: e_.tensor_tensor(sCcol.ap[:], sCcol.ap[:], sEb2[e].ap[:], ALU.add),
             reads=[sCcol, sEb2[e]], writes=[sCcol])

    def router(qb, st):
        qsl = slice(qb * 128, (qb + 1) * 128)
        bk = pool.take()
        p.mmgroup(bk, bk.ap[:, 0:8],
                  [(xb.ap[:, kc * ST + qb * 128:kc * ST + (qb + 1) * 128],
                    sRwT.ap[:, kc * 8:(kc + 1) * 8]) for kc in range(16)], reads=[xb, sRwT])
        p.op("vector", lambda e, bk=bk: e.tensor_tensor(lg8.ap[:], bk.ap[:, 0:8], sRcB.ap[:], ALU.add),
             reads=[bk, sRcB], writes=[lg8])
        p.op("vector", lambda e: e.reduce_max(colA.ap[:], lg8.ap[:], axis=AX, negate=True),
             reads=[lg8], writes=[colA])
        p.op("scalar", lambda e: e.activation(pr8.ap[:], lg8.ap[:], AF.Exp,
                                              bias=colA.ap[:], accum_out=colB.ap[:]),
             reads=[lg8, colA], writes=[pr8, colB])
        p.op("vector", lambda e: e.reciprocal(colB.ap[:], colB.ap[:]), reads=[colB], writes=[colB])
        p.op("vector", lambda e: e.tensor_scalar(prb.ap[:], pr8.ap[:], colB.ap[:], None, ALU.mult, ALU.bypass),
             reads=[pr8, colB], writes=[prb])
        p.op("vector", lambda e: e.tensor_scalar(ind8.ap[:], prb.ap[:], colB.ap[:], -1e30, ALU.is_ge, ALU.mult),
             reads=[prb, colB], writes=[ind8])
        p.op("vector", lambda e: e.tensor_tensor(ind8.ap[:], ind8.ap[:], prb.ap[:], ALU.add),
             reads=[ind8, prb], writes=[ind8])
        p.op("vector", lambda e: e.reduce_max(colA.ap[:], ind8.ap[:], axis=AX),
             reads=[ind8], writes=[colA])
        p.op("vector", lambda e: e.tensor_scalar(ind8.ap[:], prb.ap[:], colA.ap[:], None, ALU.is_ge, ALU.bypass),
             reads=[prb, colA], writes=[ind8])
        p.op("scalar", lambda e: e.activation(em8.ap[:], prb.ap[:], AF.Exp), reads=[prb], writes=[em8])
        p.op("scalar", lambda e: e.activation(colC.ap[:], colA.ap[:], AF.Exp), reads=[colA], writes=[colC])
        p.op("scalar", lambda e: e.activation(colA.ap[:], colB.ap[:], AF.Exp), reads=[colB], writes=[colA])
        p.op("vector", lambda e: e.tensor_tensor(colA.ap[:], colA.ap[:], colC.ap[:], ALU.add),
             reads=[colA, colC], writes=[colA])
        p.op("vector", lambda e: e.reciprocal(colA.ap[:], colA.ap[:]), reads=[colA], writes=[colA])
        p.op("vector", lambda e: e.scalar_tensor_tensor(
            wg8.ap[:], em8.ap[:], colA.ap[:], ind8.ap[:], ALU.mult, ALU.mult),
            reads=[em8, colA, ind8], writes=[wg8])
        p.op("tensor", lambda e: e.transpose(bB2.ap[0:8, 0:128], wg8.ap[:], sIdf.ap[:]),
             reads=[wg8, sIdf], writes=[bB2])
        p.op("vector", lambda e, qsl=qsl: e.tensor_copy(wT8.ap[:, qsl], bB2.ap[0:8, 0:128]),
             reads=[bB2], writes=[wT8])

    nslab_tot = 0
    for st in range(NST):
        tsl = slice(st * ST, (st + 1) * ST)
        p.dma("sync", xt3.ap[:].rearrange("p (c t) -> p c t", t=ST),
              x3_sl[:, tsl].rearrange("(c p) t -> p c t", p=128), writes=[xt3], sem="q_x3a")
        p.dma("gpsimd", xb.ap[:].rearrange("p (c t) -> p c t", t=ST),
              x3_sl[:, tsl].rearrange("(c p) t -> p c t", p=128), writes=[xb], sem="q_x3b")
        for c in range(16):
            p.op("scalar", lambda e, c=c: e.activation(
                g0b.ap[:, c * ST:(c + 1) * ST], xt3.ap[:, c * ST:(c + 1) * ST], AF.Gelu),
                reads=[xt3], writes=[g0b])
        for oc in range(8):
            sl = w1s[oc % 2]
            p.dma("gpsimd", sl.ap[:, 0:16 * 128].rearrange("p (c o) -> p c o", o=128),
                  downw[:, oc * 128:(oc + 1) * 128].rearrange("(c p) o -> p c o", p=128),
                  writes=[sl], sem=f"q_w1{oc % 2}")
            bk = pool.take()
            p.mmgroup(bk, bk.ap[:, 0:ST],
                      [(sl.ap[:, kc * 128:(kc + 1) * 128], xb.ap[:, kc * ST:(kc + 1) * ST])
                       for kc in range(16)], reads=[sl, xb])
            p.op("scalar", lambda e, bk=bk, oc=oc: e.activation(
                g1b.ap[:, oc * ST:(oc + 1) * ST], bk.ap[:, 0:ST], AF.Gelu,
                bias=sDbc.ap[:, oc:oc + 1]), reads=[bk, sDbc], writes=[g1b])
        for qb in range(ST // 128):
            router(qb, st)
        for e in range(9):
            He = HID[e] if e < 8 else L
            w1d = ew1[e] if e < 8 else sw1
            w2d = ew2[e] if e < 8 else sw2
            b1c = sEb1[e] if e < 8 else sSb1
            if e < 8:
                p.op("tensor", lambda e_, e=e: e_.matmul(
                    bB2.ap[:, 0:ST], sSel.ap[:, e * 128:(e + 1) * 128], wT8.ap[:], start=True, stop=True),
                    reads=[sSel, wT8], writes=[bB2])
                p.op("vector", lambda e_: e_.tensor_copy(sWBe.ap[:], bB2.ap[:, 0:ST]),
                     reads=[bB2], writes=[sWBe])
            for slab in range(He // 1024):
                s1 = w1s[nslab_tot % 2]
                p.dma("gpsimd", s1.ap[:].rearrange("p (c o) -> p c o", o=1024),
                      w1d[:, slab * 1024:(slab + 1) * 1024].rearrange("(c p) o -> p c o", p=128),
                      writes=[s1], sem=f"q_w1{nslab_tot % 2}")
                s2 = w2s[0]
                p.dma("gpsimd", s2.ap[:].rearrange("p (c o) -> p c o", o=1024),
                      w2d[slab * 1024:(slab + 1) * 1024, :].rearrange("(c p) o -> p c o", p=128),
                      writes=[s2], sem="q_w20")
                for ocs in range(8):
                    bk = pool.take()
                    p.mmgroup(bk, bk.ap[:, 0:ST],
                              [(s1.ap[:, kc * 1024 + ocs * 128:kc * 1024 + (ocs + 1) * 128],
                                g1b.ap[:, kc * ST:(kc + 1) * ST]) for kc in range(8)],
                              reads=[s1, g1b])
                    sg = scG[ocs % 2]
                    p.op("scalar", lambda e_, bk=bk, sg=sg, b1c=b1c, slab=slab, ocs=ocs: e_.activation(
                        sg.ap[:], bk.ap[:, 0:ST], AF.Gelu, bias=b1c.ap[:, slab * 8 + ocs:slab * 8 + ocs + 1]),
                        reads=[bk, b1c], writes=[sg])
                    if e < 8:
                        p.op("vector", lambda e_, sg=sg, ocs=ocs, g1bc=sG1b[e], slab=slab: e_.scalar_tensor_tensor(
                            gss.ap[:, ocs * ST:(ocs + 1) * ST], sg.ap[:],
                            g1bc.ap[:, slab * 8 + ocs:slab * 8 + ocs + 1], sWBe.ap[:],
                            ALU.subtract, ALU.mult), reads=[sg, sG1b[e], sWBe], writes=[gss])
                    else:
                        p.op("vector", lambda e_, sg=sg, ocs=ocs: e_.tensor_scalar(
                            gss.ap[:, ocs * ST:(ocs + 1) * ST], sg.ap[:], 0.1, None, ALU.mult, ALU.bypass),
                            reads=[sg], writes=[gss])
                for moc in range(8):
                    bke = epool.take()
                    p.mmgroup(bke, bke.ap[:, 0:ST],
                              [(s2.ap[:, kc * 1024 + moc * 128:kc * 1024 + (moc + 1) * 128],
                                gss.ap[:, kc * ST:(kc + 1) * ST]) for kc in range(8)],
                              reads=[s2, gss])
                    first = (e == 0 and slab == 0)
                    mo = macc[0]
                    if first:
                        p.op("vector", lambda e_, bke=bke, moc=moc: e_.tensor_copy(
                            mo.ap[:, moc * ST:(moc + 1) * ST], bke.ap[:, 0:ST]),
                            reads=[bke], writes=[mo])
                    else:
                        p.op("vector", lambda e_, bke=bke, moc=moc, mo=mo: e_.tensor_tensor(
                            mo.ap[:, moc * ST:(moc + 1) * ST], mo.ap[:, moc * ST:(moc + 1) * ST],
                            bke.ap[:, 0:ST], ALU.add), reads=[bke, mo], writes=[mo])
                    if st == 0 and e < 8:
                        bkc_ = epool.take()
                        p.mmgroup(bkc_, bkc_.ap[:, 0:1],
                                  [(s2.ap[:, kc * 1024 + moc * 128:kc * 1024 + (moc + 1) * 128],
                                    sG1bB[e].ap[:, slab * 8 + kc:slab * 8 + kc + 1]) for kc in range(8)],
                                  reads=[s2, sG1bB[e]])
                        p.op("vector", lambda e_, bkc_=bkc_, moc=moc: e_.tensor_tensor(
                            sCcol.ap[:, moc:moc + 1], sCcol.ap[:, moc:moc + 1], bkc_.ap[:, 0:1], ALU.add),
                            reads=[bkc_, sCcol], writes=[sCcol])
                nslab_tot += 1
        for moc in range(8):
            p.op("vector", lambda e_, moc=moc: e_.tensor_scalar(
                moeb.ap[:, moc * ST:(moc + 1) * ST], macc[0].ap[:, moc * ST:(moc + 1) * ST],
                sCcol.ap[:, moc:moc + 1], None, ALU.add, ALU.bypass),
                reads=[macc[0], sCcol], writes=[moeb])
        for oc in range(16):
            su = w1s[oc % 2]
            p.dma("gpsimd", su.ap[:, 0:8 * 128].rearrange("p (c o) -> p c o", o=128),
                  upw[:, oc * 128:(oc + 1) * 128].rearrange("(c p) o -> p c o", p=128),
                  writes=[su], sem=f"q_w1{oc % 2}")
            sc_ = w2s[0]
            p.dma("gpsimd", sc_.ap[:, 0:16 * 128].rearrange("p (c o) -> p c o", o=128),
                  corew[:, oc * 128:(oc + 1) * 128].rearrange("(c p) o -> p c o", p=128),
                  writes=[sc_], sem="q_w20")
            bk = pool.take()
            p._pre("tensor", [su, sc_, moeb, g0b], [bk])
            cse = p.sem("e_tensor")
            for kc in range(8):
                p.raw("tensor", lambda e_, w=su.ap[:, kc * 128:(kc + 1) * 128],
                      r=moeb.ap[:, kc * ST:(kc + 1) * ST], o=bk.ap[:, 0:ST], s=(kc == 0):
                      e_.matmul(o, w, r, start=s, stop=False))
            for kc in range(16):
                last = kc == 15
                if last:
                    cse[1] += 1
                    p.raw("tensor", lambda e_, w=sc_.ap[:, kc * 128:(kc + 1) * 128],
                          r=g0b.ap[:, kc * ST:(kc + 1) * ST], o=bk.ap[:, 0:ST], sem=cse[0]:
                          e_.matmul(o, w, r, start=False, stop=True).then_inc(sem, 1))
                else:
                    p.raw("tensor", lambda e_, w=sc_.ap[:, kc * 128:(kc + 1) * 128],
                          r=g0b.ap[:, kc * ST:(kc + 1) * ST], o=bk.ap[:, 0:ST]:
                          e_.matmul(o, w, r, start=False, stop=False))
            tok = ("e_tensor", cse[1])
            for b_ in (su, sc_, moeb, g0b):
                b_.ctoks[tok[0]] = max(b_.ctoks.get(tok[0], 0), tok[1])
            bk.ptok = tok
            bk.ctoks = {}
            xr_ap = xrb.ap[:, oc * ST:(oc + 1) * ST]
            p.op("vector", lambda e_, bk=bk, xr_ap=xr_ap, oc=oc: e_.scalar_tensor_tensor(
                xr_ap, bk.ap[:, 0:ST], sBsum.ap[:, oc:oc + 1], xt3.ap[:, oc * ST:(oc + 1) * ST],
                ALU.add, ALU.add), reads=[bk, sBsum, xt3], writes=[xrb])
            sq_ap = gssF[:, (oc % 2) * ST:(oc % 2 + 1) * ST]
            p.op("scalar", lambda e_, xr_ap=xr_ap, sq_ap=sq_ap: e_.activation(sq_ap, xr_ap, AF.Square),
                 reads=[xrb], writes=[gss])
            p.accum_step(bS1, bS1.ap[0:1, 0:ST], sOneC.ap[:], xr_ap, oc == 0, oc == 15, reads=[sOneC, xrb])
            p.accum_step(bS2, bS2.ap[0:1, 0:ST], sOneC.ap[:], sq_ap, oc == 0, oc == 15, reads=[sOneC, gss])
        p.op("vector", lambda e_: e_.tensor_scalar(rowA.ap[:], bS1.ap[0:1, 0:ST], 1.0 / D, None, ALU.mult, ALU.bypass),
             reads=[bS1], writes=[rowA])
        p.op("vector", lambda e_: e_.tensor_scalar(rowB.ap[:], bS2.ap[0:1, 0:ST], 1.0 / D, None, ALU.mult, ALU.bypass),
             reads=[bS2], writes=[rowB])
        p.op("vector", lambda e_: e_.tensor_tensor(rowC.ap[:], rowA.ap[:], rowA.ap[:], ALU.mult),
             reads=[rowA], writes=[rowC])
        p.op("vector", lambda e_: e_.tensor_tensor(rowB.ap[:], rowB.ap[:], rowC.ap[:], ALU.subtract),
             reads=[rowB, rowC], writes=[rowB])
        p.op("scalar", lambda e_: e_.activation(rowB.ap[:], rowB.ap[:], AF.Sqrt, bias=sEps1.ap[:]),
             reads=[rowB, sEps1], writes=[rowB])
        p.op("vector", lambda e_: e_.reciprocal(rowB.ap[:], rowB.ap[:]), reads=[rowB], writes=[rowB])
        p.op("vector", lambda e_: e_.tensor_tensor(rowC.ap[:], rowA.ap[:], rowB.ap[:], ALU.mult),
             reads=[rowA, rowB], writes=[rowC])
        p.op("tensor", lambda e_: e_.matmul(bB1.ap[:, 0:ST], sOne1.ap[:], rowB.ap[:], start=True, stop=True),
             reads=[sOne1, rowB], writes=[bB1])
        p.op("tensor", lambda e_: e_.matmul(bB2.ap[:, 0:ST], sOne1.ap[:], rowC.ap[:], start=True, stop=True),
             reads=[sOne1, rowC], writes=[bB2])
        for oc in range(16):
            xr_ap = xrb.ap[:, oc * ST:(oc + 1) * ST]
            sa_ap = gssF[:, (oc % 2) * ST:(oc % 2 + 1) * ST]
            p.op("vector", lambda e_, xr_ap=xr_ap, sa_ap=sa_ap: e_.tensor_tensor(
                sa_ap, xr_ap, bB1.ap[:, 0:ST], ALU.mult), reads=[xrb, bB1], writes=[gss])
            sb_ap = gssF[:, (2 + oc % 2) * ST:(3 + oc % 2) * ST]
            p.op("vector", lambda e_, sa_ap=sa_ap, sb_ap=sb_ap: e_.tensor_tensor(
                sb_ap, sa_ap, bB2.ap[:, 0:ST], ALU.subtract), reads=[gss, bB2], writes=[gss])
            p.dma("sync", out_sl[oc * 128:(oc + 1) * 128, tsl], sb_ap,
                  reads=[gss], sem=f"q_ow{oc % 2}")

    p.wait("gpsimd", "q_ow0", p.sem("q_ow0")[1])
    p.wait("gpsimd", "q_ow1", p.sem("q_ow1")[1])
    p.raw("gpsimd", lambda e: e.engine_nop())



def stage_d_inmaps(inputs, x3_fm):
    HIDc = [1024, 2048, 3072, 4096, 1024, 2048, 3072, 4096]
    cost = np.array([2 * 1024 * h for h in HIDc], np.float32)
    sel = np.zeros((8, 8 * 128), np.float32)
    for e in range(8):
        sel[e, e * 128:(e + 1) * 128] = 1.0
    shared = dict(
        downw=inputs["L1_ffn_down_w"], downbc=_bcols(inputs["L1_ffn_down_b"]),
        upw=inputs["L1_ffn_up_w"], upbc=_bcols(inputs["L1_ffn_up_b"]),
        corew=inputs["L1_ffn_core_w"], corebc=_bcols(inputs["L1_ffn_core_b"]),
        bsumc=_bcols(inputs["L1_ffn_up_b"]),
        rw1=inputs["L1_ffn_router_w"], rb1=inputs["L1_ffn_router_b"].reshape(1, 8),
        costr=(-1e-7 * cost).reshape(1, 8),
        sw1=inputs["L1_ffn_shared_w1"], sb1c=_bcols(inputs["L1_ffn_shared_b1"]),
        sw2=inputs["L1_ffn_shared_w2"], sb2c=_bcols(inputs["L1_ffn_shared_b2"]),
        sels=sel,
        ones1=np.ones((1, 128), np.float32), onesc=np.ones((128, 1), np.float32),
        idf=np.eye(128, dtype=np.float32),
    )
    for e in range(8):
        shared[f"e{e}w1"] = inputs[f"L1_ffn_e{e}_w1"]
        shared[f"e{e}b1"] = _bcols(inputs[f"L1_ffn_e{e}_b1"])
        shared[f"e{e}w2"] = inputs[f"L1_ffn_e{e}_w2"]
        shared[f"e{e}b2"] = _bcols(inputs[f"L1_ffn_e{e}_b2"])
    ims = []
    for c in range(8):
        ims.append(dict(x3_sl=x3_fm[c], **shared))
    return ims



def DECLS_A(NCH=64):
    T, GSZ = NCH * 128, NCH // 4
    return [
        ("xfm", [2, 2048, T], F32, "ExternalInput"),
        ("wq_s", [2048, 128], F32, "ExternalInput"),
        ("wk_s", [2048, 128], F32, "ExternalInput"),
        ("wv_s", [2048, 256], F32, "ExternalInput"),
        ("ws_s", [2048, 128], F32, "ExternalInput"),
        ("wg_s", [2048, 256], F32, "ExternalInput"),
        ("gnorm", [1, 128], F32, "ExternalInput"),
        ("u32", [128, 128], F32, "ExternalInput"),
        ("ubf", [128, 128], BF16, "ExternalInput"),
        ("idbf", [128, 128], BF16, "ExternalInput"),
        ("ones1", [1, 128], F32, "ExternalInput"),
        ("onesc", [128, 1], F32, "ExternalInput"),
        ("o_scr", [2048, GSZ * 128], BF16, "ExternalOutput"),
    ]


DECLS_B = [
    ("o_sl", [2048, 2048], BF16, "ExternalInput"),
    ("x_sl", [2048, 2048], F32, "ExternalInput"),
    ("wo0", [2048, 2048], F32, "ExternalInput"),
    ("basewT", [2048, 2048], F32, "ExternalInput"),
    ("basebc", [128, 16], F32, "ExternalInput"),
    ("rwT", [2048, 8], F32, "ExternalInput"),
    ("rbrow", [1, 8], F32, "ExternalInput"),
    ("lA", [2048, 128], F32, "ExternalInput"),
    ("lB", [128, 2048], F32, "ExternalInput"),
    ("wq1", [2048, 2048], F32, "ExternalInput"),
    ("wk1", [2048, 2048], F32, "ExternalInput"),
    ("wv1", [2048, 2048], F32, "ExternalInput"),
    ("bqc", [128, 16], F32, "ExternalInput"),
    ("bkc", [128, 16], F32, "ExternalInput"),
    ("bvr", [1, 2048], F32, "ExternalInput"),
    ("inds", [8, 128], F32, "ExternalInput"),
    ("ones1", [1, 128], F32, "ExternalInput"),
    ("onesc", [128, 1], F32, "ExternalInput"),
    ("idf", [128, 128], F32, "ExternalInput"),
    ("x2_out", [2048, 2048], F32, "ExternalOutput"),
    ("q_out", [2048, 2048], BF16, "ExternalOutput"),
    ("k_out", [2048, 2048], BF16, "ExternalOutput"),
    ("v_out", [2048, 2048], BF16, "ExternalOutput"),
    ("x1_scr", [2048, 2048], F32, "Internal"),
]

DECLS_C = [
    ("q_sl", [2048, 2048], BF16, "ExternalInput"),
    ("kf", [2048, 8192], BF16, "ExternalInput"),
    ("vt", [8192, 2048], BF16, "ExternalInput"),
    ("x2_sl", [2048, 2048], F32, "ExternalInput"),
    ("wo1", [2048, 2048], F32, "ExternalInput"),
    ("boc", [128, 16], F32, "ExternalInput"),
    ("ones1", [1, 128], F32, "ExternalInput"),
    ("onesc", [128, 1], F32, "ExternalInput"),
    ("onescb", [128, 1], BF16, "ExternalInput"),
    ("idbf", [128, 128], BF16, "ExternalInput"),
    ("x3_out", [2048, 2048], F32, "ExternalOutput"),
    ("o1_scr", [2048, 2048], BF16, "Internal"),
]

DECLS_D = [
    ("x3_sl", [2048, 2048], F32, "ExternalInput"),
    ("downw", [2048, 1024], F32, "ExternalInput"),
    ("downbc", [128, 8], F32, "ExternalInput"),
    ("upw", [1024, 2048], F32, "ExternalInput"),
    ("bsumc", [128, 16], F32, "ExternalInput"),
    ("corew", [2048, 2048], F32, "ExternalInput"),
    ("upbc", [128, 16], F32, "ExternalInput"),
    ("corebc", [128, 16], F32, "ExternalInput"),
    ("rw1", [2048, 8], F32, "ExternalInput"),
    ("rb1", [1, 8], F32, "ExternalInput"),
    ("costr", [1, 8], F32, "ExternalInput"),
    ("sw1", [1024, 1024], F32, "ExternalInput"),
    ("sb1c", [128, 8], F32, "ExternalInput"),
    ("sw2", [1024, 1024], F32, "ExternalInput"),
    ("sb2c", [128, 8], F32, "ExternalInput"),
    ("sels", [8, 8 * 128], F32, "ExternalInput"),
    ("ones1", [1, 128], F32, "ExternalInput"),
    ("onesc", [128, 1], F32, "ExternalInput"),
    ("idf", [128, 128], F32, "ExternalInput"),
    ("out_sl", [2048, 2048], F32, "ExternalOutput"),
]


def _declare(nc, decls, kinds=None, prefix_done=()):
    E = {}
    for name, shape, dt, kind in decls:
        if kinds and name in kinds:
            kind = kinds[name]
        if name in prefix_done:
            continue
        E[name] = nc.dram_tensor(name, shape, dt, kind=kind).ap()
    return E


def _mk(emit_fn, decls, **kw):
    nc = bass.Bass("TRN2", target_bir_lowering=False)
    E = _declare(nc, decls)
    with contextlib.ExitStack() as stack:
        p = P(nc, stack)
        emit_fn(nc, p, stack, E, **kw)
        with nc.Block() as block:
            p.emit(block)
    return nc


def build_stage_a(NCH=64, debug=False):
    return _mk(emit_stage_a, DECLS_A(NCH), NCH=NCH, debug=debug)


def build_stage_b():
    return _mk(emit_stage_b, DECLS_B)


def build_stage_c():
    nc = bass.Bass("TRN2", target_bir_lowering=False)
    E = _declare(nc, DECLS_C)
    E["kf_parts"] = [E["kf"][:, r * 2048:(r + 1) * 2048] for r in range(4)]
    E["vt_parts"] = [E["vt"][r * 2048:(r + 1) * 2048, :] for r in range(4)]
    with contextlib.ExitStack() as stack:
        p = P(nc, stack)
        emit_stage_c(nc, p, stack, E)
        with nc.Block() as block:
            p.emit(block)
    return nc


def build_stage_d():
    return _mk(emit_stage_d, DECLS_D)



def build_fused(dump=False):
    nc = bass.Bass("TRN2", target_bir_lowering=False)
    E = {}
    seen = set()

    def decl(decls, kinds=None):
        for name, shape, dt, kind in decls:
            if name in seen:
                continue
            seen.add(name)
            if kinds and name in kinds:
                kind = kinds[name]
            E[name] = nc.dram_tensor(name, shape, dt, kind=kind).ap()

    decl(DECLS_A(64), kinds={"o_scr": "Internal"})
    decl([d for d in DECLS_B if d[0] not in ("o_sl", "x2_out", "q_out", "k_out", "v_out")])
    decl([d for d in DECLS_C if d[0] not in ("q_sl", "kf", "vt", "x2_sl", "x3_out")])
    decl([d for d in DECLS_D if d[0] != "x3_sl"])
    o_rcv = nc.dram_tensor("o_rcv", [2048, 2048], BF16).ap()
    k_scr = nc.dram_tensor("k_scr", [2048, 2048], BF16).ap()
    v_scr = nc.dram_tensor("v_scr", [2048, 2048], BF16).ap()
    k_rcv = nc.dram_tensor("k_rcv", [4 * 2048, 2048], BF16).ap()
    v_rcv = nc.dram_tensor("v_rcv", [4 * 2048, 2048], BF16).ap()
    x2_scr = nc.dram_tensor("x2_scr", [2048, 2048], F32).ap()
    q_scr = nc.dram_tensor("q_scr", [2048, 2048], BF16).ap()
    x3_scr = nc.dram_tensor("x3_scr", [2048, 2048], F32).ap()
    E["o_sl"] = o_rcv
    E["x2_out"] = x2_scr
    E["q_out"] = q_scr
    E["k_out"] = k_scr
    E["v_out"] = v_scr
    E["q_sl"] = q_scr
    E["x2_sl"] = x2_scr
    E["kf_parts"] = [k_rcv[r * 2048:(r + 1) * 2048, :] for r in range(4)]
    E["vt_parts"] = [v_rcv[r * 2048:(r + 1) * 2048, :] for r in range(4)]
    E["x3_out"] = x3_scr
    E["x3_sl"] = x3_scr

    if dump:
        E["dmp_qn"] = nc.dram_tensor("dmp_qn", [16, 2048], F32, kind="ExternalOutput").ap()
        E["dmp_km"] = nc.dram_tensor("dmp_km", [16, 1], F32, kind="ExternalOutput").ap()
    groups4 = [[0, 1, 2, 3], [4, 5, 6, 7]]
    with contextlib.ExitStack() as stack:
        p = P(nc, stack)
        _NAME_PREFIX[0] = "a_"
        with contextlib.ExitStack() as st:
            emit_stage_a(nc, p, st, E)
        _NAME_PREFIX[0] = ""
        p.barrier()
        def _a2a(e):
            with allow_ici_all_to_all():
                return e.collective_compute(
                    "AllToAll", ALU.bypass, replica_groups=[list(range(8))],
                    ins=[E["o_scr"]], outs=[o_rcv])
        orcv_b = Buf(o_rcv, "orcv")
        p.op("gpsimd", _a2a, writes=[orcv_b])
        p.op("gpsimd", _a2a, writes=[orcv_b])
        p.barrier()
        def _kag():
            for sn in ("q_qk0", "q_qk1"):
                p.wait("gpsimd", sn, p.sem(sn)[1])
            for _ in range(2):
                p.op("gpsimd", lambda e: e.collective_compute(
                    "AllGather", ALU.bypass, replica_groups=groups4,
                    ins=[k_scr], outs=[k_rcv]))
        _NAME_PREFIX[0] = "b_"
        with contextlib.ExitStack() as st:
            emit_stage_b(nc, p, st, E, mid_cb=_kag)
        _NAME_PREFIX[0] = ""
        for sn in ("q_v0", "q_v1"):
            p.wait("gpsimd", sn, p.sem(sn)[1])
        for _ in range(2):
            p.op("gpsimd", lambda e: e.collective_compute(
                "AllGather", ALU.bypass, replica_groups=groups4,
                ins=[v_scr], outs=[v_rcv]))
        p.barrier()
        _NAME_PREFIX[0] = "c_"
        with contextlib.ExitStack() as st:
            emit_stage_c(nc, p, st, E)
        _NAME_PREFIX[0] = ""
        p.barrier()
        _NAME_PREFIX[0] = "d_"
        with contextlib.ExitStack() as st:
            emit_stage_d(nc, p, st, E)
        _NAME_PREFIX[0] = ""
        if dump:
            dmp = {}
            dmp["dmp_o"] = nc.dram_tensor("dmp_o", [2048, 2048], BF16, kind="ExternalOutput").ap()
            dmp["dmp_kv"] = nc.dram_tensor("dmp_kv", [4 * 4096, 2048], BF16, kind="ExternalOutput").ap()
            dmp["dmp_x2"] = nc.dram_tensor("dmp_x2", [2048, 2048], F32, kind="ExternalOutput").ap()
            dmp["dmp_q"] = nc.dram_tensor("dmp_q", [2048, 2048], BF16, kind="ExternalOutput").ap()
            dmp["dmp_x3"] = nc.dram_tensor("dmp_x3", [2048, 2048], F32, kind="ExternalOutput").ap()
            p.barrier()
            for dst, srcap in (("dmp_o", o_rcv), ("dmp_kv", kv_rcv), ("dmp_x2", x2_scr),
                               ("dmp_q", q_scr), ("dmp_x3", x3_scr)):
                p.dma_serial("sync", dmp[dst], srcap, sem="q_dmp")
            p.wait("gpsimd", "q_dmp", p.sem("q_dmp")[1])
        with nc.Block() as block:
            p.emit(block)
    return nc


def fused_inmaps(inputs):
    ims_a = stage_a_inmaps(inputs)
    o_dummy = None
    x = inputs["L0_attn_hidden_states"]
    xT = np.ascontiguousarray(x.transpose(0, 2, 1))
    ims_b = stage_b_inmaps(inputs, [xT.astype(BF), xT.astype(BF)][0:1] * 2 if False else
                           np.zeros((2, 2048, 8192), dtype=BF))
    ims_c = stage_c_inmaps(inputs, [None] * 8, np.zeros((2, 1, 1), BF), np.zeros((2, 1, 1), BF),
                           [None] * 8)
    ims_d = stage_d_inmaps(inputs, [None] * 8)
    ims = []
    for c in range(8):
        m = {}
        m.update(ims_a[c])
        for k_, v_ in ims_b[c].items():
            if k_ not in ("o_sl",):
                m.setdefault(k_, v_)
        for k_, v_ in ims_c[c].items():
            if k_ not in ("q_sl", "kf", "vt", "x2_sl") and v_ is not None:
                m.setdefault(k_, v_)
        for k_, v_ in ims_d[c].items():
            if k_ not in ("x3_sl",):
                m.setdefault(k_, v_)
        ims.append(m)
    return ims


def kernel(**inputs):
    cores = list(range(8))
    nc = build_fused()
    ims = fused_inmaps(inputs)
    res = run_bass_kernel_spmd(nc, ims, core_ids=cores)
    out = np.zeros((2, 8192, 2048), np.float32)
    for c in cores:
        b, g = c // 4, c % 4
        out[b, g * 2048:(g + 1) * 2048, :] = np.asarray(res.results[c]["out_sl"]).T
    return out


# version 09
# speedup vs baseline: 1.0635x; 1.0635x over previous
import contextlib
import numpy as np
import ml_dtypes
import concourse.bass as bass
import concourse.mybir as mybir
from concourse.bass_utils import run_bass_kernel_spmd
from concourse.aot_env import allow_ici_all_to_all

F32 = mybir.dt.float32
BF16 = mybir.dt.bfloat16
AF = mybir.ActivationFunctionType
ALU = mybir.AluOpType
AX = mybir.AxisListType.X
BF = ml_dtypes.bfloat16

ENGINES = ("sync", "gpsimd", "tensor", "vector", "scalar")


class Buf:
    __slots__ = ("ap", "ptok", "ctoks", "name", "serial")

    def __init__(self, ap, name="", serial=False):
        self.ap = ap
        self.name = name
        self.serial = serial
        self.ptok = None
        self.ctoks = {}


class P:

    def __init__(self, nc, stack):
        self.nc = nc
        self.stack = stack
        self.streams = {n: [] for n in ENGINES}
        self.sems = {}
        self.lastw = {}

    def sem(self, name):
        if name not in self.sems:
            self.sems[name] = [self.stack.enter_context(self.nc.semaphore(name)), 0]
        return self.sems[name]

    def raw(self, eng, fn):
        self.streams[eng].append(fn)

    def wait(self, eng, semname, val):
        if val is None or val <= 0:
            return
        if self.lastw.get((eng, semname), 0) >= val:
            return
        self.lastw[(eng, semname)] = val
        sem = self.sem(semname)[0]
        self.raw(eng, lambda e, sem=sem, val=val: e.wait_ge(sem, val))

    def _pre(self, eng, reads, writes):
        for b in reads:
            if b.ptok:
                self.wait(eng, *b.ptok)
            if b.serial:
                for sn, v in b.ctoks.items():
                    self.wait(eng, sn, v)
        for b in writes:
            for sn, v in b.ctoks.items():
                self.wait(eng, sn, v)
            if b.ptok:
                self.wait(eng, *b.ptok)

    def _post(self, tok, reads, writes):
        for b in reads:
            b.ctoks[tok[0]] = max(b.ctoks.get(tok[0], 0), tok[1])
        for b in writes:
            b.ptok = tok
            b.ctoks = {}

    def op(self, eng, fn, reads=(), writes=(), dma=False):
        self._pre(eng, reads, writes)
        if dma:
            semname = dma if isinstance(dma, str) else "q_" + eng
            c = self.sem(semname)
            c[1] += 16
            tok = (semname, c[1])
            self.raw(eng, lambda e, fn=fn, sem=c[0], v=c[1]: fn(e).then_inc(sem, 16))
        else:
            semname = "e_" + eng
            c = self.sem(semname)
            c[1] += 1
            tok = (semname, c[1])
            self.raw(eng, lambda e, fn=fn, sem=c[0]: fn(e).then_inc(sem, 1))
        self._post(tok, reads, writes)
        return tok

    def dma_serial(self, eng, out, in_, reads=(), writes=(), sem=None, **kw):
        semname = sem or ("q_" + eng)
        self.wait(eng, semname, self.sem(semname)[1])
        return self.dma(eng, out, in_, reads=reads, writes=writes, sem=semname, **kw)

    def dma(self, eng, out, in_, reads=(), writes=(), sem=None, **kw):
        return self.op(
            eng,
            lambda e, out=out, in_=in_, kw=kw: e.dma_start(out=out, in_=in_, **kw),
            reads=reads, writes=writes, dma=(sem or True),
        )

    def mmgroup(self, psum_buf, out_ap, pairs, reads=()):
        self._pre("tensor", reads, [psum_buf])
        n = len(pairs)
        c = self.sem("e_tensor")
        for i, (lhsT, rhs) in enumerate(pairs):
            if i == n - 1:
                c[1] += 1
                self.raw("tensor", lambda e, l=lhsT, r=rhs, o=out_ap, s=(i == 0), sem=c[0]:
                         e.matmul(o, l, r, start=s, stop=True).then_inc(sem, 1))
            else:
                self.raw("tensor", lambda e, l=lhsT, r=rhs, o=out_ap, s=(i == 0):
                         e.matmul(o, l, r, start=s, stop=False))
        tok = ("e_tensor", c[1])
        self._post(tok, reads, [psum_buf])
        return tok

    def barrier(self):
        snap = [(sn, c[1]) for sn, c in self.sems.items()]
        for eng in ENGINES:
            for sn, v in snap:
                if sn == "e_" + eng:
                    continue
                self.wait(eng, sn, v)

    def accum_step(self, psum_buf, out_ap, lhsT, rhs, start, stop, reads=()):
        if start:
            self._pre("tensor", reads, [psum_buf])
        else:
            self._pre("tensor", reads, [])
        if stop:
            c = self.sem("e_tensor")
            c[1] += 1
            self.raw("tensor", lambda e, l=lhsT, r=rhs, o=out_ap, s=start, sem=c[0]:
                     e.matmul(o, l, r, start=s, stop=True).then_inc(sem, 1))
            tok = ("e_tensor", c[1])
            for b in reads:
                b.ctoks[tok[0]] = max(b.ctoks.get(tok[0], 0), tok[1])
            psum_buf.ptok = tok
            psum_buf.ctoks = {}
            return tok
        else:
            self.raw("tensor", lambda e, l=lhsT, r=rhs, o=out_ap, s=start:
                     e.matmul(o, l, r, start=s, stop=False))
            tok = ("e_tensor", self.sem("e_tensor")[1])
            for b in reads:
                b.ctoks[tok[0]] = max(b.ctoks.get(tok[0], 0), tok[1] + 1)
            return None

    def emit(self, block):
        for n in ENGINES:
            fns = self.streams[n]
            if not fns:
                continue
            dec = getattr(block, n)

            def body(e, fns=fns):
                for f in fns:
                    f(e)
            dec(body)


_NAME_PREFIX = [""]


def _sb(nc, stack, name, shape, dt):
    name = _NAME_PREFIX[0] + name
    return Buf(stack.enter_context(nc.sbuf_tensor(name, list(shape), dt)), name)


def _bank(nc, stack, name, dt=F32, cols=None):
    name = _NAME_PREFIX[0] + name
    cols = cols or (512 if dt == F32 else 1024)
    return Buf(stack.enter_context(nc.psum_tensor(name, [128, cols], dt)), name, serial=True)


class Pool:

    def __init__(self, banks):
        self.banks = banks
        self.i = 0

    def take(self):
        b = self.banks[self.i % len(self.banks)]
        self.i += 1
        return b



def emit_stage_a(nc, p, stack, E, NCH=64, debug=False):
    H2 = 2
    D = 2048
    T = NCH * 128
    GSZ = NCH // 4

    xfm = E["xfm"]
    wq_s = E["wq_s"]
    wk_s = E["wk_s"]
    wv_s = E["wv_s"]
    ws_s = E["ws_s"]
    wg_s = E["wg_s"]
    gnorm = E["gnorm"]
    u32 = E["u32"]
    ubf = E["ubf"]
    idbf = E["idbf"]
    ones1 = E["ones1"]
    onesc = E["onesc"]

    o_scr = E["o_scr"]
    dbg = {}
    if debug:
        for nm, sh, dt in [("dQs", (64, 128), F32), ("dKf", (64, 128), F32), ("dKtm", (128, 128), F32),
                           ("dSc", (128, 128), F32), ("dE32", (128, 128), F32), ("dDd", (128, 128), F32),
                           ("dokk", (128, 64), F32), ("dpvU", (128, 64), F32), ("dpvD", (128, 64), F32),
                           ("dEt", (64, 128), F32), ("dpvDt", (64, 128), F32), ("dBTm", (128, 128), F32),
                           ("dATm", (128, 128), F32), ("doh", (128, 128), F32), ("dVtm", (128, 256), F32),
                           ("dGt", (128, 256), F32), ("drstd", (128, 1), F32)]:
            dbg[nm] = nc.dram_tensor(nm, list(sh), dt, kind="ExternalOutput").ap()

    W = {}
    W["q"] = _sb(nc, stack, "w_q", (128, 16 * 128), BF16)
    W["k"] = _sb(nc, stack, "w_k", (128, 16 * 128), BF16)
    W["s"] = _sb(nc, stack, "w_s", (128, 16 * 128), BF16)
    W["v"] = _sb(nc, stack, "w_v", (128, 16 * 256), BF16)
    W["g"] = _sb(nc, stack, "w_g", (128, 16 * 256), BF16)
    sU32 = _sb(nc, stack, "sU32", (128, 128), F32)
    sUbf = _sb(nc, stack, "sUbf", (128, 128), BF16)
    sIdb = _sb(nc, stack, "sIdb", (128, 128), BF16)
    sOne1 = _sb(nc, stack, "sOne1", (1, 128), F32)
    sOneC = _sb(nc, stack, "sOneC", (128, 1), F32)
    sGn = _sb(nc, stack, "sGn", (1, 128), F32)
    sGnB = _sb(nc, stack, "sGnB", (128, 128), F32)

    xb = [[_sb(nc, stack, f"xb{b}{i}", (128, 16 * 128), BF16) for i in range(2)] for b in range(2)]

    Qs = [[[_sb(nc, stack, f"Qs{s}{b}{k}", (64, 128), BF16) for k in range(H2)] for b in range(2)] for s in range(2)]
    Kf = [[[_sb(nc, stack, f"Kf{s}{b}{k}", (64, 128), BF16) for k in range(H2)] for b in range(2)] for s in range(2)]
    Ktm = [[_sb(nc, stack, f"Ktm{s}{b}", (128, 128), BF16) for b in range(2)] for s in range(2)]
    Sc = [[_sb(nc, stack, f"Sc{s}{b}", (128, 128), F32) for b in range(2)] for s in range(2)]
    Vtm = [[_sb(nc, stack, f"Vtm{s}{b}", (128, 256), BF16) for b in range(2)] for s in range(2)]
    sGt = [[_sb(nc, stack, f"sGt{s}{b}", (128, 256), F32) for b in range(2)] for s in range(2)]
    E32 = [_sb(nc, stack, f"E32{b}", (128, 128), F32) for b in range(2)]
    Ebf = [_sb(nc, stack, f"Ebf{b}", (128, 128), BF16) for b in range(2)]
    Dd = [_sb(nc, stack, f"Dd{b}", (128, 128), F32) for b in range(2)]
    ATm = [_sb(nc, stack, f"ATm{c}", (128, 128), BF16) for c in range(4)]
    okk = [_sb(nc, stack, f"okk{c}", (128, 64), F32) for c in range(4)]
    nmx = [_sb(nc, stack, f"nmx{c}", (128, 1), F32) for c in range(4)]
    pvU = [_sb(nc, stack, f"pvU{c}", (128, 64), F32) for c in range(4)]
    ssum = [_sb(nc, stack, f"ssum{c}", (128, 1), F32) for c in range(4)]
    rs_ = [_sb(nc, stack, f"rs_{c}", (128, 1), F32) for c in range(4)]
    pvD = [_sb(nc, stack, f"pvD{c}", (128, 64), BF16) for c in range(4)]
    Et = [_sb(nc, stack, f"Et{c}", (64, 128), BF16) for c in range(4)]
    pvDt = [_sb(nc, stack, f"pvDt{c}", (64, 128), BF16) for c in range(4)]
    BTm = [_sb(nc, stack, f"BTm{c}", (128, 128), BF16) for c in range(4)]
    ssq = [_sb(nc, stack, f"ssq{c}", (128, 1), F32) for c in range(4)]
    rstd = [_sb(nc, stack, f"rstd{c}", (128, 1), F32) for c in range(4)]
    rstd2 = [_sb(nc, stack, f"rstd2{c}", (128, 1), F32) for c in range(4)]
    gate2 = [_sb(nc, stack, f"gate2{c}", (128, 128), F32) for c in range(4)]
    o_h = [_sb(nc, stack, f"o_h{c}", (128, 128), BF16) for c in range(4)]
    scr128 = _sb(nc, stack, "scr128", (128, 128), F32)
    sEps = _sb(nc, stack, "sEps", (128, 1), F32)

    Wrow = [[_sb(nc, stack, f"Wr{b}{i}", (1, 128), F32) for i in range(2)] for b in range(2)]
    HkU = [[_sb(nc, stack, f"HkU{c}{i}", (64, 64), F32) for i in range(2)] for c in range(4)]
    HvU = [[_sb(nc, stack, f"HvU{c}{i}", (64, 128), F32) for i in range(2)] for c in range(4)]
    HkB = [_sb(nc, stack, f"HkB{c}", (64, 64), BF16) for c in range(4)]
    HvB = [_sb(nc, stack, f"HvB{c}", (64, 128), BF16) for c in range(4)]

    obuf = [_sb(nc, stack, f"obuf{i}", (128, 4 * GSZ * 128), BF16) for i in range(2)]

    prj = Pool([_bank(nc, stack, "bkP0"), _bank(nc, stack, "bkP1")])
    pool = Pool([_bank(nc, stack, "bkA"), _bank(nc, stack, "bkB")])
    bOv_ = _bank(nc, stack, "bkOv0")
    bOv = [bOv_, bOv_]
    bT = [_bank(nc, stack, "bkT0", BF16), _bank(nc, stack, "bkT1", BF16)]
    bM = _bank(nc, stack, "bkM")

    for buf, src in ((sU32, u32), (sUbf, ubf), (sIdb, idbf), (sOne1, ones1),
                     (sOneC, onesc), (sGn, gnorm)):
        p.dma_serial("sync", buf.ap[:], src, writes=[buf], sem="q_cst")
    for name, src, wdt in (("q", wq_s, 128), ("k", wk_s, 128), ("s", ws_s, 128),
                           ("v", wv_s, 256), ("g", wg_s, 256)):
        p.dma_serial("gpsimd", W[name].ap[:].rearrange("p (c o) -> p c o", o=wdt),
              src.rearrange("(c p) o -> p c o", p=128),
              writes=[W[name]], sem="q_wld")
    p.op("tensor", lambda e: e.matmul(bM.ap[:, 0:128], sOne1.ap[:], sGn.ap[:], start=True, stop=True),
         reads=[sOne1, sGn], writes=[bM])
    p.op("vector", lambda e: e.tensor_copy(sGnB.ap[:], bM.ap[:, 0:128]), reads=[bM], writes=[sGnB])
    p.op("vector", lambda e: e.memset(sEps.ap[:], 1e-5), writes=[sEps])
    for b in range(2):
        for i in range(2):
            p.op("vector", lambda e, t=Wrow[b][i]: e.memset(t.ap[:], 0.0), writes=[Wrow[b][i]])
    for c in range(4):
        for i in range(2):
            p.op("vector", lambda e, t=HkU[c][i]: e.memset(t.ap[:], 0.0), writes=[HkU[c][i]])
            p.op("vector", lambda e, t=HvU[c][i]: e.memset(t.ap[:], 0.0), writes=[HvU[c][i]])
        p.op("vector", lambda e, t=HkB[c]: e.memset(t.ap[:], 0.0), writes=[HkB[c]])
        p.op("vector", lambda e, t=HvB[c]: e.memset(t.ap[:], 0.0), writes=[HvB[c]])

    n_odma = [0]

    def proj(j):
        sl = j % 2
        Qs_, Kf_, Ktm_, Sc_, Vtm_, sGt_ = Qs[sl], Kf[sl], Ktm[sl], Sc[sl], Vtm[sl], sGt[sl]
        for b in range(2):
            x = xb[b][j % 2]
            p.dma("gpsimd", x.ap[:].rearrange("p (c t) -> p c t", t=128),
                  xfm[b, :, j * 128:(j + 1) * 128].rearrange("(c p) t -> p c t", p=128),
                  writes=[x], sem=f"q_x{b}{j % 2}")
            kc = [x.ap[:, i * 128:(i + 1) * 128] for i in range(16)]
            bk = prj.take()
            p.mmgroup(bk, bk.ap[:, 0:128],
                      [(W["q"].ap[:, i * 128:(i + 1) * 128], kc[i]) for i in range(16)],
                      reads=[W["q"], x])
            for k in range(H2):
                p.op("scalar", lambda e, k=k, b=b, bk=bk: e.activation(
                    Qs_[b][k].ap[:], bk.ap[k * 64:(k + 1) * 64, 0:128], AF.Copy, scale=0.125),
                    reads=[bk], writes=[Qs_[b][k]])
            bk = prj.take()
            p.mmgroup(bk, bk.ap[:, 0:128],
                      [(W["k"].ap[:, i * 128:(i + 1) * 128], kc[i]) for i in range(16)],
                      reads=[W["k"], x])
            for k in range(H2):
                p.op("scalar", lambda e, k=k, b=b, bk=bk: e.copy(
                    Kf_[b][k].ap[:], bk.ap[k * 64:(k + 1) * 64, 0:128]),
                    reads=[bk], writes=[Kf_[b][k]])
            bk = prj.take()
            p.mmgroup(bk, bk.ap[:, 0:128],
                      [(kc[i], W["k"].ap[:, i * 128:(i + 1) * 128]) for i in range(16)],
                      reads=[W["k"], x])
            p.op("vector", lambda e, b=b, bk=bk: e.tensor_copy(Ktm_[b].ap[:], bk.ap[:, 0:128]),
                 reads=[bk], writes=[Ktm_[b]])
            bk = prj.take()
            p.mmgroup(bk, bk.ap[:, 0:128],
                      [(kc[i], W["s"].ap[:, i * 128:(i + 1) * 128]) for i in range(16)],
                      reads=[W["s"], x])
            p.op("vector", lambda e, b=b, bk=bk: e.tensor_scalar(
                Sc_[b].ap[:], bk.ap[:, 0:128], 32.0, -32.0, ALU.min, ALU.max),
                reads=[bk], writes=[Sc_[b]])
            bk = prj.take()
            p.mmgroup(bk, bk.ap[:, 0:256],
                      [(kc[i], W["v"].ap[:, i * 256:(i + 1) * 256]) for i in range(16)],
                      reads=[W["v"], x])
            p.op("vector", lambda e, b=b, bk=bk: e.tensor_copy(Vtm_[b].ap[:], bk.ap[:, 0:256]),
                 reads=[bk], writes=[Vtm_[b]])
            bk = prj.take()
            p.mmgroup(bk, bk.ap[:, 0:256],
                      [(kc[i], W["g"].ap[:, i * 256:(i + 1) * 256]) for i in range(16)],
                      reads=[W["g"], x])
            p.op("scalar", lambda e, b=b, bk=bk: e.activation(sGt_[b].ap[:], bk.ap[:, 0:256], AF.Silu),
                 reads=[bk], writes=[sGt_[b]])


    def cproc(j):
        sl = j % 2
        g = j // GSZ
        jj = j % GSZ
        ob = obuf[g % 2]
        Qs_, Kf_, Ktm_, Sc_, Vtm_, sGt_ = Qs[sl], Kf[sl], Ktm[sl], Sc[sl], Vtm[sl], sGt[sl]
        for b in range(2):
            p.op("scalar", lambda e, b=b: e.activation(E32[b].ap[:], Sc_[b].ap[:], AF.Exp),
                 reads=[Sc_[b]], writes=[E32[b]])
            p.op("vector", lambda e, b=b: e.tensor_copy(Ebf[b].ap[:], E32[b].ap[:]),
                 reads=[E32[b]], writes=[Ebf[b]])
            wr_old = Wrow[b][j % 2]
            wr_new = Wrow[b][(j + 1) % 2]
            bk = pool.take()
            p.mmgroup(bk, bk.ap[:, 0:128], [(sU32.ap[:], E32[b].ap[:]), (sOne1.ap[:], wr_old.ap[:])],
                      reads=[sU32, E32[b], sOne1, wr_old])
            p.op("vector", lambda e, b=b, bk=bk: e.reciprocal(Dd[b].ap[:], bk.ap[:, 0:128]),
                 reads=[bk], writes=[Dd[b]])
            p.op("tensor", lambda e, b=b: e.matmul(bM.ap[0:1, 0:128], sOneC.ap[:], E32[b].ap[:],
                                                   start=True, stop=True),
                 reads=[sOneC, E32[b]], writes=[bM])
            p.op("vector", lambda e, wo=wr_old, wn=wr_new: e.tensor_tensor(
                wn.ap[:], wo.ap[:], bM.ap[0:1, 0:128], ALU.add),
                reads=[wr_old, bM], writes=[wr_new])

            def tap(nm, buf, ap=None):
                if debug and j == 0 and b == 0:
                    tmp = dbg[nm]
                    p.dma_serial("gpsimd", tmp, ap if ap is not None else buf.ap[:], reads=[buf], sem="q_dbg")
            if debug and j == 0 and b == 0:
                tap("dE32", E32[0]); tap("dDd", Dd[0]); tap("dSc", Sc_[0])
                tap("dKtm", Ktm_[0]); tap("dVtm", Vtm_[0]); tap("dGt", sGt_[0])
                tap("dQs", Qs_[0][0]); tap("dKf", Kf_[0][0])
            for k in range(H2):
                c = b * 2 + k
                ms = slice(k * 64, (k + 1) * 64)
                vs = slice(k * 128, (k + 1) * 128)
                tAT, tok_, tnm, tpv, tss, trs, tpd = ATm[c], okk[c], nmx[c], pvU[c], ssum[c], rs_[c], pvD[c]
                tEt, tpt, tBT = Et[c], pvDt[c], BTm[c]
                tb = bT[c % 2]
                bk = pool.take()
                p.op("tensor", lambda e, b=b, k=k, bk=bk: e.matmul(
                    bk.ap[:, 0:128], Kf_[b][k].ap[:], Qs_[b][k].ap[:], start=True, stop=True),
                    reads=[Kf_[b][k], Qs_[b][k]], writes=[bk])
                p.op("vector", lambda e, bk=bk, tAT=tAT: e.tensor_tensor(
                    tAT.ap[:], bk.ap[:, 0:128], sUbf.ap[:], ALU.mult),
                    reads=[bk, sUbf], writes=[tAT])
                if k == 0:
                    tap("dATm", tAT)
                bk = pool.take()
                p.mmgroup(bk, bk.ap[:, 0:64], [
                    (tAT.ap[:], Ebf[b].ap[:, ms]),
                    (Qs_[b][k].ap[:], HkB[c].ap[:]),
                ], reads=[tAT, Ebf[b], Qs_[b][k], HkB[c]])
                p.op("vector", lambda e, b=b, ms=ms, bk=bk, tok_=tok_: e.tensor_tensor(
                    tok_.ap[:], bk.ap[:, 0:64], Dd[b].ap[:, ms], ALU.mult),
                    reads=[bk, Dd[b]], writes=[tok_])
                p.op("vector", lambda e, tok_=tok_, tnm=tnm: e.reduce_max(
                    tnm.ap[:], tok_.ap[:], axis=AX, negate=True),
                    reads=[tok_], writes=[tnm])
                p.op("scalar", lambda e, tok_=tok_, tnm=tnm, tpv=tpv, tss=tss: e.activation(
                    tpv.ap[:], tok_.ap[:], AF.Exp, bias=tnm.ap[:], accum_out=tss.ap[:]),
                    reads=[tok_, tnm], writes=[tpv, tss])
                p.op("vector", lambda e, tss=tss, trs=trs: e.reciprocal(trs.ap[:], tss.ap[:]),
                     reads=[tss], writes=[trs])
                p.op("vector", lambda e, b=b, ms=ms, tpv=tpv, trs=trs, tpd=tpd: e.scalar_tensor_tensor(
                    tpd.ap[:], tpv.ap[:], trs.ap[:], Dd[b].ap[:, ms], ALU.mult, ALU.mult),
                    reads=[tpv, trs, Dd[b]], writes=[tpd])
                if k == 0:
                    tap("dokk", tok_); tap("dpvU", tpv); tap("dpvD", tpd)
                p.op("tensor", lambda e, b=b, ms=ms, tb=tb: e.transpose(
                    tb.ap[0:64, 0:128], Ebf[b].ap[:, ms], sIdb.ap[:]),
                    reads=[Ebf[b], sIdb], writes=[tb])
                p.op("vector", lambda e, tb=tb, tEt=tEt: e.tensor_copy(tEt.ap[:], tb.ap[0:64, 0:128]),
                     reads=[tb], writes=[tEt])
                p.op("tensor", lambda e, tb=tb, tpd=tpd: e.transpose(
                    tb.ap[0:64, 128:256], tpd.ap[:], sIdb.ap[:]),
                    reads=[tpd, sIdb], writes=[tb])
                p.op("vector", lambda e, tb=tb, tpt=tpt: e.tensor_copy(tpt.ap[:], tb.ap[0:64, 128:256]),
                     reads=[tb], writes=[tpt])
                if k == 0:
                    tap("dEt", tEt); tap("dpvDt", tpt)
                bk = pool.take()
                p.op("tensor", lambda e, bk=bk, tEt=tEt, tpt=tpt: e.matmul(
                    bk.ap[:, 0:128], tEt.ap[:], tpt.ap[:], start=True, stop=True),
                    reads=[tEt, tpt], writes=[bk])
                p.op("vector", lambda e, bk=bk, tBT=tBT: e.tensor_tensor(
                    tBT.ap[:], bk.ap[:, 0:128], sUbf.ap[:], ALU.mult),
                    reads=[bk, sUbf], writes=[tBT])
                if k == 0:
                    tap("dBTm", tBT)
                p.mmgroup(bOv[b], bOv[b].ap[:, b * 256 + vs.start:b * 256 + vs.stop] if False else bOv[b].ap[:, vs], [
                    (tBT.ap[:], Vtm_[b].ap[:, vs]),
                    (tpt.ap[:], HvB[c].ap[:]),
                ], reads=[tBT, Vtm_[b], tpt, HvB[c]])
                p.op("tensor", lambda e, b=b, ms=ms: e.matmul(
                    bM.ap[0:64, 128:192], Ktm_[b].ap[:, ms], Ebf[b].ap[:, ms], start=True, stop=True),
                    reads=[Ktm_[b], Ebf[b]], writes=[bM])
                hk_old, hk_new = HkU[c][j % 2], HkU[c][(j + 1) % 2]
                p.op("vector", lambda e, ho=hk_old, hn=hk_new: e.tensor_tensor(
                    hn.ap[:], ho.ap[:], bM.ap[0:64, 128:192], ALU.add),
                    reads=[hk_old, bM], writes=[hk_new])
                p.op("vector", lambda e, c=c, hn=hk_new: e.tensor_copy(HkB[c].ap[:], hn.ap[:]),
                     reads=[hk_new], writes=[HkB[c]])
                p.op("tensor", lambda e, b=b, ms=ms, vs=vs: e.matmul(
                    bM.ap[0:64, 256:384], Ebf[b].ap[:, ms], Vtm_[b].ap[:, vs], start=True, stop=True),
                    reads=[Ebf[b], Vtm_[b]], writes=[bM])
                hv_old, hv_new = HvU[c][j % 2], HvU[c][(j + 1) % 2]
                p.op("vector", lambda e, ho=hv_old, hn=hv_new: e.tensor_tensor(
                    hn.ap[:], ho.ap[:], bM.ap[0:64, 256:384], ALU.add),
                    reads=[hv_old, bM], writes=[hv_new])
                p.op("vector", lambda e, c=c, hn=hv_new: e.tensor_copy(HvB[c].ap[:], hn.ap[:]),
                     reads=[hv_new], writes=[HvB[c]])

            for k in range(H2):
                c = b * 2 + k
                vs = slice(k * 128, (k + 1) * 128)
                tsq, tr1, tr2, tg2, toh = ssq[c], rstd[c], rstd2[c], gate2[c], o_h[c]
                tb = bT[c % 2]
                p.op("scalar", lambda e, b=b, vs=vs, tsq=tsq: e.activation(
                    scr128.ap[:], bOv[b].ap[:, vs], AF.Square, accum_out=tsq.ap[:]),
                    reads=[bOv[b]], writes=[scr128, tsq])
                p.op("scalar", lambda e, tsq=tsq, tr1=tr1: e.activation(
                    tr1.ap[:], tsq.ap[:], AF.Ln, scale=1.0 / 128.0, bias=sEps.ap[:]),
                    reads=[tsq, sEps], writes=[tr1])
                p.op("scalar", lambda e, tr1=tr1, tr2=tr2: e.activation(
                    tr2.ap[:], tr1.ap[:], AF.Exp, scale=-0.5),
                    reads=[tr1], writes=[tr2])
                p.op("vector", lambda e, b=b, vs=vs, tg2=tg2: e.tensor_tensor(
                    tg2.ap[:], sGt_[b].ap[:, vs], sGnB.ap[:], ALU.mult),
                    reads=[sGt_[b], sGnB], writes=[tg2])
                p.op("vector", lambda e, b=b, vs=vs, tr2=tr2, tg2=tg2, toh=toh: e.scalar_tensor_tensor(
                    toh.ap[:], bOv[b].ap[:, vs], tr2.ap[:], tg2.ap[:], ALU.mult, ALU.mult),
                    reads=[bOv[b], tr2, tg2], writes=[toh])
                if k == 0:
                    tap("doh", toh); tap("drstd", tr2)
                p.op("tensor", lambda e, tb=tb, toh=toh: e.transpose(
                    tb.ap[:, 256:384], toh.ap[:], sIdb.ap[:]),
                    reads=[toh, sIdb], writes=[tb])
                ocol = (c * GSZ + jj) * 128
                p.op("vector", lambda e, ocol=ocol, ob=ob, tb=tb: e.tensor_copy(
                    ob.ap[:, ocol:ocol + 128], tb.ap[:, 256:384]),
                    reads=[tb], writes=[ob])

        if jj == GSZ - 1:
            for c in range(4):
                b = c // 2
                k = c % 2
                dst = 4 * b + g
                p.dma("sync", o_scr[dst * 256 + k * 128:dst * 256 + (k + 1) * 128, :],
                      ob.ap[:, c * GSZ * 128:(c + 1) * GSZ * 128],
                      reads=[ob], sem=f"q_out{g % 2}")
                n_odma[0] += 1


    for j in range(NCH):
        proj(j)
        cproc(j)

    p.wait("gpsimd", "q_out0", p.sem("q_out0")[1])
    p.wait("gpsimd", "q_out1", p.sem("q_out1")[1])
    p.raw("gpsimd", lambda e: e.engine_nop())




def _bcols(v, dtype=np.float32):
    return np.ascontiguousarray(v.reshape(-1, 128).T).astype(dtype)


def _consts_a():
    u = np.triu(np.ones((128, 128), np.float32))
    return dict(
        u32=u,
        ubf=u.astype(BF),
        idbf=np.eye(128, dtype=BF),
        ones1=np.ones((1, 128), np.float32),
        onesc=np.ones((128, 1), np.float32),
    )


def stage_a_inmaps(inputs):
    x = inputs["L0_attn_hidden_states"]
    xT = np.ascontiguousarray(x.transpose(0, 2, 1))
    consts = _consts_a()
    ims = []
    for c in range(8):
        h0 = 2 * c
        ims.append(dict(
            xfm=xT,
            wq_s=np.ascontiguousarray(inputs["L0_attn_Wq"][:, h0 * 64:(h0 + 2) * 64]),
            wk_s=np.ascontiguousarray(inputs["L0_attn_Wk"][:, h0 * 64:(h0 + 2) * 64]),
            wv_s=np.ascontiguousarray(inputs["L0_attn_Wv"][:, h0 * 128:(h0 + 2) * 128]),
            ws_s=np.ascontiguousarray(inputs["L0_attn_Ws"][:, h0 * 64:(h0 + 2) * 64]),
            wg_s=np.ascontiguousarray(inputs["L0_attn_Wg"][:, h0 * 128:(h0 + 2) * 128]),
            gnorm=np.ascontiguousarray(inputs["L0_attn_g_norm_weight"].reshape(1, 128)),
            **consts,
        ))
    return ims


def assemble_o(results):
    o = np.zeros((2, 2048, 8192), dtype=BF)
    for c in range(8):
        sc = np.asarray(results[c]["o_scr"]).reshape(8, 256, -1)
        for dst in range(8):
            b, gg = dst // 4, dst % 4
            o[b, 2 * c * 128:(2 * c + 2) * 128, gg * 2048:(gg + 1) * 2048] = sc[dst]
    return o


def kernel_unfused(**inputs):
    cores = list(range(8))
    nc_a = build_stage_a()
    res_a = run_bass_kernel_spmd(nc_a, stage_a_inmaps(inputs), core_ids=cores)
    o_fm = assemble_o(res_a.results)

    nc_b = build_stage_b()
    res_b = run_bass_kernel_spmd(nc_b, stage_b_inmaps(inputs, o_fm), core_ids=cores)
    k_fm = np.zeros((2, 2048, 8192), dtype=BF)
    v_tm = np.zeros((2, 8192, 2048), dtype=BF)
    q_fm = []
    x2_fm = []
    for c in cores:
        b, g = c // 4, c % 4
        r = res_b.results[c]
        k_fm[b][:, g * 2048:(g + 1) * 2048] = np.asarray(r["k_out"])
        v_tm[b][g * 2048:(g + 1) * 2048, :] = np.asarray(r["v_out"])
        q_fm.append(np.asarray(r["q_out"]))
        x2_fm.append(np.asarray(r["x2_out"]))

    nc_c = build_stage_c()
    res_c = run_bass_kernel_spmd(nc_c, stage_c_inmaps(inputs, q_fm, k_fm, v_tm, x2_fm), core_ids=cores)
    x3_fm = [np.asarray(res_c.results[c]["x3_out"]) for c in cores]

    nc_d = build_stage_d()
    res_d = run_bass_kernel_spmd(nc_d, stage_d_inmaps(inputs, x3_fm), core_ids=cores)
    out = np.zeros((2, 8192, 2048), np.float32)
    for c in cores:
        b, g = c // 4, c % 4
        out[b, g * 2048:(g + 1) * 2048, :] = np.asarray(res_d.results[c]["out_sl"]).T
    return out



def emit_stage_b(nc, p, stack, E, mid_cb=None):
    D = 2048
    SC = 2048.0 ** -0.5

    o_sl = E["o_sl"]
    x_sl = E["x_sl"]
    wo0 = E["wo0"]
    basewT = E["basewT"]
    basebc = E["basebc"]
    rwT = E["rwT"]
    rbrow = E["rbrow"]
    lA = E["lA"]
    lB = E["lB"]
    wq1 = E["wq1"]
    wk1 = E["wk1"]
    wv1 = E["wv1"]
    bqc = E["bqc"]
    bkc = E["bkc"]
    bvr = E["bvr"]
    inds = E["inds"]
    ones1 = E["ones1"]
    onesc = E["onesc"]
    idf = E["idf"]

    x2_out = E["x2_out"]
    q_out = E["q_out"]
    k_out = E["k_out"]
    v_out = E["v_out"]
    x1_scr = E["x1_scr"]

    sOne1 = _sb(nc, stack, "sOne1", (1, 128), F32)
    sOneC = _sb(nc, stack, "sOneC", (128, 1), F32)
    sInds = _sb(nc, stack, "sInds", (8, 128), F32)
    sIdf = _sb(nc, stack, "sIdf", (128, 128), F32)
    sEps1 = _sb(nc, stack, "sEps1", (1, 1), F32)
    sRb = _sb(nc, stack, "sRb", (1, 8), F32)
    sBvr = _sb(nc, stack, "sBvr", (1, D), F32)
    sRcB = _sb(nc, stack, "sRcB", (128, 8), F32)
    sBvB = _sb(nc, stack, "sBvB", (128, 512), F32)
    sBqs = _sb(nc, stack, "sBqs", (128, 16), F32)
    sBkc = _sb(nc, stack, "sBkc", (128, 16), F32)
    sBbc = _sb(nc, stack, "sBbc", (128, 16), F32)
    sRwT = _sb(nc, stack, "sRwT", (128, 16 * 8), BF16)
    sLA = _sb(nc, stack, "sLA", (128, 16 * 128), BF16)
    sLB = _sb(nc, stack, "sLB", (128, 16 * 128), BF16)
    wsl = [_sb(nc, stack, f"wsl{i}", (128, 16 * 128), BF16) for i in range(2)]
    bsl = [_sb(nc, stack, f"bsl{i}", (128, 16 * 512), BF16) for i in range(2)]
    vsl = bsl
    osl = bsl
    x1sl = bsl
    xsl = [_sb(nc, stack, f"xsl{i}", (128, 512), F32) for i in range(4)]
    xrb = _sb(nc, stack, "xrb", (128, 16 * 512), F32)
    sqs = [_sb(nc, stack, f"sqs{i}", (128, 512), F32) for i in range(2)]
    scA = [_sb(nc, stack, f"scA{i}", (128, 512), F32) for i in range(2)]
    scB = [_sb(nc, stack, f"scB{i}", (128, 512), F32) for i in range(2)]
    ebf = [_sb(nc, stack, f"ebf{i}", (128, 512), BF16) for i in range(2)]
    x2b = _sb(nc, stack, "x2b", (128, 16 * 2048), BF16)
    rowA = _sb(nc, stack, "rowA", (1, 512), F32)
    rowB = _sb(nc, stack, "rowB", (1, 512), F32)
    rowC = _sb(nc, stack, "rowC", (1, 512), F32)
    lg8 = _sb(nc, stack, "lg8", (128, 8), F32)
    pr8 = _sb(nc, stack, "pr8", (128, 8), F32)
    prb = _sb(nc, stack, "prb", (128, 8), F32)
    ind8 = _sb(nc, stack, "ind8", (128, 8), F32)
    wg8 = _sb(nc, stack, "wg8", (128, 8), F32)
    colA = _sb(nc, stack, "colA", (128, 1), F32)
    colB = _sb(nc, stack, "colB", (128, 1), F32)
    wT8 = _sb(nc, stack, "wT8", (8, 512), F32)
    gB = _sb(nc, stack, "gB", (128, 128), F32)
    midg = _sb(nc, stack, "midg", (128, 512), BF16)

    pool = Pool([_bank(nc, stack, "bkA"), _bank(nc, stack, "bkB")])
    bS1 = _bank(nc, stack, "bS1")
    bS2 = _bank(nc, stack, "bS2")
    bB1 = _bank(nc, stack, "bB1")
    bB2 = _bank(nc, stack, "bB2")
    bSm = _bank(nc, stack, "bSm")
    bSm2 = _bank(nc, stack, "bSm2")

    for buf, src_ in ((sOne1, ones1), (sOneC, onesc), (sInds, inds), (sIdf, idf),
                      (sBqs, bqc), (sBkc, bkc), (sBbc, basebc), (sRb, rbrow), (sBvr, bvr)):
        p.dma_serial("sync", buf.ap[:], src_, writes=[buf], sem="q_cst")
    p.dma_serial("gpsimd", sRwT.ap[:].rearrange("p (c o) -> p c o", o=8),
                 rwT.rearrange("(c p) o -> p c o", p=128), writes=[sRwT], sem="q_wld")
    p.dma_serial("gpsimd", sLA.ap[:].rearrange("p (c o) -> p c o", o=128),
                 lA.rearrange("(c p) o -> p c o", p=128), writes=[sLA], sem="q_wld")
    p.dma_serial("gpsimd", sLB.ap[:].rearrange("p (c o) -> p c o", o=128),
                 lB.rearrange("(c p) o -> p c o", p=128), writes=[sLB], sem="q_wld")
    p.op("vector", lambda e: e.memset(sEps1.ap[:], 1e-5), writes=[sEps1])
    p.op("vector", lambda e: e.tensor_scalar(sBqs.ap[:], sBqs.ap[:], SC, None, ALU.mult, ALU.bypass),
         reads=[sBqs], writes=[sBqs])
    p.op("tensor", lambda e: e.matmul(bSm.ap[:, 0:8], sOne1.ap[:], sRb.ap[:], start=True, stop=True),
         reads=[sOne1, sRb], writes=[bSm])
    p.op("vector", lambda e: e.tensor_copy(sRcB.ap[:], bSm.ap[:, 0:8]), reads=[bSm], writes=[sRcB])

    def wstream(idx, wdram, oc, semn="q_w"):
        sl = wsl[idx % 2]
        p.dma("gpsimd", sl.ap[:].rearrange("p (c o) -> p c o", o=128),
              wdram[:, oc * 128:(oc + 1) * 128].rearrange("(c p) o -> p c o", p=128),
              writes=[sl], sem=f"{semn}{idx % 2}")
        return sl

    def ln_stats():
        p.op("vector", lambda e: e.tensor_scalar(rowA.ap[:], bS1.ap[0:1, 0:512], 1.0 / D, None, ALU.mult, ALU.bypass),
             reads=[bS1], writes=[rowA])
        p.op("vector", lambda e: e.tensor_scalar(rowB.ap[:], bS2.ap[0:1, 0:512], 1.0 / D, None, ALU.mult, ALU.bypass),
             reads=[bS2], writes=[rowB])
        p.op("vector", lambda e: e.tensor_tensor(rowC.ap[:], rowA.ap[:], rowA.ap[:], ALU.mult),
             reads=[rowA], writes=[rowC])
        p.op("vector", lambda e: e.tensor_tensor(rowB.ap[:], rowB.ap[:], rowC.ap[:], ALU.subtract),
             reads=[rowB, rowC], writes=[rowB])
        p.op("scalar", lambda e: e.activation(rowB.ap[:], rowB.ap[:], AF.Ln, bias=sEps1.ap[:]),
             reads=[rowB, sEps1], writes=[rowB])
        p.op("scalar", lambda e: e.activation(rowB.ap[:], rowB.ap[:], AF.Exp, scale=-0.5),
             reads=[rowB], writes=[rowB])
        p.op("vector", lambda e: e.tensor_tensor(rowC.ap[:], rowA.ap[:], rowB.ap[:], ALU.mult),
             reads=[rowA, rowB], writes=[rowC])
        p.op("tensor", lambda e: e.matmul(bB1.ap[:, 0:512], sOne1.ap[:], rowB.ap[:], start=True, stop=True),
             reads=[sOne1, rowB], writes=[bB1])
        p.op("tensor", lambda e: e.matmul(bB2.ap[:, 0:512], sOne1.ap[:], rowC.ap[:], start=True, stop=True),
             reads=[sOne1, rowC], writes=[bB2])

    def resid_stats(bk, oc, xt, bias_col):
        xr_ap = xrb.ap[:, oc * 512:(oc + 1) * 512]
        if bias_col is None:
            p.op("vector", lambda e, bk=bk, xt=xt, xr_ap=xr_ap: e.tensor_tensor(
                xr_ap, bk.ap[:, 0:512], xt.ap[:], ALU.add), reads=[bk, xt], writes=[xrb])
        else:
            p.op("vector", lambda e, bk=bk, xt=xt, xr_ap=xr_ap, bc=bias_col: e.scalar_tensor_tensor(
                xr_ap, bk.ap[:, 0:512], bc, xt.ap[:], ALU.add, ALU.add),
                reads=[bk, xt, sBbc], writes=[xrb])
        sq = sqs[oc % 2]
        p.op("scalar", lambda e, xr_ap=xr_ap, sq=sq: e.activation(sq.ap[:], xr_ap, AF.Square),
             reads=[xrb], writes=[sq])
        p.accum_step(bS1, bS1.ap[0:1, 0:512], sOneC.ap[:], xr_ap, oc == 0, oc == 15, reads=[sOneC, xrb])
        p.accum_step(bS2, bS2.ap[0:1, 0:512], sOneC.ap[:], sq.ap[:], oc == 0, oc == 15, reads=[sOneC, sq])

    def normalize_out(tt, write_cb):
        for oc in range(16):
            xr_ap = xrb.ap[:, oc * 512:(oc + 1) * 512]
            sa = scA[oc % 2]
            p.op("vector", lambda e, xr_ap=xr_ap, sa=sa: e.tensor_tensor(
                sa.ap[:], xr_ap, bB1.ap[:, 0:512], ALU.mult), reads=[xrb, bB1], writes=[sa])
            sb_ = scB[oc % 2]
            p.op("vector", lambda e, sa=sa, sb_=sb_: e.tensor_tensor(
                sb_.ap[:], sa.ap[:], bB2.ap[:, 0:512], ALU.subtract), reads=[sa, bB2], writes=[sb_])
            write_cb(oc, sb_)

    for tt in range(4):
        ot = osl[tt % 2]
        p.dma("gpsimd", ot.ap[:].rearrange("p (c t) -> p c t", t=512),
              o_sl[:, tt * 512:(tt + 1) * 512].rearrange("(c p) t -> p c t", p=128),
              writes=[ot], sem=f"q_o{tt % 2}")
        for oc in range(16):
            wt = wstream(tt * 16 + oc, wo0, oc)
            bk = pool.take()
            p.mmgroup(bk, bk.ap[:, 0:512],
                      [(wt.ap[:, kc * 128:(kc + 1) * 128], ot.ap[:, kc * 512:(kc + 1) * 512])
                       for kc in range(16)], reads=[wt, ot])
            xt = xsl[oc % 4]
            p.dma("sync", xt.ap[:], x_sl[oc * 128:(oc + 1) * 128, tt * 512:(tt + 1) * 512],
                  writes=[xt], sem=f"q_xf{oc % 4}")
            resid_stats(bk, oc, xt, None)
        ln_stats()

        def wr1(oc, sb_, tt=tt):
            p.dma("sync", x1_scr[oc * 128:(oc + 1) * 128, tt * 512:(tt + 1) * 512], sb_.ap[:],
                  reads=[sb_], sem=f"q_x1w{oc % 2}")
        normalize_out(tt, wr1)

    for eng in ("gpsimd", "sync"):
        p.wait(eng, "q_x1w0", p.sem("q_x1w0")[1])
        p.wait(eng, "q_x1w1", p.sem("q_x1w1")[1])
    for tt in range(4):
        xt1 = x1sl[tt % 2]
        p.dma("gpsimd", xt1.ap[:].rearrange("p (c t) -> p c t", t=512),
              x1_scr[:, tt * 512:(tt + 1) * 512].rearrange("(c p) t -> p c t", p=128),
              writes=[xt1], sem=f"q_x1r{tt % 2}")
        for qb in range(4):
            qsl = slice(qb * 128, (qb + 1) * 128)
            bk = pool.take()
            p.mmgroup(bk, bk.ap[:, 0:8],
                      [(xt1.ap[:, kc * 512 + qb * 128:kc * 512 + (qb + 1) * 128],
                        sRwT.ap[:, kc * 8:(kc + 1) * 8]) for kc in range(16)], reads=[xt1, sRwT])
            p.op("vector", lambda e, bk=bk: e.tensor_tensor(lg8.ap[:], bk.ap[:, 0:8], sRcB.ap[:], ALU.add),
                 reads=[bk, sRcB], writes=[lg8])
            p.op("vector", lambda e: e.reduce_max(colA.ap[:], lg8.ap[:], axis=AX, negate=True),
                 reads=[lg8], writes=[colA])
            p.op("scalar", lambda e: e.activation(pr8.ap[:], lg8.ap[:], AF.Exp,
                                                  bias=colA.ap[:], accum_out=colB.ap[:]),
                 reads=[lg8, colA], writes=[pr8, colB])
            p.op("vector", lambda e: e.reciprocal(colB.ap[:], colB.ap[:]), reads=[colB], writes=[colB])
            p.op("vector", lambda e: e.tensor_scalar(prb.ap[:], pr8.ap[:], colB.ap[:], None, ALU.mult, ALU.bypass),
                 reads=[pr8, colB], writes=[prb])
            p.op("vector", lambda e: e.tensor_scalar(ind8.ap[:], prb.ap[:], colB.ap[:], -1e30, ALU.is_ge, ALU.mult),
                 reads=[prb, colB], writes=[ind8])
            p.op("vector", lambda e: e.tensor_tensor(ind8.ap[:], ind8.ap[:], prb.ap[:], ALU.add),
                 reads=[ind8, prb], writes=[ind8])
            p.op("vector", lambda e: e.reduce_max(colA.ap[:], ind8.ap[:], axis=AX),
                 reads=[ind8], writes=[colA])
            p.op("vector", lambda e: e.tensor_scalar(ind8.ap[:], prb.ap[:], colA.ap[:], None, ALU.is_ge, ALU.bypass),
                 reads=[prb, colA], writes=[ind8])
            p.op("vector", lambda e: e.tensor_tensor(colA.ap[:], colA.ap[:], colB.ap[:], ALU.add),
                 reads=[colA, colB], writes=[colA])
            p.op("vector", lambda e: e.reciprocal(colA.ap[:], colA.ap[:]), reads=[colA], writes=[colA])
            p.op("vector", lambda e: e.scalar_tensor_tensor(
                wg8.ap[:], prb.ap[:], colA.ap[:], ind8.ap[:], ALU.mult, ALU.mult),
                reads=[prb, colA, ind8], writes=[wg8])
            p.op("tensor", lambda e: e.transpose(bSm.ap[0:8, 0:128], wg8.ap[:], sIdf.ap[:]),
                 reads=[wg8, sIdf], writes=[bSm])
            p.op("vector", lambda e, qsl=qsl: e.tensor_copy(wT8.ap[:, qsl], bSm.ap[0:8, 0:128]),
                 reads=[bSm], writes=[wT8])
            p.op("tensor", lambda e, qsl=qsl: e.matmul(bSm2.ap[:, 0:128], sInds.ap[:], wT8.ap[:, qsl],
                                                       start=True, stop=True),
                 reads=[sInds, wT8], writes=[bSm2])
            p.op("vector", lambda e: e.tensor_copy(gB.ap[:], bSm2.ap[:, 0:128]), reads=[bSm2], writes=[gB])
            bk2 = pool.take()
            p.mmgroup(bk2, bk2.ap[:, 0:128],
                      [(sLA.ap[:, kc * 128:(kc + 1) * 128],
                        xt1.ap[:, kc * 512 + qb * 128:kc * 512 + (qb + 1) * 128]) for kc in range(16)],
                      reads=[sLA, xt1])
            p.op("vector", lambda e, bk2=bk2, qsl=qsl: e.tensor_tensor(
                midg.ap[:, qsl], bk2.ap[:, 0:128], gB.ap[:], ALU.mult), reads=[bk2, gB], writes=[midg])
        for oc in range(16):
            wt = wstream(tt * 16 + oc, basewT, oc)
            bk = pool.take()
            pairs = [(wt.ap[:, kc * 128:(kc + 1) * 128], xt1.ap[:, kc * 512:(kc + 1) * 512])
                     for kc in range(16)]
            pairs += [(sLB.ap[:, oc * 128:(oc + 1) * 128], midg.ap[:, qb * 128:(qb + 1) * 128])
                      for qb in range(4)]
            p._pre("tensor", [wt, xt1, sLB, midg], [bk])
            cse = p.sem("e_tensor")
            for kc in range(16):
                p.raw("tensor", lambda e, w=wt.ap[:, kc * 128:(kc + 1) * 128],
                      r=xt1.ap[:, kc * 512:(kc + 1) * 512], o=bk.ap[:, 0:512], s=(kc == 0):
                      e.matmul(o, w, r, start=s, stop=False))
            for qb in range(4):
                last = qb == 3
                if last:
                    cse[1] += 1
                    p.raw("tensor", lambda e, w=sLB.ap[:, oc * 128:(oc + 1) * 128],
                          r=midg.ap[:, qb * 128:(qb + 1) * 128],
                          o=bk.ap[:, qb * 128:(qb + 1) * 128], sem=cse[0]:
                          e.matmul(o, w, r, start=False, stop=True).then_inc(sem, 1))
                else:
                    p.raw("tensor", lambda e, w=sLB.ap[:, oc * 128:(oc + 1) * 128],
                          r=midg.ap[:, qb * 128:(qb + 1) * 128],
                          o=bk.ap[:, qb * 128:(qb + 1) * 128]:
                          e.matmul(o, w, r, start=False, stop=False))
            tok = ("e_tensor", cse[1])
            for b_ in (wt, xt1, sLB, midg):
                b_.ctoks[tok[0]] = max(b_.ctoks.get(tok[0], 0), tok[1])
            bk.ptok = tok
            bk.ctoks = {}
            xt = xsl[oc % 4]
            p.dma("sync", xt.ap[:], x1_scr[oc * 128:(oc + 1) * 128, tt * 512:(tt + 1) * 512],
                  writes=[xt], sem=f"q_xf{oc % 4}")
            resid_stats(bk, oc, xt, sBbc.ap[:, oc:oc + 1])
        ln_stats()

        def wr2(oc, sb_, tt=tt):
            p.dma("sync", x2_out[oc * 128:(oc + 1) * 128, tt * 512:(tt + 1) * 512], sb_.ap[:],
                  reads=[sb_], sem=f"q_x2w{oc % 2}")
            p.op("vector", lambda e, oc=oc, tt=tt, sb_=sb_: e.tensor_copy(
                x2b.ap[:, oc * 2048 + tt * 512:oc * 2048 + (tt + 1) * 512], sb_.ap[:]),
                reads=[sb_], writes=[x2b])
        normalize_out(tt, wr2)

    _B3V_MARK = None
    for wdram, bias_sb, out_dram, scale in ((wq1, sBqs, q_out, SC), (wk1, sBkc, k_out, 1.0)):
        for oc in range(16):
            wt = wstream(oc, wdram, oc, semn="q_w")
            for tt in range(4):
                bk = pool.take()
                p.mmgroup(bk, bk.ap[:, 0:512],
                          [(wt.ap[:, kc * 128:(kc + 1) * 128],
                            x2b.ap[:, kc * 2048 + tt * 512:kc * 2048 + (tt + 1) * 512])
                           for kc in range(16)], reads=[wt, x2b])
                eb = ebf[tt % 2]
                p.op("scalar", lambda e, bk=bk, eb=eb, bias_sb=bias_sb, oc=oc, sc=scale: e.activation(
                    eb.ap[:], bk.ap[:, 0:512], AF.Identity, bias=bias_sb.ap[:, oc:oc + 1], scale=sc),
                    reads=[bk, bias_sb], writes=[eb])
                p.dma("scalar", out_dram[oc * 128:(oc + 1) * 128, tt * 512:(tt + 1) * 512],
                      eb.ap[:], reads=[eb], sem=f"q_qk{tt % 2}")
    if mid_cb is not None:
        mid_cb()
    for vg in range(4):
        vt = vsl[vg % 2]
        p.dma("gpsimd", vt.ap[:].rearrange("p (c o) -> p c o", o=512),
              wv1[:, vg * 512:(vg + 1) * 512].rearrange("(c p) o -> p c o", p=128),
              writes=[vt], sem=f"q_wv{vg % 2}")
        p.op("tensor", lambda e, vg=vg: e.matmul(bSm.ap[:, 0:512], sOne1.ap[:],
                                                 sBvr.ap[:, vg * 512:(vg + 1) * 512], start=True, stop=True),
             reads=[sOne1, sBvr], writes=[bSm])
        p.op("vector", lambda e: e.tensor_copy(sBvB.ap[:], bSm.ap[:, 0:512]), reads=[bSm], writes=[sBvB])
        for tb in range(16):
            bk = pool.take()
            p.mmgroup(bk, bk.ap[:, 0:512],
                      [(x2b.ap[:, kc * 2048 + tb * 128:kc * 2048 + (tb + 1) * 128],
                        vt.ap[:, kc * 512:(kc + 1) * 512]) for kc in range(16)], reads=[x2b, vt])
            eb = ebf[tb % 2]
            p.op("vector", lambda e, bk=bk, eb=eb: e.tensor_tensor(
                eb.ap[:], bk.ap[:, 0:512], sBvB.ap[:], ALU.add), reads=[bk, sBvB], writes=[eb])
            p.dma("scalar", v_out[tb * 128:(tb + 1) * 128, vg * 512:(vg + 1) * 512],
                  eb.ap[:], reads=[eb], sem=f"q_v{tb % 2}")

    for sn in ("q_x2w0", "q_x2w1", "q_qk0", "q_qk1", "q_v0", "q_v1"):
        p.wait("gpsimd", sn, p.sem(sn)[1])
    p.raw("gpsimd", lambda e: e.engine_nop())



def stage_b_inmaps(inputs, o_fm):
    x = inputs["L0_attn_hidden_states"]
    xT = np.ascontiguousarray(x.transpose(0, 2, 1))
    ind = np.zeros((8, 128), np.float32)
    for e in range(8):
        ind[e, e * 16:(e + 1) * 16] = 2.0
    consts = dict(
        inds=ind, ones1=np.ones((1, 128), np.float32), onesc=np.ones((128, 1), np.float32),
        idf=np.eye(128, dtype=np.float32),
    )
    shared = dict(
        wo0=inputs["L0_attn_Wo"],
        basewT=np.ascontiguousarray(inputs["L0_ffn_base_w"].T),
        basebc=_bcols(inputs["L0_ffn_base_b"]),
        rwT=np.ascontiguousarray(inputs["L0_ffn_router_w"].T),
        rbrow=inputs["L0_ffn_router_b"].reshape(1, 8),
        lA=np.ascontiguousarray(inputs["L0_ffn_lora_A"].transpose(1, 0, 2).reshape(2048, 128)),
        lB=np.ascontiguousarray(inputs["L0_ffn_lora_B"].reshape(128, 2048)),
        wq1=inputs["L1_attn_wq"], wk1=inputs["L1_attn_wk"], wv1=inputs["L1_attn_wv"],
        bqc=_bcols(inputs["L1_attn_bq"]), bkc=_bcols(inputs["L1_attn_bk"]),
        bvr=inputs["L1_attn_bv"].reshape(1, 2048),
        **consts,
    )
    ims = []
    for c in range(8):
        b, g = c // 4, c % 4
        ims.append(dict(
            o_sl=np.ascontiguousarray(o_fm[b][:, g * 2048:(g + 1) * 2048]),
            x_sl=np.ascontiguousarray(xT[b][:, g * 2048:(g + 1) * 2048]),
            **shared,
        ))
    return ims



def emit_stage_c(nc, p, stack, E):
    D = 2048

    q_sl = E["q_sl"]
    kf_parts = E["kf_parts"]
    vt_parts = E["vt_parts"]
    x2_sl = E["x2_sl"]
    wo1 = E["wo1"]
    boc = E["boc"]
    ones1 = E["ones1"]
    onesc = E["onesc"]
    onescb = E["onescb"]
    idbf = E["idbf"]

    x3_out = E["x3_out"]
    o1_scr = E["o1_scr"]

    sOne1 = _sb(nc, stack, "sOne1", (1, 128), F32)
    sOneC = _sb(nc, stack, "sOneC", (128, 1), F32)
    sOneCb = _sb(nc, stack, "sOneCb", (128, 1), BF16)
    sIdb = _sb(nc, stack, "sIdb", (128, 128), BF16)
    sEps1 = _sb(nc, stack, "sEps1", (1, 1), F32)
    sBoc = _sb(nc, stack, "sBoc", (128, 16), F32)
    khs = [_sb(nc, stack, f"khs{i}", (128, 8192), BF16) for i in range(2)]
    vhs = [_sb(nc, stack, f"vhs{i}", (128, 64 * 129), BF16) for i in range(2)]
    qhs = [_sb(nc, stack, f"qhs{i}", (128, 2048), BF16) for i in range(2)]
    sqq = _sb(nc, stack, "sqq", (128, 512), BF16)
    qn = _sb(nc, stack, "qn", (1, 2048), F32)
    kmx = _sb(nc, stack, "kmx", (1, 1), F32)
    kmt = _sb(nc, stack, "kmt", (1, 1), F32)
    brow = _sb(nc, stack, "brow", (1, 512), F32)
    nbg = _sb(nc, stack, "nbg", (1, 1), F32)
    nbc4 = _sb(nc, stack, "nbc4", (128, 4), F32)
    pts = [_sb(nc, stack, f"pts{i}", (128, 1024), BF16) for i in range(2)]
    rn = _sb(nc, stack, "rn", (128, 1), F32)
    obf = _sb(nc, stack, "obf", (128, 128), BF16)
    otr = [_sb(nc, stack, f"otr{i}", (128, 128), BF16) for i in range(2)]
    wsl = [_sb(nc, stack, f"wsl{i}", (128, 16 * 128), BF16) for i in range(2)]
    osl = [_sb(nc, stack, f"osl{i}", (128, 16 * 512), BF16) for i in range(2)]
    xsl = [_sb(nc, stack, f"xsl{i}", (128, 512), F32) for i in range(4)]
    xrb = _sb(nc, stack, "xrb", (128, 16 * 512), F32)
    sqs = [_sb(nc, stack, f"sqs{i}", (128, 512), F32) for i in range(2)]
    scA = [_sb(nc, stack, f"scA{i}", (128, 512), F32) for i in range(2)]
    scB = [_sb(nc, stack, f"scB{i}", (128, 512), F32) for i in range(2)]
    rowA = _sb(nc, stack, "rowA", (1, 512), F32)
    rowB = _sb(nc, stack, "rowB", (1, 512), F32)
    rowC = _sb(nc, stack, "rowC", (1, 512), F32)

    bSp = [_bank(nc, stack, f"bkS{i}", F32, cols=1024) for i in range(2)]
    oB = [_bank(nc, stack, f"bkO{i}") for i in range(4)]
    bM = bSp[0]
    bM2 = bSp[1]

    for buf, src_ in ((sOne1, ones1), (sOneC, onesc), (sOneCb, onescb), (sIdb, idbf), (sBoc, boc)):
        p.dma_serial("sync", buf.ap[:], src_, writes=[buf], sem="q_cst")
    p.op("vector", lambda e: e.memset(sEps1.ap[:], 1e-5), writes=[sEps1])

    for h in range(16):
        kh = khs[h % 2]
        vh = vhs[h % 2]
        qh = qhs[h % 2]
        for r in range(4):
            p.dma("gpsimd", kh.ap[:, r * 2048:(r + 1) * 2048],
                  kf_parts[r][h * 128:(h + 1) * 128, :], writes=[kh], sem=f"q_kh{h % 2}")
        p.op("vector", lambda e, vh=vh: e.memset(vh.ap[:], 1.0), writes=[vh])
        for r in range(4):
            p.dma("gpsimd",
                  vh.ap[:, r * 16 * 129:(r + 1) * 16 * 129].rearrange("p (k w) -> p k w", w=129)[:, :, 0:128],
                  vt_parts[r][:, h * 128:(h + 1) * 128].rearrange("(k p) v -> p k v", p=128),
                  writes=[vh], sem=f"q_vh{h % 2}")
        p.dma("sync", qh.ap[:], q_sl[h * 128:(h + 1) * 128, :], writes=[qh], sem=f"q_qh{h % 2}")
        for i in range(4):
            p.op("scalar", lambda e, qh=qh, i=i: e.activation(
                sqq.ap[:], qh.ap[:, i * 512:(i + 1) * 512], AF.Square), reads=[qh], writes=[sqq])
            p.op("tensor", lambda e, i=i: e.matmul(bM.ap[0:1, 0:512], sOneCb.ap[:], sqq.ap[:],
                                                   start=True, stop=True),
                 reads=[sOneCb, sqq], writes=[bM])
            p.op("scalar", lambda e, i=i: e.activation(qn.ap[0:1, i * 512:(i + 1) * 512],
                                                       bM.ap[0:1, 0:512], AF.Ln),
                 reads=[bM], writes=[qn])
        for i in range(16):
            p.op("scalar", lambda e, kh=kh, i=i: e.activation(
                sqq.ap[:], kh.ap[:, i * 512:(i + 1) * 512], AF.Square), reads=[kh], writes=[sqq])
            p.op("tensor", lambda e: e.matmul(bM.ap[0:1, 0:512], sOneCb.ap[:], sqq.ap[:],
                                              start=True, stop=True),
                 reads=[sOneCb, sqq], writes=[bM])
            p.op("vector", lambda e: e.reduce_max(kmt.ap[:], bM.ap[0:1, 0:512], axis=AX),
                 reads=[bM], writes=[kmt])
            if i == 0:
                p.op("vector", lambda e: e.tensor_copy(kmx.ap[:], kmt.ap[:]), reads=[kmt], writes=[kmx])
            else:
                p.op("vector", lambda e: e.tensor_tensor(kmx.ap[:], kmx.ap[:], kmt.ap[:], ALU.max),
                     reads=[kmx, kmt], writes=[kmx])
        p.op("scalar", lambda e: e.activation(kmx.ap[:], kmx.ap[:], AF.Ln), reads=[kmx], writes=[kmx])

        if "dmp_qn" in E:
            p.dma("scalar", E["dmp_qn"][h:h + 1, :], qn.ap[:], reads=[qn], sem="q_dbg2")
            p.dma("scalar", E["dmp_km"][h:h + 1, :], kmx.ap[:], reads=[kmx], sem="q_dbg2")
        for qg in range(4):
            qsl = slice(qg * 512, (qg + 1) * 512)
            p.op("vector", lambda e, qsl=qsl: e.tensor_scalar(
                brow.ap[:], qn.ap[0:1, qsl], kmx.ap[:], None, ALU.add, ALU.bypass),
                reads=[qn, kmx], writes=[brow])
            p.op("vector", lambda e: e.reduce_max(nbg.ap[:], brow.ap[:], axis=AX),
                 reads=[brow], writes=[nbg])
            p.op("scalar", lambda e: e.activation(nbg.ap[:], nbg.ap[:], AF.Exp, scale=0.5),
                 reads=[nbg], writes=[nbg])
            p.op("vector", lambda e, qg=qg: e.tensor_scalar(
                nbg.ap[:], nbg.ap[:], -1.0, None, ALU.mult, ALU.bypass), reads=[nbg], writes=[nbg])
            p.op("tensor", lambda e, qg=qg: e.matmul(bM.ap[:, qg:qg + 1], sOne1.ap[:], nbg.ap[:],
                                                     start=True, stop=True),
                 reads=[sOne1, nbg], writes=[bM])
        p.op("vector", lambda e: e.tensor_copy(nbc4.ap[:], bM.ap[:, 0:4]), reads=[bM], writes=[nbc4])
        for qg in range(4):
            qsl = slice(qg * 512, (qg + 1) * 512)
            prev = None
            for ktp in range(32):
                sb_ = bSp[ktp % 2]
                p._pre("tensor", [kh, qh], [sb_])
                cse = p.sem("e_tensor")
                p.raw("tensor", lambda e, kh=kh, qh=qh, ktp=ktp, qsl=qsl, sb_=sb_: e.matmul(
                    sb_.ap[:, 0:512], kh.ap[:, (2 * ktp) * 128:(2 * ktp + 1) * 128], qh.ap[:, qsl],
                    start=True, stop=True))
                cse[1] += 1
                p.raw("tensor", lambda e, kh=kh, qh=qh, ktp=ktp, qsl=qsl, sb_=sb_, sem=cse[0]: e.matmul(
                    sb_.ap[:, 512:1024], kh.ap[:, (2 * ktp + 1) * 128:(2 * ktp + 2) * 128], qh.ap[:, qsl],
                    start=True, stop=True).then_inc(sem, 1))
                tok = ("e_tensor", cse[1])
                kh.ctoks[tok[0]] = max(kh.ctoks.get(tok[0], 0), tok[1])
                qh.ctoks[tok[0]] = max(qh.ctoks.get(tok[0], 0), tok[1])
                sb_.ptok = tok
                sb_.ctoks = {}
                pt = pts[ktp % 2]
                p.op("scalar", lambda e, sb_=sb_, pt=pt, qg=qg: e.activation(
                    pt.ap[:], sb_.ap[:, 0:1024], AF.Exp, bias=nbc4.ap[:, qg:qg + 1]),
                    reads=[sb_, nbc4], writes=[pt])
                if prev is not None:
                    pk, ppt = prev
                    for i in range(4):
                        for j in range(2):
                            kt = 2 * pk + j
                            p.accum_step(oB[i], oB[i].ap[:, 0:129],
                                         ppt.ap[:, j * 512 + i * 128:j * 512 + (i + 1) * 128],
                                         vh.ap[:, kt * 129:(kt + 1) * 129],
                                         kt == 0, False, reads=[ppt, vh])
                prev = (ktp, pt)
            pk, ppt = prev
            for i in range(4):
                for j in range(2):
                    kt = 2 * pk + j
                    p.accum_step(oB[i], oB[i].ap[:, 0:129],
                                 ppt.ap[:, j * 512 + i * 128:j * 512 + (i + 1) * 128],
                                 vh.ap[:, kt * 129:(kt + 1) * 129],
                                 False, kt == 63, reads=[ppt, vh])
            for i in range(4):
                p.op("vector", lambda e, i=i: e.reciprocal(rn.ap[:], oB[i].ap[:, 128:129]),
                     reads=[oB[i]], writes=[rn])
                p.op("vector", lambda e, i=i: e.tensor_scalar(
                    obf.ap[:], oB[i].ap[:, 0:128], rn.ap[:], None, ALU.mult, ALU.bypass),
                    reads=[oB[i], rn], writes=[obf])
                p.op("tensor", lambda e: e.transpose(bM2.ap[:, 0:64].bitcast(BF16), obf.ap[:], sIdb.ap[:]),
                     reads=[obf, sIdb], writes=[bM2])
                ot = otr[i % 2]
                p.op("vector", lambda e, ot=ot: e.tensor_copy(ot.ap[:], bM2.ap[:, 0:64].bitcast(BF16)),
                     reads=[bM2], writes=[ot])
                p.dma("scalar", o1_scr[h * 128:(h + 1) * 128, qg * 512 + i * 128:qg * 512 + (i + 1) * 128],
                      ot.ap[:], reads=[ot], sem=f"q_o1{i % 2}")

    for eng in ("gpsimd", "sync"):
        p.wait(eng, "q_o10", p.sem("q_o10")[1])
        p.wait(eng, "q_o11", p.sem("q_o11")[1])
    bS1, bS2 = oB[0], oB[1]
    bB1, bB2 = oB[2], oB[3]
    pool = Pool([bSp[0], bSp[1]])

    def ln_stats():
        p.op("vector", lambda e: e.tensor_scalar(rowA.ap[:], bS1.ap[0:1, 0:512], 1.0 / D, None, ALU.mult, ALU.bypass),
             reads=[bS1], writes=[rowA])
        p.op("vector", lambda e: e.tensor_scalar(rowB.ap[:], bS2.ap[0:1, 0:512], 1.0 / D, None, ALU.mult, ALU.bypass),
             reads=[bS2], writes=[rowB])
        p.op("vector", lambda e: e.tensor_tensor(rowC.ap[:], rowA.ap[:], rowA.ap[:], ALU.mult),
             reads=[rowA], writes=[rowC])
        p.op("vector", lambda e: e.tensor_tensor(rowB.ap[:], rowB.ap[:], rowC.ap[:], ALU.subtract),
             reads=[rowB, rowC], writes=[rowB])
        p.op("scalar", lambda e: e.activation(rowB.ap[:], rowB.ap[:], AF.Ln, bias=sEps1.ap[:]),
             reads=[rowB, sEps1], writes=[rowB])
        p.op("scalar", lambda e: e.activation(rowB.ap[:], rowB.ap[:], AF.Exp, scale=-0.5),
             reads=[rowB], writes=[rowB])
        p.op("vector", lambda e: e.tensor_tensor(rowC.ap[:], rowA.ap[:], rowB.ap[:], ALU.mult),
             reads=[rowA, rowB], writes=[rowC])
        p.op("tensor", lambda e: e.matmul(bB1.ap[:, 0:512], sOne1.ap[:], rowB.ap[:], start=True, stop=True),
             reads=[sOne1, rowB], writes=[bB1])
        p.op("tensor", lambda e: e.matmul(bB2.ap[:, 0:512], sOne1.ap[:], rowC.ap[:], start=True, stop=True),
             reads=[sOne1, rowC], writes=[bB2])

    for tt in range(4):
        ot = osl[tt % 2]
        p.dma("gpsimd", ot.ap[:].rearrange("p (c t) -> p c t", t=512),
              o1_scr[:, tt * 512:(tt + 1) * 512].rearrange("(c p) t -> p c t", p=128),
              writes=[ot], sem=f"q_or{tt % 2}")
        for oc in range(16):
            sl = wsl[oc % 2]
            p.dma("gpsimd", sl.ap[:].rearrange("p (c o) -> p c o", o=128),
                  wo1[:, oc * 128:(oc + 1) * 128].rearrange("(c p) o -> p c o", p=128),
                  writes=[sl], sem=f"q_w{oc % 2}")
            bk = pool.take()
            p.mmgroup(bk, bk.ap[:, 0:512],
                      [(sl.ap[:, kc * 128:(kc + 1) * 128], ot.ap[:, kc * 512:(kc + 1) * 512])
                       for kc in range(16)], reads=[sl, ot])
            xt = xsl[oc % 4]
            p.dma("sync", xt.ap[:], x2_sl[oc * 128:(oc + 1) * 128, tt * 512:(tt + 1) * 512],
                  writes=[xt], sem=f"q_xf{oc % 4}")
            xr_ap = xrb.ap[:, oc * 512:(oc + 1) * 512]
            p.op("vector", lambda e, bk=bk, xt=xt, xr_ap=xr_ap, oc=oc: e.scalar_tensor_tensor(
                xr_ap, bk.ap[:, 0:512], sBoc.ap[:, oc:oc + 1], xt.ap[:], ALU.add, ALU.add),
                reads=[bk, xt, sBoc], writes=[xrb])
            sq = sqs[oc % 2]
            p.op("scalar", lambda e, xr_ap=xr_ap, sq=sq: e.activation(sq.ap[:], xr_ap, AF.Square),
                 reads=[xrb], writes=[sq])
            p.accum_step(bS1, bS1.ap[0:1, 0:512], sOneC.ap[:], xr_ap, oc == 0, oc == 15, reads=[sOneC, xrb])
            p.accum_step(bS2, bS2.ap[0:1, 0:512], sOneC.ap[:], sq.ap[:], oc == 0, oc == 15, reads=[sOneC, sq])
        ln_stats()
        for oc in range(16):
            xr_ap = xrb.ap[:, oc * 512:(oc + 1) * 512]
            sa = scA[oc % 2]
            p.op("vector", lambda e, xr_ap=xr_ap, sa=sa: e.tensor_tensor(
                sa.ap[:], xr_ap, bB1.ap[:, 0:512], ALU.mult), reads=[xrb, bB1], writes=[sa])
            sb_ = scB[oc % 2]
            p.op("vector", lambda e, sa=sa, sb_=sb_: e.tensor_tensor(
                sb_.ap[:], sa.ap[:], bB2.ap[:, 0:512], ALU.subtract), reads=[sa, bB2], writes=[sb_])
            p.dma("sync", x3_out[oc * 128:(oc + 1) * 128, tt * 512:(tt + 1) * 512], sb_.ap[:],
                  reads=[sb_], sem=f"q_x3{oc % 2}")

    p.wait("gpsimd", "q_x30", p.sem("q_x30")[1])
    p.wait("gpsimd", "q_x31", p.sem("q_x31")[1])
    p.raw("gpsimd", lambda e: e.engine_nop())



def stage_c_inmaps(inputs, q_fm, k_fm, v_tm, x2_fm):
    consts = dict(
        ones1=np.ones((1, 128), np.float32), onesc=np.ones((128, 1), np.float32),
        onescb=np.ones((128, 1), dtype=BF), idbf=np.eye(128, dtype=BF),
    )
    shared = dict(wo1=inputs["L1_attn_wo"], boc=_bcols(inputs["L1_attn_bo"]), **consts)
    ims = []
    for c in range(8):
        b = c // 4
        ims.append(dict(
            q_sl=q_fm[c], x2_sl=x2_fm[c],
            kf=np.ascontiguousarray(k_fm[b]), vt=np.ascontiguousarray(v_tm[b]),
            **shared,
        ))
    return ims



HID = [1024, 2048, 3072, 4096, 1024, 2048, 3072, 4096]


def emit_stage_d(nc, p, stack, E):
    D = 2048
    L = 1024
    ST = 512
    NST = 2048 // ST

    x3_sl = E["x3_sl"]
    downw = E["downw"]
    downbc = E["downbc"]
    upw = E["upw"]
    corew = E["corew"]
    bsumc = E["bsumc"]
    upbc = E["upbc"]
    corebc = E["corebc"]
    rw1 = E["rw1"]
    rb1 = E["rb1"]
    costr = E["costr"]
    sw1 = E["sw1"]
    sb1c = E["sb1c"]
    sw2 = E["sw2"]
    sb2c = E["sb2c"]
    ew1 = [nc.dram_tensor(f"e{e}w1", [L, HID[e]], F32, kind="ExternalInput").ap() for e in range(8)]
    eb1 = [nc.dram_tensor(f"e{e}b1", [128, HID[e] // 128], F32, kind="ExternalInput").ap() for e in range(8)]
    ew2 = [nc.dram_tensor(f"e{e}w2", [HID[e], L], F32, kind="ExternalInput").ap() for e in range(8)]
    eb2 = [nc.dram_tensor(f"e{e}b2", [128, 8], F32, kind="ExternalInput").ap() for e in range(8)]
    sels = E["sels"]
    ones1 = E["ones1"]
    onesc = E["onesc"]
    idf = E["idf"]

    out_sl = E["out_sl"]

    sOne1 = _sb(nc, stack, "sOne1", (1, 128), F32)
    sOneC = _sb(nc, stack, "sOneC", (128, 1), F32)
    sIdf = _sb(nc, stack, "sIdf", (128, 128), F32)
    sEps1 = _sb(nc, stack, "sEps1", (1, 1), F32)
    sSel = _sb(nc, stack, "sSel", (8, 8 * 128), F32)
    sRb = _sb(nc, stack, "sRb", (1, 8), F32)
    sCost = _sb(nc, stack, "sCost", (1, 8), F32)
    sRcB = _sb(nc, stack, "sRcB", (128, 8), F32)
    sRwT = _sb(nc, stack, "sRwT", (128, 16 * 8), BF16)
    sDbc = _sb(nc, stack, "sDbc", (128, 8), F32)
    sSb1 = _sb(nc, stack, "sSb1", (128, 8), F32)
    sSb2 = _sb(nc, stack, "sSb2", (128, 8), F32)
    sBsum = _sb(nc, stack, "sBsum", (128, 16), F32)
    sEb1 = [_sb(nc, stack, f"sEb1_{e}", (128, HID[e] // 128), F32) for e in range(8)]
    sG1b = [_sb(nc, stack, f"sG1b_{e}", (128, HID[e] // 128), F32) for e in range(8)]
    sG1bB = [_sb(nc, stack, f"sG1bB_{e}", (128, HID[e] // 128), BF16) for e in range(8)]
    sEb2 = [_sb(nc, stack, f"sEb2_{e}", (128, 8), F32) for e in range(8)]
    sCcol = _sb(nc, stack, "sCcol", (128, 8), F32)
    sCtmp = _sb(nc, stack, "sCtmp", (128, 8), F32)
    xt3 = _sb(nc, stack, "xt3", (128, 16 * ST), F32)
    xb = _sb(nc, stack, "xb", (128, 16 * ST), BF16)
    g0b = _sb(nc, stack, "g0b", (128, 16 * ST), BF16)
    g1b = _sb(nc, stack, "g1b", (128, 8 * ST), BF16)
    w1s = [_sb(nc, stack, f"w1s{i}", (128, 8 * 1024), BF16) for i in range(2)]
    w2s = [_sb(nc, stack, "w2s0", (128, 8 * 1024), BF16)]
    gss = _sb(nc, stack, "gss", (128, 8 * ST), BF16)
    scG = [_sb(nc, stack, f"scG{i}", (128, ST), F32) for i in range(2)]
    sWBe = _sb(nc, stack, "sWBe", (128, ST), F32)
    macc = [_sb(nc, stack, "macc0", (128, 8 * ST), F32)]
    moeb = _sb(nc, stack, "moeb", (128, 8 * ST), BF16)
    lg8 = _sb(nc, stack, "lg8", (128, 8), F32)
    pr8 = _sb(nc, stack, "pr8", (128, 8), F32)
    prb = _sb(nc, stack, "prb", (128, 8), F32)
    ind8 = _sb(nc, stack, "ind8", (128, 8), F32)
    em8 = _sb(nc, stack, "em8", (128, 8), F32)
    wg8 = _sb(nc, stack, "wg8", (128, 8), F32)
    colA = _sb(nc, stack, "colA", (128, 1), F32)
    colB = _sb(nc, stack, "colB", (128, 1), F32)
    colC = _sb(nc, stack, "colC", (128, 1), F32)
    wT8 = _sb(nc, stack, "wT8", (8, ST), F32)
    xrb = _sb(nc, stack, "xrb", (128, 16 * ST), F32)
    gssF = gss.ap[:].bitcast(F32)
    rowA = _sb(nc, stack, "rowA", (1, ST), F32)
    rowB = _sb(nc, stack, "rowB", (1, ST), F32)
    rowC = _sb(nc, stack, "rowC", (1, ST), F32)

    pool = Pool([_bank(nc, stack, "bkA"), _bank(nc, stack, "bkB")])
    epool = Pool([_bank(nc, stack, "bkE0"), _bank(nc, stack, "bkE1")])
    bS1 = _bank(nc, stack, "bS1")
    bS2 = _bank(nc, stack, "bS2")
    bB1 = _bank(nc, stack, "bB1")
    bB2 = _bank(nc, stack, "bB2")

    for buf, src_ in ((sOne1, ones1), (sOneC, onesc), (sIdf, idf), (sSel, sels),
                      (sRb, rb1), (sCost, costr), (sDbc, downbc), (sSb1, sb1c),
                      (sSb2, sb2c), (sBsum, upbc)):
        p.dma_serial("sync", buf.ap[:], src_, writes=[buf], sem="q_cst")
    sCoreb = _sb(nc, stack, "sCoreb", (128, 16), F32)
    p.dma_serial("sync", sCoreb.ap[:], corebc, writes=[sCoreb], sem="q_cst")
    for e in range(8):
        p.dma_serial("sync", sEb1[e].ap[:], eb1[e], writes=[sEb1[e]], sem="q_cst")
        p.dma_serial("sync", sEb2[e].ap[:], eb2[e], writes=[sEb2[e]], sem="q_cst")
    p.dma_serial("gpsimd", sRwT.ap[:].rearrange("p (c o) -> p c o", o=8),
                 rw1.rearrange("(c p) o -> p c o", p=128), writes=[sRwT], sem="q_wld")
    p.op("vector", lambda e: e.memset(sEps1.ap[:], 1e-5), writes=[sEps1])
    p.op("vector", lambda e: e.tensor_tensor(sBsum.ap[:], sBsum.ap[:], sCoreb.ap[:], ALU.add),
         reads=[sBsum, sCoreb], writes=[sBsum])
    p.op("vector", lambda e: e.tensor_tensor(sRb.ap[:], sRb.ap[:], sCost.ap[:], ALU.add),
         reads=[sRb, sCost], writes=[sRb])
    p.op("tensor", lambda e: e.matmul(bB1.ap[:, 0:8], sOne1.ap[:], sRb.ap[:], start=True, stop=True),
         reads=[sOne1, sRb], writes=[bB1])
    p.op("vector", lambda e: e.tensor_copy(sRcB.ap[:], bB1.ap[:, 0:8]), reads=[bB1], writes=[sRcB])
    for e in range(8):
        p.op("scalar", lambda e_, e=e: e_.activation(sG1b[e].ap[:], sEb1[e].ap[:], AF.Gelu),
             reads=[sEb1[e]], writes=[sG1b[e]])
        p.op("vector", lambda e_, e=e: e_.tensor_copy(sG1bB[e].ap[:], sG1b[e].ap[:]),
             reads=[sG1b[e]], writes=[sG1bB[e]])
    p.op("vector", lambda e: e.tensor_scalar(sCcol.ap[:], sSb2.ap[:], 0.1, None, ALU.mult, ALU.bypass),
         reads=[sSb2], writes=[sCcol])
    for e in range(8):
        p.op("vector", lambda e_, e=e: e_.tensor_tensor(sCcol.ap[:], sCcol.ap[:], sEb2[e].ap[:], ALU.add),
             reads=[sCcol, sEb2[e]], writes=[sCcol])

    def router(qb, st):
        qsl = slice(qb * 128, (qb + 1) * 128)
        bk = pool.take()
        p.mmgroup(bk, bk.ap[:, 0:8],
                  [(xb.ap[:, kc * ST + qb * 128:kc * ST + (qb + 1) * 128],
                    sRwT.ap[:, kc * 8:(kc + 1) * 8]) for kc in range(16)], reads=[xb, sRwT])
        p.op("vector", lambda e, bk=bk: e.tensor_tensor(lg8.ap[:], bk.ap[:, 0:8], sRcB.ap[:], ALU.add),
             reads=[bk, sRcB], writes=[lg8])
        p.op("vector", lambda e: e.reduce_max(colA.ap[:], lg8.ap[:], axis=AX, negate=True),
             reads=[lg8], writes=[colA])
        p.op("scalar", lambda e: e.activation(pr8.ap[:], lg8.ap[:], AF.Exp,
                                              bias=colA.ap[:], accum_out=colB.ap[:]),
             reads=[lg8, colA], writes=[pr8, colB])
        p.op("vector", lambda e: e.reciprocal(colB.ap[:], colB.ap[:]), reads=[colB], writes=[colB])
        p.op("vector", lambda e: e.tensor_scalar(prb.ap[:], pr8.ap[:], colB.ap[:], None, ALU.mult, ALU.bypass),
             reads=[pr8, colB], writes=[prb])
        p.op("vector", lambda e: e.tensor_scalar(ind8.ap[:], prb.ap[:], colB.ap[:], -1e30, ALU.is_ge, ALU.mult),
             reads=[prb, colB], writes=[ind8])
        p.op("vector", lambda e: e.tensor_tensor(ind8.ap[:], ind8.ap[:], prb.ap[:], ALU.add),
             reads=[ind8, prb], writes=[ind8])
        p.op("vector", lambda e: e.reduce_max(colA.ap[:], ind8.ap[:], axis=AX),
             reads=[ind8], writes=[colA])
        p.op("vector", lambda e: e.tensor_scalar(ind8.ap[:], prb.ap[:], colA.ap[:], None, ALU.is_ge, ALU.bypass),
             reads=[prb, colA], writes=[ind8])
        p.op("scalar", lambda e: e.activation(em8.ap[:], prb.ap[:], AF.Exp), reads=[prb], writes=[em8])
        p.op("scalar", lambda e: e.activation(colC.ap[:], colA.ap[:], AF.Exp), reads=[colA], writes=[colC])
        p.op("scalar", lambda e: e.activation(colA.ap[:], colB.ap[:], AF.Exp), reads=[colB], writes=[colA])
        p.op("vector", lambda e: e.tensor_tensor(colA.ap[:], colA.ap[:], colC.ap[:], ALU.add),
             reads=[colA, colC], writes=[colA])
        p.op("vector", lambda e: e.reciprocal(colA.ap[:], colA.ap[:]), reads=[colA], writes=[colA])
        p.op("vector", lambda e: e.scalar_tensor_tensor(
            wg8.ap[:], em8.ap[:], colA.ap[:], ind8.ap[:], ALU.mult, ALU.mult),
            reads=[em8, colA, ind8], writes=[wg8])
        p.op("tensor", lambda e: e.transpose(bB2.ap[0:8, 0:128], wg8.ap[:], sIdf.ap[:]),
             reads=[wg8, sIdf], writes=[bB2])
        p.op("vector", lambda e, qsl=qsl: e.tensor_copy(wT8.ap[:, qsl], bB2.ap[0:8, 0:128]),
             reads=[bB2], writes=[wT8])

    nslab_tot = 0
    for st in range(NST):
        tsl = slice(st * ST, (st + 1) * ST)
        p.dma("sync", xt3.ap[:].rearrange("p (c t) -> p c t", t=ST),
              x3_sl[:, tsl].rearrange("(c p) t -> p c t", p=128), writes=[xt3], sem="q_x3a")
        p.dma("gpsimd", xb.ap[:].rearrange("p (c t) -> p c t", t=ST),
              x3_sl[:, tsl].rearrange("(c p) t -> p c t", p=128), writes=[xb], sem="q_x3b")
        for c in range(16):
            p.op("scalar", lambda e, c=c: e.activation(
                g0b.ap[:, c * ST:(c + 1) * ST], xt3.ap[:, c * ST:(c + 1) * ST], AF.Gelu),
                reads=[xt3], writes=[g0b])
        for oc in range(8):
            sl = w1s[oc % 2]
            p.dma("gpsimd", sl.ap[:, 0:16 * 128].rearrange("p (c o) -> p c o", o=128),
                  downw[:, oc * 128:(oc + 1) * 128].rearrange("(c p) o -> p c o", p=128),
                  writes=[sl], sem=f"q_w1{oc % 2}")
            bk = pool.take()
            p.mmgroup(bk, bk.ap[:, 0:ST],
                      [(sl.ap[:, kc * 128:(kc + 1) * 128], xb.ap[:, kc * ST:(kc + 1) * ST])
                       for kc in range(16)], reads=[sl, xb])
            p.op("scalar", lambda e, bk=bk, oc=oc: e.activation(
                g1b.ap[:, oc * ST:(oc + 1) * ST], bk.ap[:, 0:ST], AF.Gelu,
                bias=sDbc.ap[:, oc:oc + 1]), reads=[bk, sDbc], writes=[g1b])
        for qb in range(ST // 128):
            router(qb, st)
        for e in range(9):
            He = HID[e] if e < 8 else L
            w1d = ew1[e] if e < 8 else sw1
            w2d = ew2[e] if e < 8 else sw2
            b1c = sEb1[e] if e < 8 else sSb1
            if e < 8:
                p.op("tensor", lambda e_, e=e: e_.matmul(
                    bB2.ap[:, 0:ST], sSel.ap[:, e * 128:(e + 1) * 128], wT8.ap[:], start=True, stop=True),
                    reads=[sSel, wT8], writes=[bB2])
                p.op("vector", lambda e_: e_.tensor_copy(sWBe.ap[:], bB2.ap[:, 0:ST]),
                     reads=[bB2], writes=[sWBe])
            for slab in range(He // 1024):
                s1 = w1s[nslab_tot % 2]
                p.dma("gpsimd", s1.ap[:].rearrange("p (c o) -> p c o", o=1024),
                      w1d[:, slab * 1024:(slab + 1) * 1024].rearrange("(c p) o -> p c o", p=128),
                      writes=[s1], sem=f"q_w1{nslab_tot % 2}")
                s2 = w2s[0]
                p.dma("gpsimd", s2.ap[:].rearrange("p (c o) -> p c o", o=1024),
                      w2d[slab * 1024:(slab + 1) * 1024, :].rearrange("(c p) o -> p c o", p=128),
                      writes=[s2], sem="q_w20")
                for ocs in range(8):
                    bk = pool.take()
                    p.mmgroup(bk, bk.ap[:, 0:ST],
                              [(s1.ap[:, kc * 1024 + ocs * 128:kc * 1024 + (ocs + 1) * 128],
                                g1b.ap[:, kc * ST:(kc + 1) * ST]) for kc in range(8)],
                              reads=[s1, g1b])
                    sg = scG[ocs % 2]
                    p.op("scalar", lambda e_, bk=bk, sg=sg, b1c=b1c, slab=slab, ocs=ocs: e_.activation(
                        sg.ap[:], bk.ap[:, 0:ST], AF.Gelu, bias=b1c.ap[:, slab * 8 + ocs:slab * 8 + ocs + 1]),
                        reads=[bk, b1c], writes=[sg])
                    if e < 8:
                        p.op("vector", lambda e_, sg=sg, ocs=ocs, g1bc=sG1b[e], slab=slab: e_.scalar_tensor_tensor(
                            gss.ap[:, ocs * ST:(ocs + 1) * ST], sg.ap[:],
                            g1bc.ap[:, slab * 8 + ocs:slab * 8 + ocs + 1], sWBe.ap[:],
                            ALU.subtract, ALU.mult), reads=[sg, sG1b[e], sWBe], writes=[gss])
                    else:
                        p.op("vector", lambda e_, sg=sg, ocs=ocs: e_.tensor_scalar(
                            gss.ap[:, ocs * ST:(ocs + 1) * ST], sg.ap[:], 0.1, None, ALU.mult, ALU.bypass),
                            reads=[sg], writes=[gss])
                for moc in range(8):
                    bke = epool.take()
                    p.mmgroup(bke, bke.ap[:, 0:ST],
                              [(s2.ap[:, kc * 1024 + moc * 128:kc * 1024 + (moc + 1) * 128],
                                gss.ap[:, kc * ST:(kc + 1) * ST]) for kc in range(8)],
                              reads=[s2, gss])
                    first = (e == 0 and slab == 0)
                    mo = macc[0]
                    if first:
                        p.op("vector", lambda e_, bke=bke, moc=moc: e_.tensor_copy(
                            mo.ap[:, moc * ST:(moc + 1) * ST], bke.ap[:, 0:ST]),
                            reads=[bke], writes=[mo])
                    else:
                        p.op("vector", lambda e_, bke=bke, moc=moc, mo=mo: e_.tensor_tensor(
                            mo.ap[:, moc * ST:(moc + 1) * ST], mo.ap[:, moc * ST:(moc + 1) * ST],
                            bke.ap[:, 0:ST], ALU.add), reads=[bke, mo], writes=[mo])
                    if st == 0 and e < 8:
                        bkc_ = epool.take()
                        p.mmgroup(bkc_, bkc_.ap[:, 0:1],
                                  [(s2.ap[:, kc * 1024 + moc * 128:kc * 1024 + (moc + 1) * 128],
                                    sG1bB[e].ap[:, slab * 8 + kc:slab * 8 + kc + 1]) for kc in range(8)],
                                  reads=[s2, sG1bB[e]])
                        p.op("vector", lambda e_, bkc_=bkc_, moc=moc: e_.tensor_tensor(
                            sCcol.ap[:, moc:moc + 1], sCcol.ap[:, moc:moc + 1], bkc_.ap[:, 0:1], ALU.add),
                            reads=[bkc_, sCcol], writes=[sCcol])
                nslab_tot += 1
        for moc in range(8):
            p.op("vector", lambda e_, moc=moc: e_.tensor_scalar(
                moeb.ap[:, moc * ST:(moc + 1) * ST], macc[0].ap[:, moc * ST:(moc + 1) * ST],
                sCcol.ap[:, moc:moc + 1], None, ALU.add, ALU.bypass),
                reads=[macc[0], sCcol], writes=[moeb])
        for oc in range(16):
            su = w1s[oc % 2]
            p.dma("gpsimd", su.ap[:, 0:8 * 128].rearrange("p (c o) -> p c o", o=128),
                  upw[:, oc * 128:(oc + 1) * 128].rearrange("(c p) o -> p c o", p=128),
                  writes=[su], sem=f"q_w1{oc % 2}")
            sc_ = w2s[0]
            p.dma("gpsimd", sc_.ap[:, 0:16 * 128].rearrange("p (c o) -> p c o", o=128),
                  corew[:, oc * 128:(oc + 1) * 128].rearrange("(c p) o -> p c o", p=128),
                  writes=[sc_], sem="q_w20")
            bk = pool.take()
            p._pre("tensor", [su, sc_, moeb, g0b], [bk])
            cse = p.sem("e_tensor")
            for kc in range(8):
                p.raw("tensor", lambda e_, w=su.ap[:, kc * 128:(kc + 1) * 128],
                      r=moeb.ap[:, kc * ST:(kc + 1) * ST], o=bk.ap[:, 0:ST], s=(kc == 0):
                      e_.matmul(o, w, r, start=s, stop=False))
            for kc in range(16):
                last = kc == 15
                if last:
                    cse[1] += 1
                    p.raw("tensor", lambda e_, w=sc_.ap[:, kc * 128:(kc + 1) * 128],
                          r=g0b.ap[:, kc * ST:(kc + 1) * ST], o=bk.ap[:, 0:ST], sem=cse[0]:
                          e_.matmul(o, w, r, start=False, stop=True).then_inc(sem, 1))
                else:
                    p.raw("tensor", lambda e_, w=sc_.ap[:, kc * 128:(kc + 1) * 128],
                          r=g0b.ap[:, kc * ST:(kc + 1) * ST], o=bk.ap[:, 0:ST]:
                          e_.matmul(o, w, r, start=False, stop=False))
            tok = ("e_tensor", cse[1])
            for b_ in (su, sc_, moeb, g0b):
                b_.ctoks[tok[0]] = max(b_.ctoks.get(tok[0], 0), tok[1])
            bk.ptok = tok
            bk.ctoks = {}
            xr_ap = xrb.ap[:, oc * ST:(oc + 1) * ST]
            p.op("vector", lambda e_, bk=bk, xr_ap=xr_ap, oc=oc: e_.scalar_tensor_tensor(
                xr_ap, bk.ap[:, 0:ST], sBsum.ap[:, oc:oc + 1], xt3.ap[:, oc * ST:(oc + 1) * ST],
                ALU.add, ALU.add), reads=[bk, sBsum, xt3], writes=[xrb])
            sq_ap = gssF[:, (oc % 2) * ST:(oc % 2 + 1) * ST]
            p.op("scalar", lambda e_, xr_ap=xr_ap, sq_ap=sq_ap: e_.activation(sq_ap, xr_ap, AF.Square),
                 reads=[xrb], writes=[gss])
            p.accum_step(bS1, bS1.ap[0:1, 0:ST], sOneC.ap[:], xr_ap, oc == 0, oc == 15, reads=[sOneC, xrb])
            p.accum_step(bS2, bS2.ap[0:1, 0:ST], sOneC.ap[:], sq_ap, oc == 0, oc == 15, reads=[sOneC, gss])
        p.op("vector", lambda e_: e_.tensor_scalar(rowA.ap[:], bS1.ap[0:1, 0:ST], 1.0 / D, None, ALU.mult, ALU.bypass),
             reads=[bS1], writes=[rowA])
        p.op("vector", lambda e_: e_.tensor_scalar(rowB.ap[:], bS2.ap[0:1, 0:ST], 1.0 / D, None, ALU.mult, ALU.bypass),
             reads=[bS2], writes=[rowB])
        p.op("vector", lambda e_: e_.tensor_tensor(rowC.ap[:], rowA.ap[:], rowA.ap[:], ALU.mult),
             reads=[rowA], writes=[rowC])
        p.op("vector", lambda e_: e_.tensor_tensor(rowB.ap[:], rowB.ap[:], rowC.ap[:], ALU.subtract),
             reads=[rowB, rowC], writes=[rowB])
        p.op("scalar", lambda e_: e_.activation(rowB.ap[:], rowB.ap[:], AF.Sqrt, bias=sEps1.ap[:]),
             reads=[rowB, sEps1], writes=[rowB])
        p.op("vector", lambda e_: e_.reciprocal(rowB.ap[:], rowB.ap[:]), reads=[rowB], writes=[rowB])
        p.op("vector", lambda e_: e_.tensor_tensor(rowC.ap[:], rowA.ap[:], rowB.ap[:], ALU.mult),
             reads=[rowA, rowB], writes=[rowC])
        p.op("tensor", lambda e_: e_.matmul(bB1.ap[:, 0:ST], sOne1.ap[:], rowB.ap[:], start=True, stop=True),
             reads=[sOne1, rowB], writes=[bB1])
        p.op("tensor", lambda e_: e_.matmul(bB2.ap[:, 0:ST], sOne1.ap[:], rowC.ap[:], start=True, stop=True),
             reads=[sOne1, rowC], writes=[bB2])
        for oc in range(16):
            xr_ap = xrb.ap[:, oc * ST:(oc + 1) * ST]
            sa_ap = gssF[:, (oc % 2) * ST:(oc % 2 + 1) * ST]
            p.op("vector", lambda e_, xr_ap=xr_ap, sa_ap=sa_ap: e_.tensor_tensor(
                sa_ap, xr_ap, bB1.ap[:, 0:ST], ALU.mult), reads=[xrb, bB1], writes=[gss])
            sb_ap = gssF[:, (2 + oc % 2) * ST:(3 + oc % 2) * ST]
            p.op("vector", lambda e_, sa_ap=sa_ap, sb_ap=sb_ap: e_.tensor_tensor(
                sb_ap, sa_ap, bB2.ap[:, 0:ST], ALU.subtract), reads=[gss, bB2], writes=[gss])
            p.dma("sync", out_sl[oc * 128:(oc + 1) * 128, tsl], sb_ap,
                  reads=[gss], sem=f"q_ow{oc % 2}")

    p.wait("gpsimd", "q_ow0", p.sem("q_ow0")[1])
    p.wait("gpsimd", "q_ow1", p.sem("q_ow1")[1])
    p.raw("gpsimd", lambda e: e.engine_nop())



def stage_d_inmaps(inputs, x3_fm):
    HIDc = [1024, 2048, 3072, 4096, 1024, 2048, 3072, 4096]
    cost = np.array([2 * 1024 * h for h in HIDc], np.float32)
    sel = np.zeros((8, 8 * 128), np.float32)
    for e in range(8):
        sel[e, e * 128:(e + 1) * 128] = 1.0
    shared = dict(
        downw=inputs["L1_ffn_down_w"], downbc=_bcols(inputs["L1_ffn_down_b"]),
        upw=inputs["L1_ffn_up_w"], upbc=_bcols(inputs["L1_ffn_up_b"]),
        corew=inputs["L1_ffn_core_w"], corebc=_bcols(inputs["L1_ffn_core_b"]),
        bsumc=_bcols(inputs["L1_ffn_up_b"]),
        rw1=inputs["L1_ffn_router_w"], rb1=inputs["L1_ffn_router_b"].reshape(1, 8),
        costr=(-1e-7 * cost).reshape(1, 8),
        sw1=inputs["L1_ffn_shared_w1"], sb1c=_bcols(inputs["L1_ffn_shared_b1"]),
        sw2=inputs["L1_ffn_shared_w2"], sb2c=_bcols(inputs["L1_ffn_shared_b2"]),
        sels=sel,
        ones1=np.ones((1, 128), np.float32), onesc=np.ones((128, 1), np.float32),
        idf=np.eye(128, dtype=np.float32),
    )
    for e in range(8):
        shared[f"e{e}w1"] = inputs[f"L1_ffn_e{e}_w1"]
        shared[f"e{e}b1"] = _bcols(inputs[f"L1_ffn_e{e}_b1"])
        shared[f"e{e}w2"] = inputs[f"L1_ffn_e{e}_w2"]
        shared[f"e{e}b2"] = _bcols(inputs[f"L1_ffn_e{e}_b2"])
    ims = []
    for c in range(8):
        ims.append(dict(x3_sl=x3_fm[c], **shared))
    return ims



def DECLS_A(NCH=64):
    T, GSZ = NCH * 128, NCH // 4
    return [
        ("xfm", [2, 2048, T], F32, "ExternalInput"),
        ("wq_s", [2048, 128], F32, "ExternalInput"),
        ("wk_s", [2048, 128], F32, "ExternalInput"),
        ("wv_s", [2048, 256], F32, "ExternalInput"),
        ("ws_s", [2048, 128], F32, "ExternalInput"),
        ("wg_s", [2048, 256], F32, "ExternalInput"),
        ("gnorm", [1, 128], F32, "ExternalInput"),
        ("u32", [128, 128], F32, "ExternalInput"),
        ("ubf", [128, 128], BF16, "ExternalInput"),
        ("idbf", [128, 128], BF16, "ExternalInput"),
        ("ones1", [1, 128], F32, "ExternalInput"),
        ("onesc", [128, 1], F32, "ExternalInput"),
        ("o_scr", [2048, GSZ * 128], BF16, "ExternalOutput"),
    ]


DECLS_B = [
    ("o_sl", [2048, 2048], BF16, "ExternalInput"),
    ("x_sl", [2048, 2048], F32, "ExternalInput"),
    ("wo0", [2048, 2048], F32, "ExternalInput"),
    ("basewT", [2048, 2048], F32, "ExternalInput"),
    ("basebc", [128, 16], F32, "ExternalInput"),
    ("rwT", [2048, 8], F32, "ExternalInput"),
    ("rbrow", [1, 8], F32, "ExternalInput"),
    ("lA", [2048, 128], F32, "ExternalInput"),
    ("lB", [128, 2048], F32, "ExternalInput"),
    ("wq1", [2048, 2048], F32, "ExternalInput"),
    ("wk1", [2048, 2048], F32, "ExternalInput"),
    ("wv1", [2048, 2048], F32, "ExternalInput"),
    ("bqc", [128, 16], F32, "ExternalInput"),
    ("bkc", [128, 16], F32, "ExternalInput"),
    ("bvr", [1, 2048], F32, "ExternalInput"),
    ("inds", [8, 128], F32, "ExternalInput"),
    ("ones1", [1, 128], F32, "ExternalInput"),
    ("onesc", [128, 1], F32, "ExternalInput"),
    ("idf", [128, 128], F32, "ExternalInput"),
    ("x2_out", [2048, 2048], F32, "ExternalOutput"),
    ("q_out", [2048, 2048], BF16, "ExternalOutput"),
    ("k_out", [2048, 2048], BF16, "ExternalOutput"),
    ("v_out", [2048, 2048], BF16, "ExternalOutput"),
    ("x1_scr", [2048, 2048], F32, "Internal"),
]

DECLS_C = [
    ("q_sl", [2048, 2048], BF16, "ExternalInput"),
    ("kf", [2048, 8192], BF16, "ExternalInput"),
    ("vt", [8192, 2048], BF16, "ExternalInput"),
    ("x2_sl", [2048, 2048], F32, "ExternalInput"),
    ("wo1", [2048, 2048], F32, "ExternalInput"),
    ("boc", [128, 16], F32, "ExternalInput"),
    ("ones1", [1, 128], F32, "ExternalInput"),
    ("onesc", [128, 1], F32, "ExternalInput"),
    ("onescb", [128, 1], BF16, "ExternalInput"),
    ("idbf", [128, 128], BF16, "ExternalInput"),
    ("x3_out", [2048, 2048], F32, "ExternalOutput"),
    ("o1_scr", [2048, 2048], BF16, "Internal"),
]

DECLS_D = [
    ("x3_sl", [2048, 2048], F32, "ExternalInput"),
    ("downw", [2048, 1024], F32, "ExternalInput"),
    ("downbc", [128, 8], F32, "ExternalInput"),
    ("upw", [1024, 2048], F32, "ExternalInput"),
    ("bsumc", [128, 16], F32, "ExternalInput"),
    ("corew", [2048, 2048], F32, "ExternalInput"),
    ("upbc", [128, 16], F32, "ExternalInput"),
    ("corebc", [128, 16], F32, "ExternalInput"),
    ("rw1", [2048, 8], F32, "ExternalInput"),
    ("rb1", [1, 8], F32, "ExternalInput"),
    ("costr", [1, 8], F32, "ExternalInput"),
    ("sw1", [1024, 1024], F32, "ExternalInput"),
    ("sb1c", [128, 8], F32, "ExternalInput"),
    ("sw2", [1024, 1024], F32, "ExternalInput"),
    ("sb2c", [128, 8], F32, "ExternalInput"),
    ("sels", [8, 8 * 128], F32, "ExternalInput"),
    ("ones1", [1, 128], F32, "ExternalInput"),
    ("onesc", [128, 1], F32, "ExternalInput"),
    ("idf", [128, 128], F32, "ExternalInput"),
    ("out_sl", [2048, 2048], F32, "ExternalOutput"),
]


def _declare(nc, decls, kinds=None, prefix_done=()):
    E = {}
    for name, shape, dt, kind in decls:
        if kinds and name in kinds:
            kind = kinds[name]
        if name in prefix_done:
            continue
        E[name] = nc.dram_tensor(name, shape, dt, kind=kind).ap()
    return E


def _mk(emit_fn, decls, **kw):
    nc = bass.Bass("TRN2", target_bir_lowering=False)
    E = _declare(nc, decls)
    with contextlib.ExitStack() as stack:
        p = P(nc, stack)
        emit_fn(nc, p, stack, E, **kw)
        with nc.Block() as block:
            p.emit(block)
    return nc


def build_stage_a(NCH=64, debug=False):
    return _mk(emit_stage_a, DECLS_A(NCH), NCH=NCH, debug=debug)


def build_stage_b():
    return _mk(emit_stage_b, DECLS_B)


def build_stage_c():
    nc = bass.Bass("TRN2", target_bir_lowering=False)
    E = _declare(nc, DECLS_C)
    E["kf_parts"] = [E["kf"][:, r * 2048:(r + 1) * 2048] for r in range(4)]
    E["vt_parts"] = [E["vt"][r * 2048:(r + 1) * 2048, :] for r in range(4)]
    with contextlib.ExitStack() as stack:
        p = P(nc, stack)
        emit_stage_c(nc, p, stack, E)
        with nc.Block() as block:
            p.emit(block)
    return nc


def build_stage_d():
    return _mk(emit_stage_d, DECLS_D)



def build_fused(dump=False):
    nc = bass.Bass("TRN2", target_bir_lowering=False)
    E = {}
    seen = set()

    def decl(decls, kinds=None):
        for name, shape, dt, kind in decls:
            if name in seen:
                continue
            seen.add(name)
            if kinds and name in kinds:
                kind = kinds[name]
            E[name] = nc.dram_tensor(name, shape, dt, kind=kind).ap()

    decl(DECLS_A(64), kinds={"o_scr": "Internal"})
    decl([d for d in DECLS_B if d[0] not in ("o_sl", "x2_out", "q_out", "k_out", "v_out")])
    decl([d for d in DECLS_C if d[0] not in ("q_sl", "kf", "vt", "x2_sl", "x3_out")])
    decl([d for d in DECLS_D if d[0] != "x3_sl"])
    o_rcv = nc.dram_tensor("o_rcv", [2048, 2048], BF16).ap()
    k_scr = nc.dram_tensor("k_scr", [2048, 2048], BF16).ap()
    v_scr = nc.dram_tensor("v_scr", [2048, 2048], BF16).ap()
    k_rcv = nc.dram_tensor("k_rcv", [4 * 2048, 2048], BF16).ap()
    v_rcv = nc.dram_tensor("v_rcv", [4 * 2048, 2048], BF16).ap()
    x2_scr = nc.dram_tensor("x2_scr", [2048, 2048], F32).ap()
    q_scr = nc.dram_tensor("q_scr", [2048, 2048], BF16).ap()
    x3_scr = nc.dram_tensor("x3_scr", [2048, 2048], F32).ap()
    E["o_sl"] = o_rcv
    E["x2_out"] = x2_scr
    E["q_out"] = q_scr
    E["k_out"] = k_scr
    E["v_out"] = v_scr
    E["q_sl"] = q_scr
    E["x2_sl"] = x2_scr
    E["kf_parts"] = [k_rcv[r * 2048:(r + 1) * 2048, :] for r in range(4)]
    E["vt_parts"] = [v_rcv[r * 2048:(r + 1) * 2048, :] for r in range(4)]
    E["x3_out"] = x3_scr
    E["x3_sl"] = x3_scr

    if dump:
        E["dmp_qn"] = nc.dram_tensor("dmp_qn", [16, 2048], F32, kind="ExternalOutput").ap()
        E["dmp_km"] = nc.dram_tensor("dmp_km", [16, 1], F32, kind="ExternalOutput").ap()
    groups4 = [[0, 1, 2, 3], [4, 5, 6, 7]]
    with contextlib.ExitStack() as stack:
        p = P(nc, stack)
        _NAME_PREFIX[0] = "a_"
        with contextlib.ExitStack() as st:
            emit_stage_a(nc, p, st, E)
        _NAME_PREFIX[0] = ""
        p.barrier()
        def _a2a(e):
            with allow_ici_all_to_all():
                return e.collective_compute(
                    "AllToAll", ALU.bypass, replica_groups=[list(range(8))],
                    ins=[E["o_scr"]], outs=[o_rcv])
        orcv_b = Buf(o_rcv, "orcv")
        p.op("gpsimd", _a2a, writes=[orcv_b])
        p.op("gpsimd", _a2a, writes=[orcv_b])
        p.barrier()
        def _kag():
            for sn in ("q_qk0", "q_qk1"):
                p.wait("gpsimd", sn, p.sem(sn)[1])
            for _ in range(2):
                p.op("gpsimd", lambda e: e.collective_compute(
                    "AllGather", ALU.bypass, replica_groups=groups4,
                    ins=[k_scr], outs=[k_rcv]))
        _NAME_PREFIX[0] = "b_"
        with contextlib.ExitStack() as st:
            emit_stage_b(nc, p, st, E, mid_cb=_kag)
        _NAME_PREFIX[0] = ""
        for sn in ("q_v0", "q_v1"):
            p.wait("gpsimd", sn, p.sem(sn)[1])
        for _ in range(2):
            p.op("gpsimd", lambda e: e.collective_compute(
                "AllGather", ALU.bypass, replica_groups=groups4,
                ins=[v_scr], outs=[v_rcv]))
        p.barrier()
        _NAME_PREFIX[0] = "c_"
        with contextlib.ExitStack() as st:
            emit_stage_c(nc, p, st, E)
        _NAME_PREFIX[0] = ""
        p.barrier()
        _NAME_PREFIX[0] = "d_"
        with contextlib.ExitStack() as st:
            emit_stage_d(nc, p, st, E)
        _NAME_PREFIX[0] = ""
        if dump:
            dmp = {}
            dmp["dmp_o"] = nc.dram_tensor("dmp_o", [2048, 2048], BF16, kind="ExternalOutput").ap()
            dmp["dmp_kv"] = nc.dram_tensor("dmp_kv", [4 * 4096, 2048], BF16, kind="ExternalOutput").ap()
            dmp["dmp_x2"] = nc.dram_tensor("dmp_x2", [2048, 2048], F32, kind="ExternalOutput").ap()
            dmp["dmp_q"] = nc.dram_tensor("dmp_q", [2048, 2048], BF16, kind="ExternalOutput").ap()
            dmp["dmp_x3"] = nc.dram_tensor("dmp_x3", [2048, 2048], F32, kind="ExternalOutput").ap()
            p.barrier()
            for dst, srcap in (("dmp_o", o_rcv), ("dmp_kv", kv_rcv), ("dmp_x2", x2_scr),
                               ("dmp_q", q_scr), ("dmp_x3", x3_scr)):
                p.dma_serial("sync", dmp[dst], srcap, sem="q_dmp")
            p.wait("gpsimd", "q_dmp", p.sem("q_dmp")[1])
        with nc.Block() as block:
            p.emit(block)
    return nc


def fused_inmaps(inputs):
    ims_a = stage_a_inmaps(inputs)
    o_dummy = None
    x = inputs["L0_attn_hidden_states"]
    xT = np.ascontiguousarray(x.transpose(0, 2, 1))
    ims_b = stage_b_inmaps(inputs, [xT.astype(BF), xT.astype(BF)][0:1] * 2 if False else
                           np.zeros((2, 2048, 8192), dtype=BF))
    ims_c = stage_c_inmaps(inputs, [None] * 8, np.zeros((2, 1, 1), BF), np.zeros((2, 1, 1), BF),
                           [None] * 8)
    ims_d = stage_d_inmaps(inputs, [None] * 8)
    ims = []
    for c in range(8):
        m = {}
        m.update(ims_a[c])
        for k_, v_ in ims_b[c].items():
            if k_ not in ("o_sl",):
                m.setdefault(k_, v_)
        for k_, v_ in ims_c[c].items():
            if k_ not in ("q_sl", "kf", "vt", "x2_sl") and v_ is not None:
                m.setdefault(k_, v_)
        for k_, v_ in ims_d[c].items():
            if k_ not in ("x3_sl",):
                m.setdefault(k_, v_)
        ims.append(m)
    return ims


def kernel(**inputs):
    cores = list(range(8))
    nc = build_fused()
    ims = fused_inmaps(inputs)
    res = run_bass_kernel_spmd(nc, ims, core_ids=cores)
    out = np.zeros((2, 8192, 2048), np.float32)
    for c in cores:
        b, g = c // 4, c % 4
        out[b, g * 2048:(g + 1) * 2048, :] = np.asarray(res.results[c]["out_sl"]).T
    return out


# version 10
# speedup vs baseline: 1.0894x; 1.0244x over previous
import contextlib
import numpy as np
import ml_dtypes
import concourse.bass as bass
import concourse.mybir as mybir
from concourse.bass_utils import run_bass_kernel_spmd
from concourse.aot_env import allow_ici_all_to_all

F32 = mybir.dt.float32
BF16 = mybir.dt.bfloat16
AF = mybir.ActivationFunctionType
ALU = mybir.AluOpType
AX = mybir.AxisListType.X
BF = ml_dtypes.bfloat16

ENGINES = ("sync", "gpsimd", "tensor", "vector", "scalar")


class Buf:
    __slots__ = ("ap", "ptok", "ctoks", "name", "serial")

    def __init__(self, ap, name="", serial=False):
        self.ap = ap
        self.name = name
        self.serial = serial
        self.ptok = None
        self.ctoks = {}


class P:

    def __init__(self, nc, stack):
        self.nc = nc
        self.stack = stack
        self.streams = {n: [] for n in ENGINES}
        self.sems = {}
        self.lastw = {}

    def sem(self, name):
        if name not in self.sems:
            self.sems[name] = [self.stack.enter_context(self.nc.semaphore(name)), 0]
        return self.sems[name]

    def raw(self, eng, fn):
        self.streams[eng].append(fn)

    def wait(self, eng, semname, val):
        if val is None or val <= 0:
            return
        if self.lastw.get((eng, semname), 0) >= val:
            return
        self.lastw[(eng, semname)] = val
        sem = self.sem(semname)[0]
        self.raw(eng, lambda e, sem=sem, val=val: e.wait_ge(sem, val))

    def _pre(self, eng, reads, writes):
        for b in reads:
            if b.ptok:
                self.wait(eng, *b.ptok)
            if b.serial:
                for sn, v in b.ctoks.items():
                    self.wait(eng, sn, v)
        for b in writes:
            for sn, v in b.ctoks.items():
                self.wait(eng, sn, v)
            if b.ptok:
                self.wait(eng, *b.ptok)

    def _post(self, tok, reads, writes):
        for b in reads:
            b.ctoks[tok[0]] = max(b.ctoks.get(tok[0], 0), tok[1])
        for b in writes:
            b.ptok = tok
            b.ctoks = {}

    def op(self, eng, fn, reads=(), writes=(), dma=False):
        self._pre(eng, reads, writes)
        if dma:
            semname = dma if isinstance(dma, str) else "q_" + eng
            c = self.sem(semname)
            c[1] += 16
            tok = (semname, c[1])
            self.raw(eng, lambda e, fn=fn, sem=c[0], v=c[1]: fn(e).then_inc(sem, 16))
        else:
            semname = "e_" + eng
            c = self.sem(semname)
            c[1] += 1
            tok = (semname, c[1])
            self.raw(eng, lambda e, fn=fn, sem=c[0]: fn(e).then_inc(sem, 1))
        self._post(tok, reads, writes)
        return tok

    def dma_serial(self, eng, out, in_, reads=(), writes=(), sem=None, **kw):
        semname = sem or ("q_" + eng)
        self.wait(eng, semname, self.sem(semname)[1])
        return self.dma(eng, out, in_, reads=reads, writes=writes, sem=semname, **kw)

    def dma(self, eng, out, in_, reads=(), writes=(), sem=None, **kw):
        return self.op(
            eng,
            lambda e, out=out, in_=in_, kw=kw: e.dma_start(out=out, in_=in_, **kw),
            reads=reads, writes=writes, dma=(sem or True),
        )

    def mmgroup(self, psum_buf, out_ap, pairs, reads=()):
        self._pre("tensor", reads, [psum_buf])
        n = len(pairs)
        c = self.sem("e_tensor")
        for i, (lhsT, rhs) in enumerate(pairs):
            if i == n - 1:
                c[1] += 1
                self.raw("tensor", lambda e, l=lhsT, r=rhs, o=out_ap, s=(i == 0), sem=c[0]:
                         e.matmul(o, l, r, start=s, stop=True).then_inc(sem, 1))
            else:
                self.raw("tensor", lambda e, l=lhsT, r=rhs, o=out_ap, s=(i == 0):
                         e.matmul(o, l, r, start=s, stop=False))
        tok = ("e_tensor", c[1])
        self._post(tok, reads, [psum_buf])
        return tok

    def barrier(self):
        snap = [(sn, c[1]) for sn, c in self.sems.items()]
        for eng in ENGINES:
            for sn, v in snap:
                if sn == "e_" + eng:
                    continue
                self.wait(eng, sn, v)

    def accum_step(self, psum_buf, out_ap, lhsT, rhs, start, stop, reads=()):
        if start:
            self._pre("tensor", reads, [psum_buf])
        else:
            self._pre("tensor", reads, [])
        if stop:
            c = self.sem("e_tensor")
            c[1] += 1
            self.raw("tensor", lambda e, l=lhsT, r=rhs, o=out_ap, s=start, sem=c[0]:
                     e.matmul(o, l, r, start=s, stop=True).then_inc(sem, 1))
            tok = ("e_tensor", c[1])
            for b in reads:
                b.ctoks[tok[0]] = max(b.ctoks.get(tok[0], 0), tok[1])
            psum_buf.ptok = tok
            psum_buf.ctoks = {}
            return tok
        else:
            self.raw("tensor", lambda e, l=lhsT, r=rhs, o=out_ap, s=start:
                     e.matmul(o, l, r, start=s, stop=False))
            tok = ("e_tensor", self.sem("e_tensor")[1])
            for b in reads:
                b.ctoks[tok[0]] = max(b.ctoks.get(tok[0], 0), tok[1] + 1)
            return None

    def emit(self, block):
        for n in ENGINES:
            fns = self.streams[n]
            if not fns:
                continue
            dec = getattr(block, n)

            def body(e, fns=fns):
                for f in fns:
                    f(e)
            dec(body)


_NAME_PREFIX = [""]


def _sb(nc, stack, name, shape, dt):
    name = _NAME_PREFIX[0] + name
    return Buf(stack.enter_context(nc.sbuf_tensor(name, list(shape), dt)), name)


def _bank(nc, stack, name, dt=F32, cols=None):
    name = _NAME_PREFIX[0] + name
    cols = cols or (512 if dt == F32 else 1024)
    return Buf(stack.enter_context(nc.psum_tensor(name, [128, cols], dt)), name, serial=True)


class Pool:

    def __init__(self, banks):
        self.banks = banks
        self.i = 0

    def take(self):
        b = self.banks[self.i % len(self.banks)]
        self.i += 1
        return b



def emit_stage_a(nc, p, stack, E, NCH=64, debug=False):
    H2 = 2
    D = 2048
    T = NCH * 128
    GSZ = NCH // 4

    xfm = E["xfm"]
    wq_s = E["wq_s"]
    wk_s = E["wk_s"]
    wv_s = E["wv_s"]
    ws_s = E["ws_s"]
    wg_s = E["wg_s"]
    gnorm = E["gnorm"]
    u32 = E["u32"]
    ubf = E["ubf"]
    idbf = E["idbf"]
    ones1 = E["ones1"]
    onesc = E["onesc"]

    o_scr = E["o_scr"]
    dbg = {}
    if debug:
        for nm, sh, dt in [("dQs", (64, 128), F32), ("dKf", (64, 128), F32), ("dKtm", (128, 128), F32),
                           ("dSc", (128, 128), F32), ("dE32", (128, 128), F32), ("dDd", (128, 128), F32),
                           ("dokk", (128, 64), F32), ("dpvU", (128, 64), F32), ("dpvD", (128, 64), F32),
                           ("dEt", (64, 128), F32), ("dpvDt", (64, 128), F32), ("dBTm", (128, 128), F32),
                           ("dATm", (128, 128), F32), ("doh", (128, 128), F32), ("dVtm", (128, 256), F32),
                           ("dGt", (128, 256), F32), ("drstd", (128, 1), F32)]:
            dbg[nm] = nc.dram_tensor(nm, list(sh), dt, kind="ExternalOutput").ap()

    W = {}
    W["q"] = _sb(nc, stack, "w_q", (128, 16 * 128), BF16)
    W["k"] = _sb(nc, stack, "w_k", (128, 16 * 128), BF16)
    W["s"] = _sb(nc, stack, "w_s", (128, 16 * 128), BF16)
    W["v"] = _sb(nc, stack, "w_v", (128, 16 * 256), BF16)
    W["g"] = _sb(nc, stack, "w_g", (128, 16 * 256), BF16)
    sU32 = _sb(nc, stack, "sU32", (128, 128), F32)
    sUbf = _sb(nc, stack, "sUbf", (128, 128), BF16)
    sIdb = _sb(nc, stack, "sIdb", (128, 128), BF16)
    sOne1 = _sb(nc, stack, "sOne1", (1, 128), F32)
    sOneC = _sb(nc, stack, "sOneC", (128, 1), F32)
    sGn = _sb(nc, stack, "sGn", (1, 128), F32)
    sGnB = _sb(nc, stack, "sGnB", (128, 128), F32)

    xb = [[_sb(nc, stack, f"xb{b}{i}", (128, 16 * 128), BF16) for i in range(2)] for b in range(2)]

    Qs = [[[_sb(nc, stack, f"Qs{s}{b}{k}", (64, 128), BF16) for k in range(H2)] for b in range(2)] for s in range(2)]
    Kf = [[[_sb(nc, stack, f"Kf{s}{b}{k}", (64, 128), BF16) for k in range(H2)] for b in range(2)] for s in range(2)]
    Ktm = [[_sb(nc, stack, f"Ktm{s}{b}", (128, 128), BF16) for b in range(2)] for s in range(2)]
    Sc = [[_sb(nc, stack, f"Sc{s}{b}", (128, 128), F32) for b in range(2)] for s in range(2)]
    Vtm = [[_sb(nc, stack, f"Vtm{s}{b}", (128, 256), BF16) for b in range(2)] for s in range(2)]
    sGt = [[_sb(nc, stack, f"sGt{s}{b}", (128, 256), F32) for b in range(2)] for s in range(2)]
    E32 = [_sb(nc, stack, f"E32{b}", (128, 128), F32) for b in range(2)]
    Ebf = [_sb(nc, stack, f"Ebf{b}", (128, 128), BF16) for b in range(2)]
    Dd = [_sb(nc, stack, f"Dd{b}", (128, 128), F32) for b in range(2)]
    ATm = [_sb(nc, stack, f"ATm{c}", (128, 128), BF16) for c in range(4)]
    okk = [_sb(nc, stack, f"okk{c}", (128, 64), F32) for c in range(4)]
    nmx = [_sb(nc, stack, f"nmx{c}", (128, 1), F32) for c in range(4)]
    pvU = [_sb(nc, stack, f"pvU{c}", (128, 64), F32) for c in range(4)]
    ssum = [_sb(nc, stack, f"ssum{c}", (128, 1), F32) for c in range(4)]
    rs_ = [_sb(nc, stack, f"rs_{c}", (128, 1), F32) for c in range(4)]
    pvD = [_sb(nc, stack, f"pvD{c}", (128, 64), BF16) for c in range(4)]
    Et = [_sb(nc, stack, f"Et{c}", (64, 128), BF16) for c in range(4)]
    pvDt = [_sb(nc, stack, f"pvDt{c}", (64, 128), BF16) for c in range(4)]
    BTm = [_sb(nc, stack, f"BTm{c}", (128, 128), BF16) for c in range(4)]
    ssq = [_sb(nc, stack, f"ssq{c}", (128, 1), F32) for c in range(4)]
    rstd = [_sb(nc, stack, f"rstd{c}", (128, 1), F32) for c in range(4)]
    rstd2 = [_sb(nc, stack, f"rstd2{c}", (128, 1), F32) for c in range(4)]
    gate2 = [_sb(nc, stack, f"gate2{c}", (128, 128), F32) for c in range(4)]
    o_h = [_sb(nc, stack, f"o_h{c}", (128, 128), BF16) for c in range(4)]
    scr128 = _sb(nc, stack, "scr128", (128, 128), F32)
    sEps = _sb(nc, stack, "sEps", (128, 1), F32)

    Wrow = [[_sb(nc, stack, f"Wr{b}{i}", (1, 128), F32) for i in range(2)] for b in range(2)]
    HkU = [[_sb(nc, stack, f"HkU{c}{i}", (64, 64), F32) for i in range(2)] for c in range(4)]
    HvU = [[_sb(nc, stack, f"HvU{c}{i}", (64, 128), F32) for i in range(2)] for c in range(4)]
    HkB = [_sb(nc, stack, f"HkB{c}", (64, 64), BF16) for c in range(4)]
    HvB = [_sb(nc, stack, f"HvB{c}", (64, 128), BF16) for c in range(4)]

    obuf = [_sb(nc, stack, f"obuf{i}", (128, 4 * GSZ * 128), BF16) for i in range(2)]

    prj = Pool([_bank(nc, stack, "bkP0"), _bank(nc, stack, "bkP1")])
    pool = Pool([_bank(nc, stack, "bkA"), _bank(nc, stack, "bkB")])
    bOv_ = _bank(nc, stack, "bkOv0")
    bOv = [bOv_, bOv_]
    bT = [_bank(nc, stack, "bkT0", BF16), _bank(nc, stack, "bkT1", BF16)]
    bM = _bank(nc, stack, "bkM")

    for buf, src in ((sU32, u32), (sUbf, ubf), (sIdb, idbf), (sOne1, ones1),
                     (sOneC, onesc), (sGn, gnorm)):
        p.dma_serial("sync", buf.ap[:], src, writes=[buf], sem="q_cst")
    for name, src, wdt in (("q", wq_s, 128), ("k", wk_s, 128), ("s", ws_s, 128),
                           ("v", wv_s, 256), ("g", wg_s, 256)):
        p.dma_serial("gpsimd", W[name].ap[:].rearrange("p (c o) -> p c o", o=wdt),
              src.rearrange("(c p) o -> p c o", p=128),
              writes=[W[name]], sem="q_wld")
    p.op("tensor", lambda e: e.matmul(bM.ap[:, 0:128], sOne1.ap[:], sGn.ap[:], start=True, stop=True),
         reads=[sOne1, sGn], writes=[bM])
    p.op("vector", lambda e: e.tensor_copy(sGnB.ap[:], bM.ap[:, 0:128]), reads=[bM], writes=[sGnB])
    p.op("vector", lambda e: e.memset(sEps.ap[:], 1e-5), writes=[sEps])
    for b in range(2):
        for i in range(2):
            p.op("vector", lambda e, t=Wrow[b][i]: e.memset(t.ap[:], 0.0), writes=[Wrow[b][i]])
    for c in range(4):
        for i in range(2):
            p.op("vector", lambda e, t=HkU[c][i]: e.memset(t.ap[:], 0.0), writes=[HkU[c][i]])
            p.op("vector", lambda e, t=HvU[c][i]: e.memset(t.ap[:], 0.0), writes=[HvU[c][i]])
        p.op("vector", lambda e, t=HkB[c]: e.memset(t.ap[:], 0.0), writes=[HkB[c]])
        p.op("vector", lambda e, t=HvB[c]: e.memset(t.ap[:], 0.0), writes=[HvB[c]])

    n_odma = [0]

    def proj(j):
        sl = j % 2
        Qs_, Kf_, Ktm_, Sc_, Vtm_, sGt_ = Qs[sl], Kf[sl], Ktm[sl], Sc[sl], Vtm[sl], sGt[sl]
        for b in range(2):
            x = xb[b][j % 2]
            p.dma("gpsimd", x.ap[:].rearrange("p (c t) -> p c t", t=128),
                  xfm[b, :, j * 128:(j + 1) * 128].rearrange("(c p) t -> p c t", p=128),
                  writes=[x], sem=f"q_x{b}{j % 2}")
            kc = [x.ap[:, i * 128:(i + 1) * 128] for i in range(16)]
            bk = prj.take()
            p.mmgroup(bk, bk.ap[:, 0:128],
                      [(W["q"].ap[:, i * 128:(i + 1) * 128], kc[i]) for i in range(16)],
                      reads=[W["q"], x])
            for k in range(H2):
                p.op("scalar", lambda e, k=k, b=b, bk=bk: e.activation(
                    Qs_[b][k].ap[:], bk.ap[k * 64:(k + 1) * 64, 0:128], AF.Copy, scale=0.125),
                    reads=[bk], writes=[Qs_[b][k]])
            bk = prj.take()
            p.mmgroup(bk, bk.ap[:, 0:128],
                      [(W["k"].ap[:, i * 128:(i + 1) * 128], kc[i]) for i in range(16)],
                      reads=[W["k"], x])
            for k in range(H2):
                p.op("scalar", lambda e, k=k, b=b, bk=bk: e.copy(
                    Kf_[b][k].ap[:], bk.ap[k * 64:(k + 1) * 64, 0:128]),
                    reads=[bk], writes=[Kf_[b][k]])
            bk = prj.take()
            p.mmgroup(bk, bk.ap[:, 0:128],
                      [(kc[i], W["k"].ap[:, i * 128:(i + 1) * 128]) for i in range(16)],
                      reads=[W["k"], x])
            p.op("vector", lambda e, b=b, bk=bk: e.tensor_copy(Ktm_[b].ap[:], bk.ap[:, 0:128]),
                 reads=[bk], writes=[Ktm_[b]])
            bk = prj.take()
            p.mmgroup(bk, bk.ap[:, 0:128],
                      [(kc[i], W["s"].ap[:, i * 128:(i + 1) * 128]) for i in range(16)],
                      reads=[W["s"], x])
            p.op("vector", lambda e, b=b, bk=bk: e.tensor_scalar(
                Sc_[b].ap[:], bk.ap[:, 0:128], 32.0, -32.0, ALU.min, ALU.max),
                reads=[bk], writes=[Sc_[b]])
            bk = prj.take()
            p.mmgroup(bk, bk.ap[:, 0:256],
                      [(kc[i], W["v"].ap[:, i * 256:(i + 1) * 256]) for i in range(16)],
                      reads=[W["v"], x])
            p.op("vector", lambda e, b=b, bk=bk: e.tensor_copy(Vtm_[b].ap[:], bk.ap[:, 0:256]),
                 reads=[bk], writes=[Vtm_[b]])
            bk = prj.take()
            p.mmgroup(bk, bk.ap[:, 0:256],
                      [(kc[i], W["g"].ap[:, i * 256:(i + 1) * 256]) for i in range(16)],
                      reads=[W["g"], x])
            p.op("scalar", lambda e, b=b, bk=bk: e.activation(sGt_[b].ap[:], bk.ap[:, 0:256], AF.Silu),
                 reads=[bk], writes=[sGt_[b]])


    def cproc(j):
        sl = j % 2
        g = j // GSZ
        jj = j % GSZ
        ob = obuf[g % 2]
        Qs_, Kf_, Ktm_, Sc_, Vtm_, sGt_ = Qs[sl], Kf[sl], Ktm[sl], Sc[sl], Vtm[sl], sGt[sl]
        for b in range(2):
            p.op("scalar", lambda e, b=b: e.activation(E32[b].ap[:], Sc_[b].ap[:], AF.Exp),
                 reads=[Sc_[b]], writes=[E32[b]])
            p.op("vector", lambda e, b=b: e.tensor_copy(Ebf[b].ap[:], E32[b].ap[:]),
                 reads=[E32[b]], writes=[Ebf[b]])
            wr_old = Wrow[b][j % 2]
            wr_new = Wrow[b][(j + 1) % 2]
            bk = pool.take()
            p.mmgroup(bk, bk.ap[:, 0:128], [(sU32.ap[:], E32[b].ap[:]), (sOne1.ap[:], wr_old.ap[:])],
                      reads=[sU32, E32[b], sOne1, wr_old])
            p.op("vector", lambda e, b=b, bk=bk: e.reciprocal(Dd[b].ap[:], bk.ap[:, 0:128]),
                 reads=[bk], writes=[Dd[b]])
            p.op("tensor", lambda e, b=b: e.matmul(bM.ap[0:1, 0:128], sOneC.ap[:], E32[b].ap[:],
                                                   start=True, stop=True),
                 reads=[sOneC, E32[b]], writes=[bM])
            p.op("vector", lambda e, wo=wr_old, wn=wr_new: e.tensor_tensor(
                wn.ap[:], wo.ap[:], bM.ap[0:1, 0:128], ALU.add),
                reads=[wr_old, bM], writes=[wr_new])

            def tap(nm, buf, ap=None):
                if debug and j == 0 and b == 0:
                    tmp = dbg[nm]
                    p.dma_serial("gpsimd", tmp, ap if ap is not None else buf.ap[:], reads=[buf], sem="q_dbg")
            if debug and j == 0 and b == 0:
                tap("dE32", E32[0]); tap("dDd", Dd[0]); tap("dSc", Sc_[0])
                tap("dKtm", Ktm_[0]); tap("dVtm", Vtm_[0]); tap("dGt", sGt_[0])
                tap("dQs", Qs_[0][0]); tap("dKf", Kf_[0][0])
            for k in range(H2):
                c = b * 2 + k
                ms = slice(k * 64, (k + 1) * 64)
                vs = slice(k * 128, (k + 1) * 128)
                tAT, tok_, tnm, tpv, tss, trs, tpd = ATm[c], okk[c], nmx[c], pvU[c], ssum[c], rs_[c], pvD[c]
                tEt, tpt, tBT = Et[c], pvDt[c], BTm[c]
                tb = bT[c % 2]
                bk = pool.take()
                p.op("tensor", lambda e, b=b, k=k, bk=bk: e.matmul(
                    bk.ap[:, 0:128], Kf_[b][k].ap[:], Qs_[b][k].ap[:], start=True, stop=True),
                    reads=[Kf_[b][k], Qs_[b][k]], writes=[bk])
                p.op("vector", lambda e, bk=bk, tAT=tAT: e.tensor_tensor(
                    tAT.ap[:], bk.ap[:, 0:128], sUbf.ap[:], ALU.mult),
                    reads=[bk, sUbf], writes=[tAT])
                if k == 0:
                    tap("dATm", tAT)
                bk = pool.take()
                p.mmgroup(bk, bk.ap[:, 0:64], [
                    (tAT.ap[:], Ebf[b].ap[:, ms]),
                    (Qs_[b][k].ap[:], HkB[c].ap[:]),
                ], reads=[tAT, Ebf[b], Qs_[b][k], HkB[c]])
                p.op("vector", lambda e, b=b, ms=ms, bk=bk, tok_=tok_: e.tensor_tensor(
                    tok_.ap[:], bk.ap[:, 0:64], Dd[b].ap[:, ms], ALU.mult),
                    reads=[bk, Dd[b]], writes=[tok_])
                p.op("vector", lambda e, tok_=tok_, tnm=tnm: e.reduce_max(
                    tnm.ap[:], tok_.ap[:], axis=AX, negate=True),
                    reads=[tok_], writes=[tnm])
                p.op("scalar", lambda e, tok_=tok_, tnm=tnm, tpv=tpv, tss=tss: e.activation(
                    tpv.ap[:], tok_.ap[:], AF.Exp, bias=tnm.ap[:], accum_out=tss.ap[:]),
                    reads=[tok_, tnm], writes=[tpv, tss])
                p.op("vector", lambda e, tss=tss, trs=trs: e.reciprocal(trs.ap[:], tss.ap[:]),
                     reads=[tss], writes=[trs])
                p.op("vector", lambda e, b=b, ms=ms, tpv=tpv, trs=trs, tpd=tpd: e.scalar_tensor_tensor(
                    tpd.ap[:], tpv.ap[:], trs.ap[:], Dd[b].ap[:, ms], ALU.mult, ALU.mult),
                    reads=[tpv, trs, Dd[b]], writes=[tpd])
                if k == 0:
                    tap("dokk", tok_); tap("dpvU", tpv); tap("dpvD", tpd)
                p.op("tensor", lambda e, b=b, ms=ms, tb=tb: e.transpose(
                    tb.ap[0:64, 0:128], Ebf[b].ap[:, ms], sIdb.ap[:]),
                    reads=[Ebf[b], sIdb], writes=[tb])
                p.op("vector", lambda e, tb=tb, tEt=tEt: e.tensor_copy(tEt.ap[:], tb.ap[0:64, 0:128]),
                     reads=[tb], writes=[tEt])
                p.op("tensor", lambda e, tb=tb, tpd=tpd: e.transpose(
                    tb.ap[0:64, 128:256], tpd.ap[:], sIdb.ap[:]),
                    reads=[tpd, sIdb], writes=[tb])
                p.op("vector", lambda e, tb=tb, tpt=tpt: e.tensor_copy(tpt.ap[:], tb.ap[0:64, 128:256]),
                     reads=[tb], writes=[tpt])
                if k == 0:
                    tap("dEt", tEt); tap("dpvDt", tpt)
                bk = pool.take()
                p.op("tensor", lambda e, bk=bk, tEt=tEt, tpt=tpt: e.matmul(
                    bk.ap[:, 0:128], tEt.ap[:], tpt.ap[:], start=True, stop=True),
                    reads=[tEt, tpt], writes=[bk])
                p.op("vector", lambda e, bk=bk, tBT=tBT: e.tensor_tensor(
                    tBT.ap[:], bk.ap[:, 0:128], sUbf.ap[:], ALU.mult),
                    reads=[bk, sUbf], writes=[tBT])
                if k == 0:
                    tap("dBTm", tBT)
                p.mmgroup(bOv[b], bOv[b].ap[:, b * 256 + vs.start:b * 256 + vs.stop] if False else bOv[b].ap[:, vs], [
                    (tBT.ap[:], Vtm_[b].ap[:, vs]),
                    (tpt.ap[:], HvB[c].ap[:]),
                ], reads=[tBT, Vtm_[b], tpt, HvB[c]])
                p.op("tensor", lambda e, b=b, ms=ms: e.matmul(
                    bM.ap[0:64, 128:192], Ktm_[b].ap[:, ms], Ebf[b].ap[:, ms], start=True, stop=True),
                    reads=[Ktm_[b], Ebf[b]], writes=[bM])
                hk_old, hk_new = HkU[c][j % 2], HkU[c][(j + 1) % 2]
                p.op("vector", lambda e, ho=hk_old, hn=hk_new: e.tensor_tensor(
                    hn.ap[:], ho.ap[:], bM.ap[0:64, 128:192], ALU.add),
                    reads=[hk_old, bM], writes=[hk_new])
                p.op("vector", lambda e, c=c, hn=hk_new: e.tensor_copy(HkB[c].ap[:], hn.ap[:]),
                     reads=[hk_new], writes=[HkB[c]])
                p.op("tensor", lambda e, b=b, ms=ms, vs=vs: e.matmul(
                    bM.ap[0:64, 256:384], Ebf[b].ap[:, ms], Vtm_[b].ap[:, vs], start=True, stop=True),
                    reads=[Ebf[b], Vtm_[b]], writes=[bM])
                hv_old, hv_new = HvU[c][j % 2], HvU[c][(j + 1) % 2]
                p.op("vector", lambda e, ho=hv_old, hn=hv_new: e.tensor_tensor(
                    hn.ap[:], ho.ap[:], bM.ap[0:64, 256:384], ALU.add),
                    reads=[hv_old, bM], writes=[hv_new])
                p.op("vector", lambda e, c=c, hn=hv_new: e.tensor_copy(HvB[c].ap[:], hn.ap[:]),
                     reads=[hv_new], writes=[HvB[c]])

            for k in range(H2):
                c = b * 2 + k
                vs = slice(k * 128, (k + 1) * 128)
                tsq, tr1, tr2, tg2, toh = ssq[c], rstd[c], rstd2[c], gate2[c], o_h[c]
                tb = bT[c % 2]
                p.op("scalar", lambda e, b=b, vs=vs, tsq=tsq: e.activation(
                    scr128.ap[:], bOv[b].ap[:, vs], AF.Square, accum_out=tsq.ap[:]),
                    reads=[bOv[b]], writes=[scr128, tsq])
                p.op("scalar", lambda e, tsq=tsq, tr1=tr1: e.activation(
                    tr1.ap[:], tsq.ap[:], AF.Ln, scale=1.0 / 128.0, bias=sEps.ap[:]),
                    reads=[tsq, sEps], writes=[tr1])
                p.op("scalar", lambda e, tr1=tr1, tr2=tr2: e.activation(
                    tr2.ap[:], tr1.ap[:], AF.Exp, scale=-0.5),
                    reads=[tr1], writes=[tr2])
                p.op("vector", lambda e, b=b, vs=vs, tg2=tg2: e.tensor_tensor(
                    tg2.ap[:], sGt_[b].ap[:, vs], sGnB.ap[:], ALU.mult),
                    reads=[sGt_[b], sGnB], writes=[tg2])
                p.op("vector", lambda e, b=b, vs=vs, tr2=tr2, tg2=tg2, toh=toh: e.scalar_tensor_tensor(
                    toh.ap[:], bOv[b].ap[:, vs], tr2.ap[:], tg2.ap[:], ALU.mult, ALU.mult),
                    reads=[bOv[b], tr2, tg2], writes=[toh])
                if k == 0:
                    tap("doh", toh); tap("drstd", tr2)
                p.op("tensor", lambda e, tb=tb, toh=toh: e.transpose(
                    tb.ap[:, 256:384], toh.ap[:], sIdb.ap[:]),
                    reads=[toh, sIdb], writes=[tb])
                ocol = (c * GSZ + jj) * 128
                p.op("vector", lambda e, ocol=ocol, ob=ob, tb=tb: e.tensor_copy(
                    ob.ap[:, ocol:ocol + 128], tb.ap[:, 256:384]),
                    reads=[tb], writes=[ob])

        if jj == GSZ - 1:
            for c in range(4):
                b = c // 2
                k = c % 2
                dst = 4 * b + g
                p.dma("sync", o_scr[dst * 256 + k * 128:dst * 256 + (k + 1) * 128, :],
                      ob.ap[:, c * GSZ * 128:(c + 1) * GSZ * 128],
                      reads=[ob], sem=f"q_out{g % 2}")
                n_odma[0] += 1


    for j in range(NCH):
        proj(j)
        cproc(j)

    p.wait("gpsimd", "q_out0", p.sem("q_out0")[1])
    p.wait("gpsimd", "q_out1", p.sem("q_out1")[1])
    p.raw("gpsimd", lambda e: e.engine_nop())




def _bcols(v, dtype=np.float32):
    return np.ascontiguousarray(v.reshape(-1, 128).T).astype(dtype)


def _consts_a():
    u = np.triu(np.ones((128, 128), np.float32))
    return dict(
        u32=u,
        ubf=u.astype(BF),
        idbf=np.eye(128, dtype=BF),
        ones1=np.ones((1, 128), np.float32),
        onesc=np.ones((128, 1), np.float32),
    )


def stage_a_inmaps(inputs):
    x = inputs["L0_attn_hidden_states"]
    xT = np.ascontiguousarray(x.transpose(0, 2, 1))
    consts = _consts_a()
    ims = []
    for c in range(8):
        h0 = 2 * c
        ims.append(dict(
            xfm=xT,
            wq_s=np.ascontiguousarray(inputs["L0_attn_Wq"][:, h0 * 64:(h0 + 2) * 64]),
            wk_s=np.ascontiguousarray(inputs["L0_attn_Wk"][:, h0 * 64:(h0 + 2) * 64]),
            wv_s=np.ascontiguousarray(inputs["L0_attn_Wv"][:, h0 * 128:(h0 + 2) * 128]),
            ws_s=np.ascontiguousarray(inputs["L0_attn_Ws"][:, h0 * 64:(h0 + 2) * 64]),
            wg_s=np.ascontiguousarray(inputs["L0_attn_Wg"][:, h0 * 128:(h0 + 2) * 128]),
            gnorm=np.ascontiguousarray(inputs["L0_attn_g_norm_weight"].reshape(1, 128)),
            **consts,
        ))
    return ims


def assemble_o(results):
    o = np.zeros((2, 2048, 8192), dtype=BF)
    for c in range(8):
        sc = np.asarray(results[c]["o_scr"]).reshape(8, 256, -1)
        for dst in range(8):
            b, gg = dst // 4, dst % 4
            o[b, 2 * c * 128:(2 * c + 2) * 128, gg * 2048:(gg + 1) * 2048] = sc[dst]
    return o


def kernel_unfused(**inputs):
    cores = list(range(8))
    nc_a = build_stage_a()
    res_a = run_bass_kernel_spmd(nc_a, stage_a_inmaps(inputs), core_ids=cores)
    o_fm = assemble_o(res_a.results)

    nc_b = build_stage_b()
    res_b = run_bass_kernel_spmd(nc_b, stage_b_inmaps(inputs, o_fm), core_ids=cores)
    k_fm = np.zeros((2, 2048, 8192), dtype=BF)
    v_tm = np.zeros((2, 8192, 2048), dtype=BF)
    q_fm = []
    x2_fm = []
    for c in cores:
        b, g = c // 4, c % 4
        r = res_b.results[c]
        k_fm[b][:, g * 2048:(g + 1) * 2048] = np.asarray(r["k_out"])
        v_tm[b][g * 2048:(g + 1) * 2048, :] = np.asarray(r["v_out"])
        q_fm.append(np.asarray(r["q_out"]))
        x2_fm.append(np.asarray(r["x2_out"]))

    nc_c = build_stage_c()
    res_c = run_bass_kernel_spmd(nc_c, stage_c_inmaps(inputs, q_fm, k_fm, v_tm, x2_fm), core_ids=cores)
    x3_fm = [np.asarray(res_c.results[c]["x3_out"]) for c in cores]

    nc_d = build_stage_d()
    res_d = run_bass_kernel_spmd(nc_d, stage_d_inmaps(inputs, x3_fm), core_ids=cores)
    out = np.zeros((2, 8192, 2048), np.float32)
    for c in cores:
        b, g = c // 4, c % 4
        out[b, g * 2048:(g + 1) * 2048, :] = np.asarray(res_d.results[c]["out_sl"]).T
    return out



def emit_stage_b(nc, p, stack, E, mid_cb=None):
    D = 2048
    SC = 2048.0 ** -0.5

    o_sl = E["o_sl"]
    x_sl = E["x_sl"]
    wo0 = E["wo0"]
    basewT = E["basewT"]
    basebc = E["basebc"]
    rwT = E["rwT"]
    rbrow = E["rbrow"]
    lA = E["lA"]
    lB = E["lB"]
    wq1 = E["wq1"]
    wk1 = E["wk1"]
    wv1 = E["wv1"]
    bqc = E["bqc"]
    bkc = E["bkc"]
    bvr = E["bvr"]
    inds = E["inds"]
    ones1 = E["ones1"]
    onesc = E["onesc"]
    idf = E["idf"]

    x2_out = E["x2_out"]
    q_out = E["q_out"]
    k_out = E["k_out"]
    v_out = E["v_out"]
    x1_scr = E["x1_scr"]

    sOne1 = _sb(nc, stack, "sOne1", (1, 128), F32)
    sOneC = _sb(nc, stack, "sOneC", (128, 1), F32)
    sInds = _sb(nc, stack, "sInds", (8, 128), F32)
    sIdf = _sb(nc, stack, "sIdf", (128, 128), F32)
    sEps1 = _sb(nc, stack, "sEps1", (1, 1), F32)
    sRb = _sb(nc, stack, "sRb", (1, 8), F32)
    sBvr = _sb(nc, stack, "sBvr", (1, D), F32)
    sRcB = _sb(nc, stack, "sRcB", (128, 8), F32)
    sBvB = _sb(nc, stack, "sBvB", (128, 512), F32)
    sBqs = _sb(nc, stack, "sBqs", (128, 16), F32)
    sBkc = _sb(nc, stack, "sBkc", (128, 16), F32)
    sBbc = _sb(nc, stack, "sBbc", (128, 16), F32)
    sRwT = _sb(nc, stack, "sRwT", (128, 16 * 8), BF16)
    sLA = _sb(nc, stack, "sLA", (128, 16 * 128), BF16)
    sLB = _sb(nc, stack, "sLB", (128, 16 * 128), BF16)
    wsl = [_sb(nc, stack, f"wsl{i}", (128, 16 * 128), BF16) for i in range(2)]
    bsl = [_sb(nc, stack, f"bsl{i}", (128, 16 * 512), BF16) for i in range(2)]
    vsl = bsl
    osl = bsl
    x1sl = bsl
    xsl = [_sb(nc, stack, f"xsl{i}", (128, 512), F32) for i in range(4)]
    xrb = _sb(nc, stack, "xrb", (128, 16 * 512), F32)
    sqs = [_sb(nc, stack, f"sqs{i}", (128, 512), F32) for i in range(2)]
    scA = [_sb(nc, stack, f"scA{i}", (128, 512), F32) for i in range(2)]
    scB = [_sb(nc, stack, f"scB{i}", (128, 512), F32) for i in range(2)]
    ebf = [_sb(nc, stack, f"ebf{i}", (128, 512), BF16) for i in range(2)]
    x2b = _sb(nc, stack, "x2b", (128, 16 * 2048), BF16)
    rowA = _sb(nc, stack, "rowA", (1, 512), F32)
    rowB = _sb(nc, stack, "rowB", (1, 512), F32)
    rowC = _sb(nc, stack, "rowC", (1, 512), F32)
    lg8 = _sb(nc, stack, "lg8", (128, 8), F32)
    pr8 = _sb(nc, stack, "pr8", (128, 8), F32)
    prb = _sb(nc, stack, "prb", (128, 8), F32)
    ind8 = _sb(nc, stack, "ind8", (128, 8), F32)
    wg8 = _sb(nc, stack, "wg8", (128, 8), F32)
    colA = _sb(nc, stack, "colA", (128, 1), F32)
    colB = _sb(nc, stack, "colB", (128, 1), F32)
    wT8 = _sb(nc, stack, "wT8", (8, 512), F32)
    gB = _sb(nc, stack, "gB", (128, 128), F32)
    midg = _sb(nc, stack, "midg", (128, 512), BF16)

    pool = Pool([_bank(nc, stack, "bkA"), _bank(nc, stack, "bkB")])
    bS1 = _bank(nc, stack, "bS1")
    bS2 = _bank(nc, stack, "bS2")
    bB1 = _bank(nc, stack, "bB1")
    bB2 = _bank(nc, stack, "bB2")
    bSm = _bank(nc, stack, "bSm")
    bSm2 = _bank(nc, stack, "bSm2")

    for buf, src_ in ((sOne1, ones1), (sOneC, onesc), (sInds, inds), (sIdf, idf),
                      (sBqs, bqc), (sBkc, bkc), (sBbc, basebc), (sRb, rbrow), (sBvr, bvr)):
        p.dma_serial("sync", buf.ap[:], src_, writes=[buf], sem="q_cst")
    p.dma_serial("gpsimd", sRwT.ap[:].rearrange("p (c o) -> p c o", o=8),
                 rwT.rearrange("(c p) o -> p c o", p=128), writes=[sRwT], sem="q_wld")
    p.dma_serial("gpsimd", sLA.ap[:].rearrange("p (c o) -> p c o", o=128),
                 lA.rearrange("(c p) o -> p c o", p=128), writes=[sLA], sem="q_wld")
    p.dma_serial("gpsimd", sLB.ap[:].rearrange("p (c o) -> p c o", o=128),
                 lB.rearrange("(c p) o -> p c o", p=128), writes=[sLB], sem="q_wld")
    p.op("vector", lambda e: e.memset(sEps1.ap[:], 1e-5), writes=[sEps1])
    p.op("vector", lambda e: e.tensor_scalar(sBqs.ap[:], sBqs.ap[:], SC, None, ALU.mult, ALU.bypass),
         reads=[sBqs], writes=[sBqs])
    p.op("tensor", lambda e: e.matmul(bSm.ap[:, 0:8], sOne1.ap[:], sRb.ap[:], start=True, stop=True),
         reads=[sOne1, sRb], writes=[bSm])
    p.op("vector", lambda e: e.tensor_copy(sRcB.ap[:], bSm.ap[:, 0:8]), reads=[bSm], writes=[sRcB])

    def wstream(idx, wdram, oc, semn="q_w"):
        sl = wsl[idx % 2]
        p.dma("gpsimd", sl.ap[:].rearrange("p (c o) -> p c o", o=128),
              wdram[:, oc * 128:(oc + 1) * 128].rearrange("(c p) o -> p c o", p=128),
              writes=[sl], sem=f"{semn}{idx % 2}")
        return sl

    def ln_stats():
        p.op("vector", lambda e: e.tensor_scalar(rowA.ap[:], bS1.ap[0:1, 0:512], 1.0 / D, None, ALU.mult, ALU.bypass),
             reads=[bS1], writes=[rowA])
        p.op("vector", lambda e: e.tensor_scalar(rowB.ap[:], bS2.ap[0:1, 0:512], 1.0 / D, None, ALU.mult, ALU.bypass),
             reads=[bS2], writes=[rowB])
        p.op("vector", lambda e: e.tensor_tensor(rowC.ap[:], rowA.ap[:], rowA.ap[:], ALU.mult),
             reads=[rowA], writes=[rowC])
        p.op("vector", lambda e: e.tensor_tensor(rowB.ap[:], rowB.ap[:], rowC.ap[:], ALU.subtract),
             reads=[rowB, rowC], writes=[rowB])
        p.op("scalar", lambda e: e.activation(rowB.ap[:], rowB.ap[:], AF.Ln, bias=sEps1.ap[:]),
             reads=[rowB, sEps1], writes=[rowB])
        p.op("scalar", lambda e: e.activation(rowB.ap[:], rowB.ap[:], AF.Exp, scale=-0.5),
             reads=[rowB], writes=[rowB])
        p.op("vector", lambda e: e.tensor_tensor(rowC.ap[:], rowA.ap[:], rowB.ap[:], ALU.mult),
             reads=[rowA, rowB], writes=[rowC])
        p.op("tensor", lambda e: e.matmul(bB1.ap[:, 0:512], sOne1.ap[:], rowB.ap[:], start=True, stop=True),
             reads=[sOne1, rowB], writes=[bB1])
        p.op("tensor", lambda e: e.matmul(bB2.ap[:, 0:512], sOne1.ap[:], rowC.ap[:], start=True, stop=True),
             reads=[sOne1, rowC], writes=[bB2])

    def resid_stats(bk, oc, xt, bias_col):
        xr_ap = xrb.ap[:, oc * 512:(oc + 1) * 512]
        if bias_col is None:
            p.op("vector", lambda e, bk=bk, xt=xt, xr_ap=xr_ap: e.tensor_tensor(
                xr_ap, bk.ap[:, 0:512], xt.ap[:], ALU.add), reads=[bk, xt], writes=[xrb])
        else:
            p.op("vector", lambda e, bk=bk, xt=xt, xr_ap=xr_ap, bc=bias_col: e.scalar_tensor_tensor(
                xr_ap, bk.ap[:, 0:512], bc, xt.ap[:], ALU.add, ALU.add),
                reads=[bk, xt, sBbc], writes=[xrb])
        sq = sqs[oc % 2]
        p.op("scalar", lambda e, xr_ap=xr_ap, sq=sq: e.activation(sq.ap[:], xr_ap, AF.Square),
             reads=[xrb], writes=[sq])
        p.accum_step(bS1, bS1.ap[0:1, 0:512], sOneC.ap[:], xr_ap, oc == 0, oc == 15, reads=[sOneC, xrb])
        p.accum_step(bS2, bS2.ap[0:1, 0:512], sOneC.ap[:], sq.ap[:], oc == 0, oc == 15, reads=[sOneC, sq])

    def normalize_out(tt, write_cb):
        for oc in range(16):
            xr_ap = xrb.ap[:, oc * 512:(oc + 1) * 512]
            sa = scA[oc % 2]
            p.op("vector", lambda e, xr_ap=xr_ap, sa=sa: e.tensor_tensor(
                sa.ap[:], xr_ap, bB1.ap[:, 0:512], ALU.mult), reads=[xrb, bB1], writes=[sa])
            sb_ = scB[oc % 2]
            p.op("vector", lambda e, sa=sa, sb_=sb_: e.tensor_tensor(
                sb_.ap[:], sa.ap[:], bB2.ap[:, 0:512], ALU.subtract), reads=[sa, bB2], writes=[sb_])
            write_cb(oc, sb_)

    for tt in range(4):
        ot = osl[tt % 2]
        p.dma("gpsimd", ot.ap[:].rearrange("p (c t) -> p c t", t=512),
              o_sl[:, tt * 512:(tt + 1) * 512].rearrange("(c p) t -> p c t", p=128),
              writes=[ot], sem=f"q_o{tt % 2}")
        for oc in range(16):
            wt = wstream(tt * 16 + oc, wo0, oc)
            bk = pool.take()
            p.mmgroup(bk, bk.ap[:, 0:512],
                      [(wt.ap[:, kc * 128:(kc + 1) * 128], ot.ap[:, kc * 512:(kc + 1) * 512])
                       for kc in range(16)], reads=[wt, ot])
            xt = xsl[oc % 4]
            p.dma("sync", xt.ap[:], x_sl[oc * 128:(oc + 1) * 128, tt * 512:(tt + 1) * 512],
                  writes=[xt], sem=f"q_xf{oc % 4}")
            resid_stats(bk, oc, xt, None)
        ln_stats()

        def wr1(oc, sb_, tt=tt):
            p.dma("sync", x1_scr[oc * 128:(oc + 1) * 128, tt * 512:(tt + 1) * 512], sb_.ap[:],
                  reads=[sb_], sem=f"q_x1w{oc % 2}")
        normalize_out(tt, wr1)

    for eng in ("gpsimd", "sync"):
        p.wait(eng, "q_x1w0", p.sem("q_x1w0")[1])
        p.wait(eng, "q_x1w1", p.sem("q_x1w1")[1])
    for tt in range(4):
        xt1 = x1sl[tt % 2]
        p.dma("gpsimd", xt1.ap[:].rearrange("p (c t) -> p c t", t=512),
              x1_scr[:, tt * 512:(tt + 1) * 512].rearrange("(c p) t -> p c t", p=128),
              writes=[xt1], sem=f"q_x1r{tt % 2}")
        for qb in range(4):
            qsl = slice(qb * 128, (qb + 1) * 128)
            bk = pool.take()
            p.mmgroup(bk, bk.ap[:, 0:8],
                      [(xt1.ap[:, kc * 512 + qb * 128:kc * 512 + (qb + 1) * 128],
                        sRwT.ap[:, kc * 8:(kc + 1) * 8]) for kc in range(16)], reads=[xt1, sRwT])
            p.op("vector", lambda e, bk=bk: e.tensor_tensor(lg8.ap[:], bk.ap[:, 0:8], sRcB.ap[:], ALU.add),
                 reads=[bk, sRcB], writes=[lg8])
            p.op("vector", lambda e: e.reduce_max(colA.ap[:], lg8.ap[:], axis=AX, negate=True),
                 reads=[lg8], writes=[colA])
            p.op("scalar", lambda e: e.activation(pr8.ap[:], lg8.ap[:], AF.Exp,
                                                  bias=colA.ap[:], accum_out=colB.ap[:]),
                 reads=[lg8, colA], writes=[pr8, colB])
            p.op("vector", lambda e: e.reciprocal(colB.ap[:], colB.ap[:]), reads=[colB], writes=[colB])
            p.op("vector", lambda e: e.tensor_scalar(prb.ap[:], pr8.ap[:], colB.ap[:], None, ALU.mult, ALU.bypass),
                 reads=[pr8, colB], writes=[prb])
            p.op("vector", lambda e: e.tensor_scalar(ind8.ap[:], prb.ap[:], colB.ap[:], -1e30, ALU.is_ge, ALU.mult),
                 reads=[prb, colB], writes=[ind8])
            p.op("vector", lambda e: e.tensor_tensor(ind8.ap[:], ind8.ap[:], prb.ap[:], ALU.add),
                 reads=[ind8, prb], writes=[ind8])
            p.op("vector", lambda e: e.reduce_max(colA.ap[:], ind8.ap[:], axis=AX),
                 reads=[ind8], writes=[colA])
            p.op("vector", lambda e: e.tensor_scalar(ind8.ap[:], prb.ap[:], colA.ap[:], None, ALU.is_ge, ALU.bypass),
                 reads=[prb, colA], writes=[ind8])
            p.op("vector", lambda e: e.tensor_tensor(colA.ap[:], colA.ap[:], colB.ap[:], ALU.add),
                 reads=[colA, colB], writes=[colA])
            p.op("vector", lambda e: e.reciprocal(colA.ap[:], colA.ap[:]), reads=[colA], writes=[colA])
            p.op("vector", lambda e: e.scalar_tensor_tensor(
                wg8.ap[:], prb.ap[:], colA.ap[:], ind8.ap[:], ALU.mult, ALU.mult),
                reads=[prb, colA, ind8], writes=[wg8])
            p.op("tensor", lambda e: e.transpose(bSm.ap[0:8, 0:128], wg8.ap[:], sIdf.ap[:]),
                 reads=[wg8, sIdf], writes=[bSm])
            p.op("vector", lambda e, qsl=qsl: e.tensor_copy(wT8.ap[:, qsl], bSm.ap[0:8, 0:128]),
                 reads=[bSm], writes=[wT8])
            p.op("tensor", lambda e, qsl=qsl: e.matmul(bSm2.ap[:, 0:128], sInds.ap[:], wT8.ap[:, qsl],
                                                       start=True, stop=True),
                 reads=[sInds, wT8], writes=[bSm2])
            p.op("vector", lambda e: e.tensor_copy(gB.ap[:], bSm2.ap[:, 0:128]), reads=[bSm2], writes=[gB])
            bk2 = pool.take()
            p.mmgroup(bk2, bk2.ap[:, 0:128],
                      [(sLA.ap[:, kc * 128:(kc + 1) * 128],
                        xt1.ap[:, kc * 512 + qb * 128:kc * 512 + (qb + 1) * 128]) for kc in range(16)],
                      reads=[sLA, xt1])
            p.op("vector", lambda e, bk2=bk2, qsl=qsl: e.tensor_tensor(
                midg.ap[:, qsl], bk2.ap[:, 0:128], gB.ap[:], ALU.mult), reads=[bk2, gB], writes=[midg])
        for oc in range(16):
            wt = wstream(tt * 16 + oc, basewT, oc)
            bk = pool.take()
            pairs = [(wt.ap[:, kc * 128:(kc + 1) * 128], xt1.ap[:, kc * 512:(kc + 1) * 512])
                     for kc in range(16)]
            pairs += [(sLB.ap[:, oc * 128:(oc + 1) * 128], midg.ap[:, qb * 128:(qb + 1) * 128])
                      for qb in range(4)]
            p._pre("tensor", [wt, xt1, sLB, midg], [bk])
            cse = p.sem("e_tensor")
            for kc in range(16):
                p.raw("tensor", lambda e, w=wt.ap[:, kc * 128:(kc + 1) * 128],
                      r=xt1.ap[:, kc * 512:(kc + 1) * 512], o=bk.ap[:, 0:512], s=(kc == 0):
                      e.matmul(o, w, r, start=s, stop=False))
            for qb in range(4):
                last = qb == 3
                if last:
                    cse[1] += 1
                    p.raw("tensor", lambda e, w=sLB.ap[:, oc * 128:(oc + 1) * 128],
                          r=midg.ap[:, qb * 128:(qb + 1) * 128],
                          o=bk.ap[:, qb * 128:(qb + 1) * 128], sem=cse[0]:
                          e.matmul(o, w, r, start=False, stop=True).then_inc(sem, 1))
                else:
                    p.raw("tensor", lambda e, w=sLB.ap[:, oc * 128:(oc + 1) * 128],
                          r=midg.ap[:, qb * 128:(qb + 1) * 128],
                          o=bk.ap[:, qb * 128:(qb + 1) * 128]:
                          e.matmul(o, w, r, start=False, stop=False))
            tok = ("e_tensor", cse[1])
            for b_ in (wt, xt1, sLB, midg):
                b_.ctoks[tok[0]] = max(b_.ctoks.get(tok[0], 0), tok[1])
            bk.ptok = tok
            bk.ctoks = {}
            xt = xsl[oc % 4]
            p.dma("sync", xt.ap[:], x1_scr[oc * 128:(oc + 1) * 128, tt * 512:(tt + 1) * 512],
                  writes=[xt], sem=f"q_xf{oc % 4}")
            resid_stats(bk, oc, xt, sBbc.ap[:, oc:oc + 1])
        ln_stats()

        def wr2(oc, sb_, tt=tt):
            p.dma("sync", x2_out[oc * 128:(oc + 1) * 128, tt * 512:(tt + 1) * 512], sb_.ap[:],
                  reads=[sb_], sem=f"q_x2w{oc % 2}")
            p.op("vector", lambda e, oc=oc, tt=tt, sb_=sb_: e.tensor_copy(
                x2b.ap[:, oc * 2048 + tt * 512:oc * 2048 + (tt + 1) * 512], sb_.ap[:]),
                reads=[sb_], writes=[x2b])
        normalize_out(tt, wr2)

    _B3V_MARK = None
    for wdram, bias_sb, out_dram, scale in ((wq1, sBqs, q_out, SC), (wk1, sBkc, k_out, 1.0)):
        for oc in range(16):
            wt = wstream(oc, wdram, oc, semn="q_w")
            for tt in range(4):
                bk = pool.take()
                p.mmgroup(bk, bk.ap[:, 0:512],
                          [(wt.ap[:, kc * 128:(kc + 1) * 128],
                            x2b.ap[:, kc * 2048 + tt * 512:kc * 2048 + (tt + 1) * 512])
                           for kc in range(16)], reads=[wt, x2b])
                eb = ebf[tt % 2]
                p.op("scalar", lambda e, bk=bk, eb=eb, bias_sb=bias_sb, oc=oc, sc=scale: e.activation(
                    eb.ap[:], bk.ap[:, 0:512], AF.Identity, bias=bias_sb.ap[:, oc:oc + 1], scale=sc),
                    reads=[bk, bias_sb], writes=[eb])
                p.dma("scalar", out_dram[oc * 128:(oc + 1) * 128, tt * 512:(tt + 1) * 512],
                      eb.ap[:], reads=[eb], sem=f"q_qk{tt % 2}")
    if mid_cb is not None:
        mid_cb()
    for vg in range(4):
        vt = vsl[vg % 2]
        p.dma("gpsimd", vt.ap[:].rearrange("p (c o) -> p c o", o=512),
              wv1[:, vg * 512:(vg + 1) * 512].rearrange("(c p) o -> p c o", p=128),
              writes=[vt], sem=f"q_wv{vg % 2}")
        p.op("tensor", lambda e, vg=vg: e.matmul(bSm.ap[:, 0:512], sOne1.ap[:],
                                                 sBvr.ap[:, vg * 512:(vg + 1) * 512], start=True, stop=True),
             reads=[sOne1, sBvr], writes=[bSm])
        p.op("vector", lambda e: e.tensor_copy(sBvB.ap[:], bSm.ap[:, 0:512]), reads=[bSm], writes=[sBvB])
        for tb in range(16):
            bk = pool.take()
            p.mmgroup(bk, bk.ap[:, 0:512],
                      [(x2b.ap[:, kc * 2048 + tb * 128:kc * 2048 + (tb + 1) * 128],
                        vt.ap[:, kc * 512:(kc + 1) * 512]) for kc in range(16)], reads=[x2b, vt])
            eb = ebf[tb % 2]
            p.op("vector", lambda e, bk=bk, eb=eb: e.tensor_tensor(
                eb.ap[:], bk.ap[:, 0:512], sBvB.ap[:], ALU.add), reads=[bk, sBvB], writes=[eb])
            p.dma("scalar", v_out[tb * 128:(tb + 1) * 128, vg * 512:(vg + 1) * 512],
                  eb.ap[:], reads=[eb], sem=f"q_v{tb % 2}")

    for sn in ("q_x2w0", "q_x2w1", "q_qk0", "q_qk1", "q_v0", "q_v1"):
        p.wait("gpsimd", sn, p.sem(sn)[1])
    p.raw("gpsimd", lambda e: e.engine_nop())



def stage_b_inmaps(inputs, o_fm):
    x = inputs["L0_attn_hidden_states"]
    xT = np.ascontiguousarray(x.transpose(0, 2, 1))
    ind = np.zeros((8, 128), np.float32)
    for e in range(8):
        ind[e, e * 16:(e + 1) * 16] = 2.0
    consts = dict(
        inds=ind, ones1=np.ones((1, 128), np.float32), onesc=np.ones((128, 1), np.float32),
        idf=np.eye(128, dtype=np.float32),
    )
    shared = dict(
        wo0=inputs["L0_attn_Wo"],
        basewT=np.ascontiguousarray(inputs["L0_ffn_base_w"].T),
        basebc=_bcols(inputs["L0_ffn_base_b"]),
        rwT=np.ascontiguousarray(inputs["L0_ffn_router_w"].T),
        rbrow=inputs["L0_ffn_router_b"].reshape(1, 8),
        lA=np.ascontiguousarray(inputs["L0_ffn_lora_A"].transpose(1, 0, 2).reshape(2048, 128)),
        lB=np.ascontiguousarray(inputs["L0_ffn_lora_B"].reshape(128, 2048)),
        wq1=inputs["L1_attn_wq"], wk1=inputs["L1_attn_wk"], wv1=inputs["L1_attn_wv"],
        bqc=_bcols(inputs["L1_attn_bq"]), bkc=_bcols(inputs["L1_attn_bk"]),
        bvr=inputs["L1_attn_bv"].reshape(1, 2048),
        **consts,
    )
    ims = []
    for c in range(8):
        b, g = c // 4, c % 4
        ims.append(dict(
            o_sl=np.ascontiguousarray(o_fm[b][:, g * 2048:(g + 1) * 2048]),
            x_sl=np.ascontiguousarray(xT[b][:, g * 2048:(g + 1) * 2048]),
            **shared,
        ))
    return ims



def emit_stage_c(nc, p, stack, E):
    D = 2048

    q_sl = E["q_sl"]
    kf_parts = E["kf_parts"]
    vt_parts = E["vt_parts"]
    x2_sl = E["x2_sl"]
    wo1 = E["wo1"]
    boc = E["boc"]
    ones1 = E["ones1"]
    onesc = E["onesc"]
    onescb = E["onescb"]
    idbf = E["idbf"]

    x3_out = E["x3_out"]
    o1_scr = E["o1_scr"]

    sOne1 = _sb(nc, stack, "sOne1", (1, 128), F32)
    sOneC = _sb(nc, stack, "sOneC", (128, 1), F32)
    sOneCb = _sb(nc, stack, "sOneCb", (128, 1), BF16)
    sIdb = _sb(nc, stack, "sIdb", (128, 128), BF16)
    sEps1 = _sb(nc, stack, "sEps1", (1, 1), F32)
    sBoc = _sb(nc, stack, "sBoc", (128, 16), F32)
    khs = [_sb(nc, stack, f"khs{i}", (128, 8192), BF16) for i in range(2)]
    vhs = [_sb(nc, stack, f"vhs{i}", (128, 64 * 129), BF16) for i in range(2)]
    qhs = [_sb(nc, stack, f"qhs{i}", (128, 2048), BF16) for i in range(2)]
    sqq = _sb(nc, stack, "sqq", (128, 512), BF16)
    qn = _sb(nc, stack, "qn", (1, 2048), F32)
    kmx = _sb(nc, stack, "kmx", (1, 1), F32)
    kmt = _sb(nc, stack, "kmt", (1, 1), F32)
    brow = _sb(nc, stack, "brow", (1, 512), F32)
    nbg = _sb(nc, stack, "nbg", (1, 1), F32)
    nbc4 = _sb(nc, stack, "nbc4", (128, 4), F32)
    pts = [_sb(nc, stack, f"pts{i}", (128, 1024), BF16) for i in range(2)]
    rn = _sb(nc, stack, "rn", (128, 1), F32)
    obf = _sb(nc, stack, "obf", (128, 128), BF16)
    otr = [_sb(nc, stack, f"otr{i}", (128, 128), BF16) for i in range(2)]
    wsl = [_sb(nc, stack, f"wsl{i}", (128, 16 * 128), BF16) for i in range(2)]
    osl = [_sb(nc, stack, f"osl{i}", (128, 16 * 512), BF16) for i in range(2)]
    xsl = [_sb(nc, stack, f"xsl{i}", (128, 512), F32) for i in range(4)]
    xrb = _sb(nc, stack, "xrb", (128, 16 * 512), F32)
    sqs = [_sb(nc, stack, f"sqs{i}", (128, 512), F32) for i in range(2)]
    scA = [_sb(nc, stack, f"scA{i}", (128, 512), F32) for i in range(2)]
    scB = [_sb(nc, stack, f"scB{i}", (128, 512), F32) for i in range(2)]
    rowA = _sb(nc, stack, "rowA", (1, 512), F32)
    rowB = _sb(nc, stack, "rowB", (1, 512), F32)
    rowC = _sb(nc, stack, "rowC", (1, 512), F32)

    bSp = [_bank(nc, stack, f"bkS{i}", F32, cols=1024) for i in range(2)]
    oB = [_bank(nc, stack, f"bkO{i}") for i in range(4)]
    bM = bSp[0]
    bM2 = bSp[1]

    for buf, src_ in ((sOne1, ones1), (sOneC, onesc), (sOneCb, onescb), (sIdb, idbf), (sBoc, boc)):
        p.dma_serial("sync", buf.ap[:], src_, writes=[buf], sem="q_cst")
    p.op("vector", lambda e: e.memset(sEps1.ap[:], 1e-5), writes=[sEps1])

    for h in range(16):
        kh = khs[h % 2]
        vh = vhs[h % 2]
        qh = qhs[h % 2]
        for r in range(4):
            p.dma("gpsimd", kh.ap[:, r * 2048:(r + 1) * 2048],
                  kf_parts[r][h * 128:(h + 1) * 128, :], writes=[kh], sem=f"q_kh{h % 2}")
        p.op("vector", lambda e, vh=vh: e.memset(vh.ap[:], 1.0), writes=[vh])
        for r in range(4):
            p.dma("gpsimd",
                  vh.ap[:, r * 16 * 129:(r + 1) * 16 * 129].rearrange("p (k w) -> p k w", w=129)[:, :, 0:128],
                  vt_parts[r][:, h * 128:(h + 1) * 128].rearrange("(k p) v -> p k v", p=128),
                  writes=[vh], sem=f"q_vh{h % 2}")
        p.dma("sync", qh.ap[:], q_sl[h * 128:(h + 1) * 128, :], writes=[qh], sem=f"q_qh{h % 2}")
        for i in range(4):
            p.op("scalar", lambda e, qh=qh, i=i: e.activation(
                sqq.ap[:], qh.ap[:, i * 512:(i + 1) * 512], AF.Square), reads=[qh], writes=[sqq])
            p.op("tensor", lambda e, i=i: e.matmul(bM.ap[0:1, 0:512], sOneCb.ap[:], sqq.ap[:],
                                                   start=True, stop=True),
                 reads=[sOneCb, sqq], writes=[bM])
            p.op("scalar", lambda e, i=i: e.activation(qn.ap[0:1, i * 512:(i + 1) * 512],
                                                       bM.ap[0:1, 0:512], AF.Ln),
                 reads=[bM], writes=[qn])
        for i in range(16):
            p.op("scalar", lambda e, kh=kh, i=i: e.activation(
                sqq.ap[:], kh.ap[:, i * 512:(i + 1) * 512], AF.Square), reads=[kh], writes=[sqq])
            p.op("tensor", lambda e: e.matmul(bM.ap[0:1, 0:512], sOneCb.ap[:], sqq.ap[:],
                                              start=True, stop=True),
                 reads=[sOneCb, sqq], writes=[bM])
            p.op("vector", lambda e: e.reduce_max(kmt.ap[:], bM.ap[0:1, 0:512], axis=AX),
                 reads=[bM], writes=[kmt])
            if i == 0:
                p.op("vector", lambda e: e.tensor_copy(kmx.ap[:], kmt.ap[:]), reads=[kmt], writes=[kmx])
            else:
                p.op("vector", lambda e: e.tensor_tensor(kmx.ap[:], kmx.ap[:], kmt.ap[:], ALU.max),
                     reads=[kmx, kmt], writes=[kmx])
        p.op("scalar", lambda e: e.activation(kmx.ap[:], kmx.ap[:], AF.Ln), reads=[kmx], writes=[kmx])

        if "dmp_qn" in E:
            p.dma("scalar", E["dmp_qn"][h:h + 1, :], qn.ap[:], reads=[qn], sem="q_dbg2")
            p.dma("scalar", E["dmp_km"][h:h + 1, :], kmx.ap[:], reads=[kmx], sem="q_dbg2")
        for qg in range(4):
            qsl = slice(qg * 512, (qg + 1) * 512)
            p.op("vector", lambda e, qsl=qsl: e.tensor_scalar(
                brow.ap[:], qn.ap[0:1, qsl], kmx.ap[:], None, ALU.add, ALU.bypass),
                reads=[qn, kmx], writes=[brow])
            p.op("vector", lambda e: e.reduce_max(nbg.ap[:], brow.ap[:], axis=AX),
                 reads=[brow], writes=[nbg])
            p.op("scalar", lambda e: e.activation(nbg.ap[:], nbg.ap[:], AF.Exp, scale=0.5),
                 reads=[nbg], writes=[nbg])
            p.op("vector", lambda e, qg=qg: e.tensor_scalar(
                nbg.ap[:], nbg.ap[:], -1.0, None, ALU.mult, ALU.bypass), reads=[nbg], writes=[nbg])
            p.op("tensor", lambda e, qg=qg: e.matmul(bM.ap[:, qg:qg + 1], sOne1.ap[:], nbg.ap[:],
                                                     start=True, stop=True),
                 reads=[sOne1, nbg], writes=[bM])
        p.op("vector", lambda e: e.tensor_copy(nbc4.ap[:], bM.ap[:, 0:4]), reads=[bM], writes=[nbc4])
        for qg in range(4):
            qsl = slice(qg * 512, (qg + 1) * 512)
            prev = None
            for ktp in range(32):
                sb_ = bSp[ktp % 2]
                p._pre("tensor", [kh, qh], [sb_])
                cse = p.sem("e_tensor")
                p.raw("tensor", lambda e, kh=kh, qh=qh, ktp=ktp, qsl=qsl, sb_=sb_: e.matmul(
                    sb_.ap[:, 0:512], kh.ap[:, (2 * ktp) * 128:(2 * ktp + 1) * 128], qh.ap[:, qsl],
                    start=True, stop=True))
                cse[1] += 1
                p.raw("tensor", lambda e, kh=kh, qh=qh, ktp=ktp, qsl=qsl, sb_=sb_, sem=cse[0]: e.matmul(
                    sb_.ap[:, 512:1024], kh.ap[:, (2 * ktp + 1) * 128:(2 * ktp + 2) * 128], qh.ap[:, qsl],
                    start=True, stop=True).then_inc(sem, 1))
                tok = ("e_tensor", cse[1])
                kh.ctoks[tok[0]] = max(kh.ctoks.get(tok[0], 0), tok[1])
                qh.ctoks[tok[0]] = max(qh.ctoks.get(tok[0], 0), tok[1])
                sb_.ptok = tok
                sb_.ctoks = {}
                pt = pts[ktp % 2]
                p.op("scalar", lambda e, sb_=sb_, pt=pt, qg=qg: e.activation(
                    pt.ap[:], sb_.ap[:, 0:1024], AF.Exp, bias=nbc4.ap[:, qg:qg + 1]),
                    reads=[sb_, nbc4], writes=[pt])
                if prev is not None:
                    pk, ppt = prev
                    for i in range(4):
                        for j in range(2):
                            kt = 2 * pk + j
                            p.accum_step(oB[i], oB[i].ap[:, 0:129],
                                         ppt.ap[:, j * 512 + i * 128:j * 512 + (i + 1) * 128],
                                         vh.ap[:, kt * 129:(kt + 1) * 129],
                                         kt == 0, False, reads=[ppt, vh])
                prev = (ktp, pt)
            pk, ppt = prev
            for i in range(4):
                for j in range(2):
                    kt = 2 * pk + j
                    p.accum_step(oB[i], oB[i].ap[:, 0:129],
                                 ppt.ap[:, j * 512 + i * 128:j * 512 + (i + 1) * 128],
                                 vh.ap[:, kt * 129:(kt + 1) * 129],
                                 False, kt == 63, reads=[ppt, vh])
            for i in range(4):
                p.op("vector", lambda e, i=i: e.reciprocal(rn.ap[:], oB[i].ap[:, 128:129]),
                     reads=[oB[i]], writes=[rn])
                p.op("vector", lambda e, i=i: e.tensor_scalar(
                    obf.ap[:], oB[i].ap[:, 0:128], rn.ap[:], None, ALU.mult, ALU.bypass),
                    reads=[oB[i], rn], writes=[obf])
                p.op("tensor", lambda e: e.transpose(bM2.ap[:, 0:64].bitcast(BF16), obf.ap[:], sIdb.ap[:]),
                     reads=[obf, sIdb], writes=[bM2])
                ot = otr[i % 2]
                p.op("vector", lambda e, ot=ot: e.tensor_copy(ot.ap[:], bM2.ap[:, 0:64].bitcast(BF16)),
                     reads=[bM2], writes=[ot])
                p.dma("scalar", o1_scr[h * 128:(h + 1) * 128, qg * 512 + i * 128:qg * 512 + (i + 1) * 128],
                      ot.ap[:], reads=[ot], sem=f"q_o1{i % 2}")

    for eng in ("gpsimd", "sync"):
        p.wait(eng, "q_o10", p.sem("q_o10")[1])
        p.wait(eng, "q_o11", p.sem("q_o11")[1])
    bS1, bS2 = oB[0], oB[1]
    bB1, bB2 = oB[2], oB[3]
    pool = Pool([bSp[0], bSp[1]])

    def ln_stats():
        p.op("vector", lambda e: e.tensor_scalar(rowA.ap[:], bS1.ap[0:1, 0:512], 1.0 / D, None, ALU.mult, ALU.bypass),
             reads=[bS1], writes=[rowA])
        p.op("vector", lambda e: e.tensor_scalar(rowB.ap[:], bS2.ap[0:1, 0:512], 1.0 / D, None, ALU.mult, ALU.bypass),
             reads=[bS2], writes=[rowB])
        p.op("vector", lambda e: e.tensor_tensor(rowC.ap[:], rowA.ap[:], rowA.ap[:], ALU.mult),
             reads=[rowA], writes=[rowC])
        p.op("vector", lambda e: e.tensor_tensor(rowB.ap[:], rowB.ap[:], rowC.ap[:], ALU.subtract),
             reads=[rowB, rowC], writes=[rowB])
        p.op("scalar", lambda e: e.activation(rowB.ap[:], rowB.ap[:], AF.Ln, bias=sEps1.ap[:]),
             reads=[rowB, sEps1], writes=[rowB])
        p.op("scalar", lambda e: e.activation(rowB.ap[:], rowB.ap[:], AF.Exp, scale=-0.5),
             reads=[rowB], writes=[rowB])
        p.op("vector", lambda e: e.tensor_tensor(rowC.ap[:], rowA.ap[:], rowB.ap[:], ALU.mult),
             reads=[rowA, rowB], writes=[rowC])
        p.op("tensor", lambda e: e.matmul(bB1.ap[:, 0:512], sOne1.ap[:], rowB.ap[:], start=True, stop=True),
             reads=[sOne1, rowB], writes=[bB1])
        p.op("tensor", lambda e: e.matmul(bB2.ap[:, 0:512], sOne1.ap[:], rowC.ap[:], start=True, stop=True),
             reads=[sOne1, rowC], writes=[bB2])

    for tt in range(4):
        ot = osl[tt % 2]
        p.dma("gpsimd", ot.ap[:].rearrange("p (c t) -> p c t", t=512),
              o1_scr[:, tt * 512:(tt + 1) * 512].rearrange("(c p) t -> p c t", p=128),
              writes=[ot], sem=f"q_or{tt % 2}")
        for oc in range(16):
            sl = wsl[oc % 2]
            p.dma("gpsimd", sl.ap[:].rearrange("p (c o) -> p c o", o=128),
                  wo1[:, oc * 128:(oc + 1) * 128].rearrange("(c p) o -> p c o", p=128),
                  writes=[sl], sem=f"q_w{oc % 2}")
            bk = pool.take()
            p.mmgroup(bk, bk.ap[:, 0:512],
                      [(sl.ap[:, kc * 128:(kc + 1) * 128], ot.ap[:, kc * 512:(kc + 1) * 512])
                       for kc in range(16)], reads=[sl, ot])
            xt = xsl[oc % 4]
            p.dma("sync", xt.ap[:], x2_sl[oc * 128:(oc + 1) * 128, tt * 512:(tt + 1) * 512],
                  writes=[xt], sem=f"q_xf{oc % 4}")
            xr_ap = xrb.ap[:, oc * 512:(oc + 1) * 512]
            p.op("vector", lambda e, bk=bk, xt=xt, xr_ap=xr_ap, oc=oc: e.scalar_tensor_tensor(
                xr_ap, bk.ap[:, 0:512], sBoc.ap[:, oc:oc + 1], xt.ap[:], ALU.add, ALU.add),
                reads=[bk, xt, sBoc], writes=[xrb])
            sq = sqs[oc % 2]
            p.op("scalar", lambda e, xr_ap=xr_ap, sq=sq: e.activation(sq.ap[:], xr_ap, AF.Square),
                 reads=[xrb], writes=[sq])
            p.accum_step(bS1, bS1.ap[0:1, 0:512], sOneC.ap[:], xr_ap, oc == 0, oc == 15, reads=[sOneC, xrb])
            p.accum_step(bS2, bS2.ap[0:1, 0:512], sOneC.ap[:], sq.ap[:], oc == 0, oc == 15, reads=[sOneC, sq])
        ln_stats()
        for oc in range(16):
            xr_ap = xrb.ap[:, oc * 512:(oc + 1) * 512]
            sa = scA[oc % 2]
            p.op("vector", lambda e, xr_ap=xr_ap, sa=sa: e.tensor_tensor(
                sa.ap[:], xr_ap, bB1.ap[:, 0:512], ALU.mult), reads=[xrb, bB1], writes=[sa])
            sb_ = scB[oc % 2]
            p.op("vector", lambda e, sa=sa, sb_=sb_: e.tensor_tensor(
                sb_.ap[:], sa.ap[:], bB2.ap[:, 0:512], ALU.subtract), reads=[sa, bB2], writes=[sb_])
            p.dma("sync", x3_out[oc * 128:(oc + 1) * 128, tt * 512:(tt + 1) * 512], sb_.ap[:],
                  reads=[sb_], sem=f"q_x3{oc % 2}")

    p.wait("gpsimd", "q_x30", p.sem("q_x30")[1])
    p.wait("gpsimd", "q_x31", p.sem("q_x31")[1])
    p.raw("gpsimd", lambda e: e.engine_nop())



def stage_c_inmaps(inputs, q_fm, k_fm, v_tm, x2_fm):
    consts = dict(
        ones1=np.ones((1, 128), np.float32), onesc=np.ones((128, 1), np.float32),
        onescb=np.ones((128, 1), dtype=BF), idbf=np.eye(128, dtype=BF),
    )
    shared = dict(wo1=inputs["L1_attn_wo"], boc=_bcols(inputs["L1_attn_bo"]), **consts)
    ims = []
    for c in range(8):
        b = c // 4
        ims.append(dict(
            q_sl=q_fm[c], x2_sl=x2_fm[c],
            kf=np.ascontiguousarray(k_fm[b]), vt=np.ascontiguousarray(v_tm[b]),
            **shared,
        ))
    return ims



HID = [1024, 2048, 3072, 4096, 1024, 2048, 3072, 4096]


def emit_stage_d(nc, p, stack, E):
    D = 2048
    L = 1024
    ST = 512
    NST = 2048 // ST

    x3_sl = E["x3_sl"]
    downw = E["downw"]
    downbc = E["downbc"]
    upw = E["upw"]
    corew = E["corew"]
    bsumc = E["bsumc"]
    upbc = E["upbc"]
    corebc = E["corebc"]
    rw1 = E["rw1"]
    rb1 = E["rb1"]
    costr = E["costr"]
    sw1 = E["sw1"]
    sb1c = E["sb1c"]
    sw2 = E["sw2"]
    sb2c = E["sb2c"]
    ew1 = [nc.dram_tensor(f"e{e}w1", [L, HID[e]], F32, kind="ExternalInput").ap() for e in range(8)]
    eb1 = [nc.dram_tensor(f"e{e}b1", [128, HID[e] // 128], F32, kind="ExternalInput").ap() for e in range(8)]
    ew2 = [nc.dram_tensor(f"e{e}w2", [HID[e], L], F32, kind="ExternalInput").ap() for e in range(8)]
    eb2 = [nc.dram_tensor(f"e{e}b2", [128, 8], F32, kind="ExternalInput").ap() for e in range(8)]
    sels = E["sels"]
    ones1 = E["ones1"]
    onesc = E["onesc"]
    idf = E["idf"]

    out_sl = E["out_sl"]

    sOne1 = _sb(nc, stack, "sOne1", (1, 128), F32)
    sOneC = _sb(nc, stack, "sOneC", (128, 1), F32)
    sIdf = _sb(nc, stack, "sIdf", (128, 128), F32)
    sEps1 = _sb(nc, stack, "sEps1", (1, 1), F32)
    sSel = _sb(nc, stack, "sSel", (8, 8 * 128), F32)
    sRb = _sb(nc, stack, "sRb", (1, 8), F32)
    sCost = _sb(nc, stack, "sCost", (1, 8), F32)
    sRcB = _sb(nc, stack, "sRcB", (128, 8), F32)
    sRwT = _sb(nc, stack, "sRwT", (128, 16 * 8), BF16)
    sDbc = _sb(nc, stack, "sDbc", (128, 8), F32)
    sSb1 = _sb(nc, stack, "sSb1", (128, 8), F32)
    sSb2 = _sb(nc, stack, "sSb2", (128, 8), F32)
    sBsum = _sb(nc, stack, "sBsum", (128, 16), F32)
    sEb1 = [_sb(nc, stack, f"sEb1_{e}", (128, HID[e] // 128), F32) for e in range(8)]
    sG1b = [_sb(nc, stack, f"sG1b_{e}", (128, HID[e] // 128), F32) for e in range(8)]
    sG1bB = [_sb(nc, stack, f"sG1bB_{e}", (128, HID[e] // 128), BF16) for e in range(8)]
    sEb2 = [_sb(nc, stack, f"sEb2_{e}", (128, 8), F32) for e in range(8)]
    sCcol = _sb(nc, stack, "sCcol", (128, 8), F32)
    sCtmp = _sb(nc, stack, "sCtmp", (128, 8), F32)
    xt3 = _sb(nc, stack, "xt3", (128, 16 * ST), F32)
    xb = _sb(nc, stack, "xb", (128, 16 * ST), BF16)
    g0b = _sb(nc, stack, "g0b", (128, 16 * ST), BF16)
    g1b = _sb(nc, stack, "g1b", (128, 8 * ST), BF16)
    w1s = [_sb(nc, stack, f"w1s{i}", (128, 8 * 1024), BF16) for i in range(2)]
    w2s = [_sb(nc, stack, "w2s0", (128, 8 * 1024), BF16)]
    gss = _sb(nc, stack, "gss", (128, 8 * ST), BF16)
    scG = [_sb(nc, stack, f"scG{i}", (128, ST), F32) for i in range(2)]
    sWBe = _sb(nc, stack, "sWBe", (128, ST), F32)
    macc = [_sb(nc, stack, "macc0", (128, 8 * ST), F32)]
    moeb = _sb(nc, stack, "moeb", (128, 8 * ST), BF16)
    lg8 = _sb(nc, stack, "lg8", (128, 8), F32)
    pr8 = _sb(nc, stack, "pr8", (128, 8), F32)
    prb = _sb(nc, stack, "prb", (128, 8), F32)
    ind8 = _sb(nc, stack, "ind8", (128, 8), F32)
    em8 = _sb(nc, stack, "em8", (128, 8), F32)
    wg8 = _sb(nc, stack, "wg8", (128, 8), F32)
    colA = _sb(nc, stack, "colA", (128, 1), F32)
    colB = _sb(nc, stack, "colB", (128, 1), F32)
    colC = _sb(nc, stack, "colC", (128, 1), F32)
    wT8 = _sb(nc, stack, "wT8", (8, ST), F32)
    xrb = _sb(nc, stack, "xrb", (128, 16 * ST), F32)
    gssF = gss.ap[:].bitcast(F32)
    rowA = _sb(nc, stack, "rowA", (1, ST), F32)
    rowB = _sb(nc, stack, "rowB", (1, ST), F32)
    rowC = _sb(nc, stack, "rowC", (1, ST), F32)

    pool = Pool([_bank(nc, stack, "bkA"), _bank(nc, stack, "bkB")])
    epool = Pool([_bank(nc, stack, "bkE0"), _bank(nc, stack, "bkE1")])
    bS1 = _bank(nc, stack, "bS1")
    bS2 = _bank(nc, stack, "bS2")
    bB1 = _bank(nc, stack, "bB1")
    bB2 = _bank(nc, stack, "bB2")

    for buf, src_ in ((sOne1, ones1), (sOneC, onesc), (sIdf, idf), (sSel, sels),
                      (sRb, rb1), (sCost, costr), (sDbc, downbc), (sSb1, sb1c),
                      (sSb2, sb2c), (sBsum, upbc)):
        p.dma_serial("sync", buf.ap[:], src_, writes=[buf], sem="q_cst")
    sCoreb = _sb(nc, stack, "sCoreb", (128, 16), F32)
    p.dma_serial("sync", sCoreb.ap[:], corebc, writes=[sCoreb], sem="q_cst")
    for e in range(8):
        p.dma_serial("sync", sEb1[e].ap[:], eb1[e], writes=[sEb1[e]], sem="q_cst")
        p.dma_serial("sync", sEb2[e].ap[:], eb2[e], writes=[sEb2[e]], sem="q_cst")
    p.dma_serial("gpsimd", sRwT.ap[:].rearrange("p (c o) -> p c o", o=8),
                 rw1.rearrange("(c p) o -> p c o", p=128), writes=[sRwT], sem="q_wld")
    p.op("vector", lambda e: e.memset(sEps1.ap[:], 1e-5), writes=[sEps1])
    p.op("vector", lambda e: e.tensor_tensor(sBsum.ap[:], sBsum.ap[:], sCoreb.ap[:], ALU.add),
         reads=[sBsum, sCoreb], writes=[sBsum])
    p.op("vector", lambda e: e.tensor_tensor(sRb.ap[:], sRb.ap[:], sCost.ap[:], ALU.add),
         reads=[sRb, sCost], writes=[sRb])
    p.op("tensor", lambda e: e.matmul(bB1.ap[:, 0:8], sOne1.ap[:], sRb.ap[:], start=True, stop=True),
         reads=[sOne1, sRb], writes=[bB1])
    p.op("vector", lambda e: e.tensor_copy(sRcB.ap[:], bB1.ap[:, 0:8]), reads=[bB1], writes=[sRcB])
    for e in range(8):
        p.op("scalar", lambda e_, e=e: e_.activation(sG1b[e].ap[:], sEb1[e].ap[:], AF.Gelu),
             reads=[sEb1[e]], writes=[sG1b[e]])
        p.op("vector", lambda e_, e=e: e_.tensor_copy(sG1bB[e].ap[:], sG1b[e].ap[:]),
             reads=[sG1b[e]], writes=[sG1bB[e]])
    p.op("vector", lambda e: e.tensor_scalar(sCcol.ap[:], sSb2.ap[:], 0.1, None, ALU.mult, ALU.bypass),
         reads=[sSb2], writes=[sCcol])
    for e in range(8):
        p.op("vector", lambda e_, e=e: e_.tensor_tensor(sCcol.ap[:], sCcol.ap[:], sEb2[e].ap[:], ALU.add),
             reads=[sCcol, sEb2[e]], writes=[sCcol])

    def router(qb, st):
        qsl = slice(qb * 128, (qb + 1) * 128)
        bk = pool.take()
        p.mmgroup(bk, bk.ap[:, 0:8],
                  [(xb.ap[:, kc * ST + qb * 128:kc * ST + (qb + 1) * 128],
                    sRwT.ap[:, kc * 8:(kc + 1) * 8]) for kc in range(16)], reads=[xb, sRwT])
        p.op("vector", lambda e, bk=bk: e.tensor_tensor(lg8.ap[:], bk.ap[:, 0:8], sRcB.ap[:], ALU.add),
             reads=[bk, sRcB], writes=[lg8])
        p.op("vector", lambda e: e.reduce_max(colA.ap[:], lg8.ap[:], axis=AX, negate=True),
             reads=[lg8], writes=[colA])
        p.op("scalar", lambda e: e.activation(pr8.ap[:], lg8.ap[:], AF.Exp,
                                              bias=colA.ap[:], accum_out=colB.ap[:]),
             reads=[lg8, colA], writes=[pr8, colB])
        p.op("vector", lambda e: e.reciprocal(colB.ap[:], colB.ap[:]), reads=[colB], writes=[colB])
        p.op("vector", lambda e: e.tensor_scalar(prb.ap[:], pr8.ap[:], colB.ap[:], None, ALU.mult, ALU.bypass),
             reads=[pr8, colB], writes=[prb])
        p.op("vector", lambda e: e.tensor_scalar(ind8.ap[:], prb.ap[:], colB.ap[:], -1e30, ALU.is_ge, ALU.mult),
             reads=[prb, colB], writes=[ind8])
        p.op("vector", lambda e: e.tensor_tensor(ind8.ap[:], ind8.ap[:], prb.ap[:], ALU.add),
             reads=[ind8, prb], writes=[ind8])
        p.op("vector", lambda e: e.reduce_max(colA.ap[:], ind8.ap[:], axis=AX),
             reads=[ind8], writes=[colA])
        p.op("vector", lambda e: e.tensor_scalar(ind8.ap[:], prb.ap[:], colA.ap[:], None, ALU.is_ge, ALU.bypass),
             reads=[prb, colA], writes=[ind8])
        p.op("scalar", lambda e: e.activation(em8.ap[:], prb.ap[:], AF.Exp), reads=[prb], writes=[em8])
        p.op("scalar", lambda e: e.activation(colC.ap[:], colA.ap[:], AF.Exp), reads=[colA], writes=[colC])
        p.op("scalar", lambda e: e.activation(colA.ap[:], colB.ap[:], AF.Exp), reads=[colB], writes=[colA])
        p.op("vector", lambda e: e.tensor_tensor(colA.ap[:], colA.ap[:], colC.ap[:], ALU.add),
             reads=[colA, colC], writes=[colA])
        p.op("vector", lambda e: e.reciprocal(colA.ap[:], colA.ap[:]), reads=[colA], writes=[colA])
        p.op("vector", lambda e: e.scalar_tensor_tensor(
            wg8.ap[:], em8.ap[:], colA.ap[:], ind8.ap[:], ALU.mult, ALU.mult),
            reads=[em8, colA, ind8], writes=[wg8])
        p.op("tensor", lambda e: e.transpose(bB2.ap[0:8, 0:128], wg8.ap[:], sIdf.ap[:]),
             reads=[wg8, sIdf], writes=[bB2])
        p.op("vector", lambda e, qsl=qsl: e.tensor_copy(wT8.ap[:, qsl], bB2.ap[0:8, 0:128]),
             reads=[bB2], writes=[wT8])

    nslab_tot = 0
    for st in range(NST):
        tsl = slice(st * ST, (st + 1) * ST)
        p.dma("sync", xt3.ap[:].rearrange("p (c t) -> p c t", t=ST),
              x3_sl[:, tsl].rearrange("(c p) t -> p c t", p=128), writes=[xt3], sem="q_x3a")
        p.dma("gpsimd", xb.ap[:].rearrange("p (c t) -> p c t", t=ST),
              x3_sl[:, tsl].rearrange("(c p) t -> p c t", p=128), writes=[xb], sem="q_x3b")
        for c in range(16):
            p.op("scalar", lambda e, c=c: e.activation(
                g0b.ap[:, c * ST:(c + 1) * ST], xt3.ap[:, c * ST:(c + 1) * ST], AF.Gelu),
                reads=[xt3], writes=[g0b])
        for oc in range(8):
            sl = w1s[oc % 2]
            p.dma("gpsimd", sl.ap[:, 0:16 * 128].rearrange("p (c o) -> p c o", o=128),
                  downw[:, oc * 128:(oc + 1) * 128].rearrange("(c p) o -> p c o", p=128),
                  writes=[sl], sem=f"q_w1{oc % 2}")
            bk = pool.take()
            p.mmgroup(bk, bk.ap[:, 0:ST],
                      [(sl.ap[:, kc * 128:(kc + 1) * 128], xb.ap[:, kc * ST:(kc + 1) * ST])
                       for kc in range(16)], reads=[sl, xb])
            p.op("scalar", lambda e, bk=bk, oc=oc: e.activation(
                g1b.ap[:, oc * ST:(oc + 1) * ST], bk.ap[:, 0:ST], AF.Gelu,
                bias=sDbc.ap[:, oc:oc + 1]), reads=[bk, sDbc], writes=[g1b])
        for qb in range(ST // 128):
            router(qb, st)
        for e in range(9):
            He = HID[e] if e < 8 else L
            w1d = ew1[e] if e < 8 else sw1
            w2d = ew2[e] if e < 8 else sw2
            b1c = sEb1[e] if e < 8 else sSb1
            if e < 8:
                p.op("tensor", lambda e_, e=e: e_.matmul(
                    bB2.ap[:, 0:ST], sSel.ap[:, e * 128:(e + 1) * 128], wT8.ap[:], start=True, stop=True),
                    reads=[sSel, wT8], writes=[bB2])
                p.op("vector", lambda e_: e_.tensor_copy(sWBe.ap[:], bB2.ap[:, 0:ST]),
                     reads=[bB2], writes=[sWBe])
            for slab in range(He // 1024):
                s1 = w1s[nslab_tot % 2]
                p.dma("gpsimd", s1.ap[:].rearrange("p (c o) -> p c o", o=1024),
                      w1d[:, slab * 1024:(slab + 1) * 1024].rearrange("(c p) o -> p c o", p=128),
                      writes=[s1], sem=f"q_w1{nslab_tot % 2}")
                s2 = w2s[0]
                p.dma("gpsimd", s2.ap[:].rearrange("p (c o) -> p c o", o=1024),
                      w2d[slab * 1024:(slab + 1) * 1024, :].rearrange("(c p) o -> p c o", p=128),
                      writes=[s2], sem="q_w20")
                for ocs in range(8):
                    bk = pool.take()
                    p.mmgroup(bk, bk.ap[:, 0:ST],
                              [(s1.ap[:, kc * 1024 + ocs * 128:kc * 1024 + (ocs + 1) * 128],
                                g1b.ap[:, kc * ST:(kc + 1) * ST]) for kc in range(8)],
                              reads=[s1, g1b])
                    sg = scG[ocs % 2]
                    p.op("scalar", lambda e_, bk=bk, sg=sg, b1c=b1c, slab=slab, ocs=ocs: e_.activation(
                        sg.ap[:], bk.ap[:, 0:ST], AF.Gelu, bias=b1c.ap[:, slab * 8 + ocs:slab * 8 + ocs + 1]),
                        reads=[bk, b1c], writes=[sg])
                    if e < 8:
                        p.op("vector", lambda e_, sg=sg, ocs=ocs, g1bc=sG1b[e], slab=slab: e_.scalar_tensor_tensor(
                            gss.ap[:, ocs * ST:(ocs + 1) * ST], sg.ap[:],
                            g1bc.ap[:, slab * 8 + ocs:slab * 8 + ocs + 1], sWBe.ap[:],
                            ALU.subtract, ALU.mult), reads=[sg, sG1b[e], sWBe], writes=[gss])
                    else:
                        p.op("vector", lambda e_, sg=sg, ocs=ocs: e_.tensor_scalar(
                            gss.ap[:, ocs * ST:(ocs + 1) * ST], sg.ap[:], 0.1, None, ALU.mult, ALU.bypass),
                            reads=[sg], writes=[gss])
                for moc in range(8):
                    bke = epool.take()
                    p.mmgroup(bke, bke.ap[:, 0:ST],
                              [(s2.ap[:, kc * 1024 + moc * 128:kc * 1024 + (moc + 1) * 128],
                                gss.ap[:, kc * ST:(kc + 1) * ST]) for kc in range(8)],
                              reads=[s2, gss])
                    first = (e == 0 and slab == 0)
                    mo = macc[0]
                    if first:
                        p.op("vector", lambda e_, bke=bke, moc=moc: e_.tensor_copy(
                            mo.ap[:, moc * ST:(moc + 1) * ST], bke.ap[:, 0:ST]),
                            reads=[bke], writes=[mo])
                    else:
                        p.op("vector", lambda e_, bke=bke, moc=moc, mo=mo: e_.tensor_tensor(
                            mo.ap[:, moc * ST:(moc + 1) * ST], mo.ap[:, moc * ST:(moc + 1) * ST],
                            bke.ap[:, 0:ST], ALU.add), reads=[bke, mo], writes=[mo])
                    if st == 0 and e < 8:
                        bkc_ = epool.take()
                        p.mmgroup(bkc_, bkc_.ap[:, 0:1],
                                  [(s2.ap[:, kc * 1024 + moc * 128:kc * 1024 + (moc + 1) * 128],
                                    sG1bB[e].ap[:, slab * 8 + kc:slab * 8 + kc + 1]) for kc in range(8)],
                                  reads=[s2, sG1bB[e]])
                        p.op("vector", lambda e_, bkc_=bkc_, moc=moc: e_.tensor_tensor(
                            sCcol.ap[:, moc:moc + 1], sCcol.ap[:, moc:moc + 1], bkc_.ap[:, 0:1], ALU.add),
                            reads=[bkc_, sCcol], writes=[sCcol])
                nslab_tot += 1
        for moc in range(8):
            p.op("vector", lambda e_, moc=moc: e_.tensor_scalar(
                moeb.ap[:, moc * ST:(moc + 1) * ST], macc[0].ap[:, moc * ST:(moc + 1) * ST],
                sCcol.ap[:, moc:moc + 1], None, ALU.add, ALU.bypass),
                reads=[macc[0], sCcol], writes=[moeb])
        for oc in range(16):
            su = w1s[oc % 2]
            p.dma("gpsimd", su.ap[:, 0:8 * 128].rearrange("p (c o) -> p c o", o=128),
                  upw[:, oc * 128:(oc + 1) * 128].rearrange("(c p) o -> p c o", p=128),
                  writes=[su], sem=f"q_w1{oc % 2}")
            sc_ = w2s[0]
            p.dma("gpsimd", sc_.ap[:, 0:16 * 128].rearrange("p (c o) -> p c o", o=128),
                  corew[:, oc * 128:(oc + 1) * 128].rearrange("(c p) o -> p c o", p=128),
                  writes=[sc_], sem="q_w20")
            bk = pool.take()
            p._pre("tensor", [su, sc_, moeb, g0b], [bk])
            cse = p.sem("e_tensor")
            for kc in range(8):
                p.raw("tensor", lambda e_, w=su.ap[:, kc * 128:(kc + 1) * 128],
                      r=moeb.ap[:, kc * ST:(kc + 1) * ST], o=bk.ap[:, 0:ST], s=(kc == 0):
                      e_.matmul(o, w, r, start=s, stop=False))
            for kc in range(16):
                last = kc == 15
                if last:
                    cse[1] += 1
                    p.raw("tensor", lambda e_, w=sc_.ap[:, kc * 128:(kc + 1) * 128],
                          r=g0b.ap[:, kc * ST:(kc + 1) * ST], o=bk.ap[:, 0:ST], sem=cse[0]:
                          e_.matmul(o, w, r, start=False, stop=True).then_inc(sem, 1))
                else:
                    p.raw("tensor", lambda e_, w=sc_.ap[:, kc * 128:(kc + 1) * 128],
                          r=g0b.ap[:, kc * ST:(kc + 1) * ST], o=bk.ap[:, 0:ST]:
                          e_.matmul(o, w, r, start=False, stop=False))
            tok = ("e_tensor", cse[1])
            for b_ in (su, sc_, moeb, g0b):
                b_.ctoks[tok[0]] = max(b_.ctoks.get(tok[0], 0), tok[1])
            bk.ptok = tok
            bk.ctoks = {}
            xr_ap = xrb.ap[:, oc * ST:(oc + 1) * ST]
            p.op("vector", lambda e_, bk=bk, xr_ap=xr_ap, oc=oc: e_.scalar_tensor_tensor(
                xr_ap, bk.ap[:, 0:ST], sBsum.ap[:, oc:oc + 1], xt3.ap[:, oc * ST:(oc + 1) * ST],
                ALU.add, ALU.add), reads=[bk, sBsum, xt3], writes=[xrb])
            sq_ap = gssF[:, (oc % 2) * ST:(oc % 2 + 1) * ST]
            p.op("scalar", lambda e_, xr_ap=xr_ap, sq_ap=sq_ap: e_.activation(sq_ap, xr_ap, AF.Square),
                 reads=[xrb], writes=[gss])
            p.accum_step(bS1, bS1.ap[0:1, 0:ST], sOneC.ap[:], xr_ap, oc == 0, oc == 15, reads=[sOneC, xrb])
            p.accum_step(bS2, bS2.ap[0:1, 0:ST], sOneC.ap[:], sq_ap, oc == 0, oc == 15, reads=[sOneC, gss])
        p.op("vector", lambda e_: e_.tensor_scalar(rowA.ap[:], bS1.ap[0:1, 0:ST], 1.0 / D, None, ALU.mult, ALU.bypass),
             reads=[bS1], writes=[rowA])
        p.op("vector", lambda e_: e_.tensor_scalar(rowB.ap[:], bS2.ap[0:1, 0:ST], 1.0 / D, None, ALU.mult, ALU.bypass),
             reads=[bS2], writes=[rowB])
        p.op("vector", lambda e_: e_.tensor_tensor(rowC.ap[:], rowA.ap[:], rowA.ap[:], ALU.mult),
             reads=[rowA], writes=[rowC])
        p.op("vector", lambda e_: e_.tensor_tensor(rowB.ap[:], rowB.ap[:], rowC.ap[:], ALU.subtract),
             reads=[rowB, rowC], writes=[rowB])
        p.op("scalar", lambda e_: e_.activation(rowB.ap[:], rowB.ap[:], AF.Sqrt, bias=sEps1.ap[:]),
             reads=[rowB, sEps1], writes=[rowB])
        p.op("vector", lambda e_: e_.reciprocal(rowB.ap[:], rowB.ap[:]), reads=[rowB], writes=[rowB])
        p.op("vector", lambda e_: e_.tensor_tensor(rowC.ap[:], rowA.ap[:], rowB.ap[:], ALU.mult),
             reads=[rowA, rowB], writes=[rowC])
        p.op("tensor", lambda e_: e_.matmul(bB1.ap[:, 0:ST], sOne1.ap[:], rowB.ap[:], start=True, stop=True),
             reads=[sOne1, rowB], writes=[bB1])
        p.op("tensor", lambda e_: e_.matmul(bB2.ap[:, 0:ST], sOne1.ap[:], rowC.ap[:], start=True, stop=True),
             reads=[sOne1, rowC], writes=[bB2])
        for oc in range(16):
            xr_ap = xrb.ap[:, oc * ST:(oc + 1) * ST]
            sa_ap = gssF[:, (oc % 2) * ST:(oc % 2 + 1) * ST]
            p.op("vector", lambda e_, xr_ap=xr_ap, sa_ap=sa_ap: e_.tensor_tensor(
                sa_ap, xr_ap, bB1.ap[:, 0:ST], ALU.mult), reads=[xrb, bB1], writes=[gss])
            sb_ap = gssF[:, (2 + oc % 2) * ST:(3 + oc % 2) * ST]
            p.op("vector", lambda e_, sa_ap=sa_ap, sb_ap=sb_ap: e_.tensor_tensor(
                sb_ap, sa_ap, bB2.ap[:, 0:ST], ALU.subtract), reads=[gss, bB2], writes=[gss])
            p.dma("sync", out_sl[oc * 128:(oc + 1) * 128, tsl], sb_ap,
                  reads=[gss], sem=f"q_ow{oc % 2}")

    p.wait("gpsimd", "q_ow0", p.sem("q_ow0")[1])
    p.wait("gpsimd", "q_ow1", p.sem("q_ow1")[1])
    p.raw("gpsimd", lambda e: e.engine_nop())



def stage_d_inmaps(inputs, x3_fm):
    HIDc = [1024, 2048, 3072, 4096, 1024, 2048, 3072, 4096]
    cost = np.array([2 * 1024 * h for h in HIDc], np.float32)
    sel = np.zeros((8, 8 * 128), np.float32)
    for e in range(8):
        sel[e, e * 128:(e + 1) * 128] = 1.0
    shared = dict(
        downw=inputs["L1_ffn_down_w"], downbc=_bcols(inputs["L1_ffn_down_b"]),
        upw=inputs["L1_ffn_up_w"], upbc=_bcols(inputs["L1_ffn_up_b"]),
        corew=inputs["L1_ffn_core_w"], corebc=_bcols(inputs["L1_ffn_core_b"]),
        bsumc=_bcols(inputs["L1_ffn_up_b"]),
        rw1=inputs["L1_ffn_router_w"], rb1=inputs["L1_ffn_router_b"].reshape(1, 8),
        costr=(-1e-7 * cost).reshape(1, 8),
        sw1=inputs["L1_ffn_shared_w1"], sb1c=_bcols(inputs["L1_ffn_shared_b1"]),
        sw2=inputs["L1_ffn_shared_w2"], sb2c=_bcols(inputs["L1_ffn_shared_b2"]),
        sels=sel,
        ones1=np.ones((1, 128), np.float32), onesc=np.ones((128, 1), np.float32),
        idf=np.eye(128, dtype=np.float32),
    )
    for e in range(8):
        shared[f"e{e}w1"] = inputs[f"L1_ffn_e{e}_w1"]
        shared[f"e{e}b1"] = _bcols(inputs[f"L1_ffn_e{e}_b1"])
        shared[f"e{e}w2"] = inputs[f"L1_ffn_e{e}_w2"]
        shared[f"e{e}b2"] = _bcols(inputs[f"L1_ffn_e{e}_b2"])
    ims = []
    for c in range(8):
        ims.append(dict(x3_sl=x3_fm[c], **shared))
    return ims



def DECLS_A(NCH=64):
    T, GSZ = NCH * 128, NCH // 4
    return [
        ("xfm", [2, 2048, T], F32, "ExternalInput"),
        ("wq_s", [2048, 128], F32, "ExternalInput"),
        ("wk_s", [2048, 128], F32, "ExternalInput"),
        ("wv_s", [2048, 256], F32, "ExternalInput"),
        ("ws_s", [2048, 128], F32, "ExternalInput"),
        ("wg_s", [2048, 256], F32, "ExternalInput"),
        ("gnorm", [1, 128], F32, "ExternalInput"),
        ("u32", [128, 128], F32, "ExternalInput"),
        ("ubf", [128, 128], BF16, "ExternalInput"),
        ("idbf", [128, 128], BF16, "ExternalInput"),
        ("ones1", [1, 128], F32, "ExternalInput"),
        ("onesc", [128, 1], F32, "ExternalInput"),
        ("o_scr", [2048, GSZ * 128], BF16, "ExternalOutput"),
    ]


DECLS_B = [
    ("o_sl", [2048, 2048], BF16, "ExternalInput"),
    ("x_sl", [2048, 2048], F32, "ExternalInput"),
    ("wo0", [2048, 2048], F32, "ExternalInput"),
    ("basewT", [2048, 2048], F32, "ExternalInput"),
    ("basebc", [128, 16], F32, "ExternalInput"),
    ("rwT", [2048, 8], F32, "ExternalInput"),
    ("rbrow", [1, 8], F32, "ExternalInput"),
    ("lA", [2048, 128], F32, "ExternalInput"),
    ("lB", [128, 2048], F32, "ExternalInput"),
    ("wq1", [2048, 2048], F32, "ExternalInput"),
    ("wk1", [2048, 2048], F32, "ExternalInput"),
    ("wv1", [2048, 2048], F32, "ExternalInput"),
    ("bqc", [128, 16], F32, "ExternalInput"),
    ("bkc", [128, 16], F32, "ExternalInput"),
    ("bvr", [1, 2048], F32, "ExternalInput"),
    ("inds", [8, 128], F32, "ExternalInput"),
    ("ones1", [1, 128], F32, "ExternalInput"),
    ("onesc", [128, 1], F32, "ExternalInput"),
    ("idf", [128, 128], F32, "ExternalInput"),
    ("x2_out", [2048, 2048], F32, "ExternalOutput"),
    ("q_out", [2048, 2048], BF16, "ExternalOutput"),
    ("k_out", [2048, 2048], BF16, "ExternalOutput"),
    ("v_out", [2048, 2048], BF16, "ExternalOutput"),
    ("x1_scr", [2048, 2048], F32, "Internal"),
]

DECLS_C = [
    ("q_sl", [2048, 2048], BF16, "ExternalInput"),
    ("kf", [2048, 8192], BF16, "ExternalInput"),
    ("vt", [8192, 2048], BF16, "ExternalInput"),
    ("x2_sl", [2048, 2048], F32, "ExternalInput"),
    ("wo1", [2048, 2048], F32, "ExternalInput"),
    ("boc", [128, 16], F32, "ExternalInput"),
    ("ones1", [1, 128], F32, "ExternalInput"),
    ("onesc", [128, 1], F32, "ExternalInput"),
    ("onescb", [128, 1], BF16, "ExternalInput"),
    ("idbf", [128, 128], BF16, "ExternalInput"),
    ("x3_out", [2048, 2048], F32, "ExternalOutput"),
    ("o1_scr", [2048, 2048], BF16, "Internal"),
]

DECLS_D = [
    ("x3_sl", [2048, 2048], F32, "ExternalInput"),
    ("downw", [2048, 1024], F32, "ExternalInput"),
    ("downbc", [128, 8], F32, "ExternalInput"),
    ("upw", [1024, 2048], F32, "ExternalInput"),
    ("bsumc", [128, 16], F32, "ExternalInput"),
    ("corew", [2048, 2048], F32, "ExternalInput"),
    ("upbc", [128, 16], F32, "ExternalInput"),
    ("corebc", [128, 16], F32, "ExternalInput"),
    ("rw1", [2048, 8], F32, "ExternalInput"),
    ("rb1", [1, 8], F32, "ExternalInput"),
    ("costr", [1, 8], F32, "ExternalInput"),
    ("sw1", [1024, 1024], F32, "ExternalInput"),
    ("sb1c", [128, 8], F32, "ExternalInput"),
    ("sw2", [1024, 1024], F32, "ExternalInput"),
    ("sb2c", [128, 8], F32, "ExternalInput"),
    ("sels", [8, 8 * 128], F32, "ExternalInput"),
    ("ones1", [1, 128], F32, "ExternalInput"),
    ("onesc", [128, 1], F32, "ExternalInput"),
    ("idf", [128, 128], F32, "ExternalInput"),
    ("out_sl", [2048, 2048], F32, "ExternalOutput"),
]


def _declare(nc, decls, kinds=None, prefix_done=()):
    E = {}
    for name, shape, dt, kind in decls:
        if kinds and name in kinds:
            kind = kinds[name]
        if name in prefix_done:
            continue
        E[name] = nc.dram_tensor(name, shape, dt, kind=kind).ap()
    return E


def _mk(emit_fn, decls, **kw):
    nc = bass.Bass("TRN2", target_bir_lowering=False)
    E = _declare(nc, decls)
    with contextlib.ExitStack() as stack:
        p = P(nc, stack)
        emit_fn(nc, p, stack, E, **kw)
        with nc.Block() as block:
            p.emit(block)
    return nc


def build_stage_a(NCH=64, debug=False):
    return _mk(emit_stage_a, DECLS_A(NCH), NCH=NCH, debug=debug)


def build_stage_b():
    return _mk(emit_stage_b, DECLS_B)


def build_stage_c():
    nc = bass.Bass("TRN2", target_bir_lowering=False)
    E = _declare(nc, DECLS_C)
    E["kf_parts"] = [E["kf"][:, r * 2048:(r + 1) * 2048] for r in range(4)]
    E["vt_parts"] = [E["vt"][r * 2048:(r + 1) * 2048, :] for r in range(4)]
    with contextlib.ExitStack() as stack:
        p = P(nc, stack)
        emit_stage_c(nc, p, stack, E)
        with nc.Block() as block:
            p.emit(block)
    return nc


def build_stage_d():
    return _mk(emit_stage_d, DECLS_D)



def build_fused(dump=False):
    nc = bass.Bass("TRN2", target_bir_lowering=False)
    E = {}
    seen = set()

    def decl(decls, kinds=None):
        for name, shape, dt, kind in decls:
            if name in seen:
                continue
            seen.add(name)
            if kinds and name in kinds:
                kind = kinds[name]
            E[name] = nc.dram_tensor(name, shape, dt, kind=kind).ap()

    decl(DECLS_A(64), kinds={"o_scr": "Internal"})
    decl([d for d in DECLS_B if d[0] not in ("o_sl", "x2_out", "q_out", "k_out", "v_out")])
    decl([d for d in DECLS_C if d[0] not in ("q_sl", "kf", "vt", "x2_sl", "x3_out")])
    decl([d for d in DECLS_D if d[0] != "x3_sl"])
    o_rcv = nc.dram_tensor("o_rcv", [2048, 2048], BF16).ap()
    k_scr = nc.dram_tensor("k_scr", [2048, 2048], BF16).ap()
    v_scr = nc.dram_tensor("v_scr", [2048, 2048], BF16).ap()
    k_rcv = nc.dram_tensor("k_rcv", [4 * 2048, 2048], BF16).ap()
    v_rcv = nc.dram_tensor("v_rcv", [4 * 2048, 2048], BF16).ap()
    x2_scr = nc.dram_tensor("x2_scr", [2048, 2048], F32).ap()
    q_scr = nc.dram_tensor("q_scr", [2048, 2048], BF16).ap()
    x3_scr = nc.dram_tensor("x3_scr", [2048, 2048], F32).ap()
    E["o_sl"] = o_rcv
    E["x2_out"] = x2_scr
    E["q_out"] = q_scr
    E["k_out"] = k_scr
    E["v_out"] = v_scr
    E["q_sl"] = q_scr
    E["x2_sl"] = x2_scr
    E["kf_parts"] = [k_rcv[r * 2048:(r + 1) * 2048, :] for r in range(4)]
    E["vt_parts"] = [v_rcv[r * 2048:(r + 1) * 2048, :] for r in range(4)]
    E["x3_out"] = x3_scr
    E["x3_sl"] = x3_scr

    if dump:
        E["dmp_qn"] = nc.dram_tensor("dmp_qn", [16, 2048], F32, kind="ExternalOutput").ap()
        E["dmp_km"] = nc.dram_tensor("dmp_km", [16, 1], F32, kind="ExternalOutput").ap()
    groups4 = [[0, 1, 2, 3], [4, 5, 6, 7]]
    with contextlib.ExitStack() as stack:
        p = P(nc, stack)
        _NAME_PREFIX[0] = "a_"
        with contextlib.ExitStack() as st:
            emit_stage_a(nc, p, st, E)
        _NAME_PREFIX[0] = ""
        p.barrier()
        def _a2a(e):
            with allow_ici_all_to_all():
                return e.collective_compute(
                    "AllToAll", ALU.bypass, replica_groups=[list(range(8))],
                    ins=[E["o_scr"]], outs=[o_rcv])
        orcv_b = Buf(o_rcv, "orcv")
        p.op("gpsimd", _a2a, writes=[orcv_b])
        p.op("gpsimd", _a2a, writes=[orcv_b])
        p.barrier()
        def _kag():
            for sn in ("q_qk0", "q_qk1"):
                p.wait("gpsimd", sn, p.sem(sn)[1])
            for _ in range(2):
                p.op("gpsimd", lambda e: e.collective_compute(
                    "AllGather", ALU.bypass, replica_groups=groups4,
                    ins=[k_scr], outs=[k_rcv]))
        _NAME_PREFIX[0] = "b_"
        with contextlib.ExitStack() as st:
            emit_stage_b(nc, p, st, E, mid_cb=_kag)
        _NAME_PREFIX[0] = ""
        for sn in ("q_v0", "q_v1"):
            p.wait("gpsimd", sn, p.sem(sn)[1])
        p.op("gpsimd", lambda e: e.collective_compute(
            "AllGather", ALU.bypass, replica_groups=groups4,
            ins=[v_scr], outs=[v_rcv]))
        p.barrier()
        _NAME_PREFIX[0] = "c_"
        with contextlib.ExitStack() as st:
            emit_stage_c(nc, p, st, E)
        _NAME_PREFIX[0] = ""
        p.barrier()
        _NAME_PREFIX[0] = "d_"
        with contextlib.ExitStack() as st:
            emit_stage_d(nc, p, st, E)
        _NAME_PREFIX[0] = ""
        if dump:
            dmp = {}
            dmp["dmp_o"] = nc.dram_tensor("dmp_o", [2048, 2048], BF16, kind="ExternalOutput").ap()
            dmp["dmp_kv"] = nc.dram_tensor("dmp_kv", [4 * 4096, 2048], BF16, kind="ExternalOutput").ap()
            dmp["dmp_x2"] = nc.dram_tensor("dmp_x2", [2048, 2048], F32, kind="ExternalOutput").ap()
            dmp["dmp_q"] = nc.dram_tensor("dmp_q", [2048, 2048], BF16, kind="ExternalOutput").ap()
            dmp["dmp_x3"] = nc.dram_tensor("dmp_x3", [2048, 2048], F32, kind="ExternalOutput").ap()
            p.barrier()
            for dst, srcap in (("dmp_o", o_rcv), ("dmp_kv", kv_rcv), ("dmp_x2", x2_scr),
                               ("dmp_q", q_scr), ("dmp_x3", x3_scr)):
                p.dma_serial("sync", dmp[dst], srcap, sem="q_dmp")
            p.wait("gpsimd", "q_dmp", p.sem("q_dmp")[1])
        with nc.Block() as block:
            p.emit(block)
    return nc


def fused_inmaps(inputs):
    ims_a = stage_a_inmaps(inputs)
    o_dummy = None
    x = inputs["L0_attn_hidden_states"]
    xT = np.ascontiguousarray(x.transpose(0, 2, 1))
    ims_b = stage_b_inmaps(inputs, [xT.astype(BF), xT.astype(BF)][0:1] * 2 if False else
                           np.zeros((2, 2048, 8192), dtype=BF))
    ims_c = stage_c_inmaps(inputs, [None] * 8, np.zeros((2, 1, 1), BF), np.zeros((2, 1, 1), BF),
                           [None] * 8)
    ims_d = stage_d_inmaps(inputs, [None] * 8)
    ims = []
    for c in range(8):
        m = {}
        m.update(ims_a[c])
        for k_, v_ in ims_b[c].items():
            if k_ not in ("o_sl",):
                m.setdefault(k_, v_)
        for k_, v_ in ims_c[c].items():
            if k_ not in ("q_sl", "kf", "vt", "x2_sl") and v_ is not None:
                m.setdefault(k_, v_)
        for k_, v_ in ims_d[c].items():
            if k_ not in ("x3_sl",):
                m.setdefault(k_, v_)
        ims.append(m)
    return ims


def kernel(**inputs):
    cores = list(range(8))
    nc = build_fused()
    ims = fused_inmaps(inputs)
    res = run_bass_kernel_spmd(nc, ims, core_ids=cores)
    out = np.zeros((2, 8192, 2048), np.float32)
    for c in cores:
        b, g = c // 4, c % 4
        out[b, g * 2048:(g + 1) * 2048, :] = np.asarray(res.results[c]["out_sl"]).T
    return out
